# version 12
# baseline (speedup 1.0000x reference)
import math
from collections import Counter
from contextlib import ExitStack

import numpy as np
import ml_dtypes

import concourse.bass as bass
import concourse.mybir as mybir
from concourse.bass_utils import run_bass_kernel_spmd

F32 = mybir.dt.float32
BF16 = mybir.dt.bfloat16
I32 = mybir.dt.int32
AF = mybir.ActivationFunctionType
ALU = mybir.AluOpType
AX = mybir.AxisListType

B, T, D = 4, 4096, 2048
NHL = 4
HD, HH = 256, 128
CQ = NHL * HD
HF = 5632
H13 = 2 * HF
MYT = 2048
TCA = T // 128
TCM = MYT // 128
NDC = D // 128
NCC = CQ // 128
NPG = HF // 512
NHC = HF // 128
LAMBDA_INIT = 0.2
RMS_EPS = 1.1920929e-07
LN_EPS = 1e-05
ROPE_BASE = 10000.0
SCALE = 1.0 / math.sqrt(128.0)

DEBUG = False


class Prog:
    def __init__(self):
        self.items = []
        self.cnt = Counter()

    def op(self, eng, fn, inc=None, n=1):
        if inc is not None:
            self.cnt[inc] += n
        self.items.append((eng, "op", fn, inc, n))

    def dma(self, eng, fn, inc):
        self.op(eng, fn, inc, 16)

    def wait(self, eng, sem, val=None):
        v = self.cnt[sem] if val is None else val
        if v > 0:
            self.items.append((eng, "wait", sem, v))

    def barrier(self, engines, sems):
        for e in engines:
            for s in sems:
                self.wait(e, s)


def build_nc():
    nc = bass.Bass("TRN2", target_bir_lowering=False)

    xb = nc.dram_tensor("xb", [T, D], F32, kind="ExternalInput").ap()
    xmy = nc.dram_tensor("xmy", [MYT, D], F32, kind="ExternalInput").ap()
    wq = nc.dram_tensor("wq", [CQ, D], F32, kind="ExternalInput").ap()
    wk = nc.dram_tensor("wk", [CQ, D], F32, kind="ExternalInput").ap()
    wv = nc.dram_tensor("wv", [CQ, D], F32, kind="ExternalInput").ap()
    wo = nc.dram_tensor("wo", [D, CQ], F32, kind="ExternalInput").ap()
    w13 = nc.dram_tensor("w13", [H13, D], F32, kind="ExternalInput").ap()
    w2 = nc.dram_tensor("w2", [D, HF], F32, kind="ExternalInput").ap()
    nw_in = nc.dram_tensor("nw_in", [1, HF], F32, kind="ExternalInput").ap()
    lqk = nc.dram_tensor("lqk", [4, HD], F32, kind="ExternalInput").ap()
    rotc_in = nc.dram_tensor("rotc_in", [128, CQ], F32, kind="ExternalInput").ap()
    rots_in = nc.dram_tensor("rots_in", [128, CQ], F32, kind="ExternalInput").ap()
    msk_in = nc.dram_tensor("msk_in", [2, 128, 1024], BF16, kind="ExternalInput").ap()

    out_c = nc.dram_tensor("out_c", [MYT, D], F32, kind="ExternalOutput").ap()

    xscr = nc.dram_tensor("xscr", [T, D], BF16, kind="Internal").ap()
    wqscr = nc.dram_tensor("wqscr", [CQ, D], BF16, kind="Internal").ap()
    wkscr = nc.dram_tensor("wkscr", [CQ, D], BF16, kind="Internal").ap()
    wvscr = nc.dram_tensor("wvscr", [CQ, D], BF16, kind="Internal").ap()
    woscr = nc.dram_tensor("woscr", [D, CQ], BF16, kind="Internal").ap()
    w13scr = nc.dram_tensor("w13scr", [H13, D], BF16, kind="Internal").ap()
    w2scr = nc.dram_tensor("w2scr", [D, HF], BF16, kind="Internal").ap()
    qscr = nc.dram_tensor("qscr", [T, CQ], BF16, kind="Internal").ap()
    kscr = nc.dram_tensor("kscr", [T, CQ], BF16, kind="Internal").ap()
    vscr = nc.dram_tensor("vscr", [T, CQ], BF16, kind="Internal").ap()
    yscr = nc.dram_tensor("yscr", [T, CQ], BF16, kind="Internal").ap()
    apart = nc.dram_tensor("apart", [T, D], F32, kind="Internal").ap()
    ascat = nc.dram_tensor("ascat", [MYT, D], F32, kind="Internal").ap()
    xpscr = nc.dram_tensor("xpscr", [MYT, D], F32, kind="Internal").ap()
    xqscr = nc.dram_tensor("xqscr", [MYT, D], BF16, kind="Internal").ap()
    tmpscr = nc.dram_tensor("tmpscr", [MYT, HF], BF16, kind="Internal").ap()
    iqscr = nc.dram_tensor("iqscr", [NHC, 128, MYT], BF16, kind="Internal").ap()
    rsscr = nc.dram_tensor("rsscr", [2, MYT], F32, kind="Internal").ap()

    dbg = {}
    if DEBUG:
        for name, src in [("d_qscr", qscr), ("d_kscr", kscr), ("d_vscr", vscr),
                          ("d_yscr", yscr), ("d_apart", apart), ("d_ascat", ascat),
                          ("d_xpscr", xpscr), ("d_xqscr", xqscr), ("d_tmpscr", tmpscr),
                          ("d_iqscr", iqscr)]:
            dbg[name] = (nc.dram_tensor(name, list(src.shape), src.dtype,
                                        kind="ExternalOutput").ap(), src)

    stack = ExitStack()
    with stack:
        consts = stack.enter_context(nc.sbuf_tensor("consts", [128, 8], F32))
        CONST_VALS = [SCALE, 1.0 / 128.0, RMS_EPS, LN_EPS, 256.5]
        for i, v in enumerate(CONST_VALS):
            nc.const_aps.register(F32, v, consts[:, i:i + 1])
        onesf32 = stack.enter_context(nc.sbuf_tensor("onesf32", [1, 128], F32))
        lamt = stack.enter_context(nc.sbuf_tensor("lamt", [1, 1544], F32))
        neg08lam = stack.enter_context(nc.sbuf_tensor("neg08lam", [128, 1], F32))
        stat = stack.enter_context(nc.sbuf_tensor("stat", [128, 64], F32))
        fstat = stack.enter_context(nc.sbuf_tensor("fstat", [128, 2 * TCM * NPG + 2 * TCM], F32))
        trash_bf = stack.enter_context(nc.sbuf_tensor("trash_bf", [128, D], BF16))

        psT = [stack.enter_context(nc.psum_tensor(f"ps{i}", [128, 512], F32))
               for i in range(4)]
        psB = [stack.enter_context(nc.psum_tensor(f"psb{i}", [128, 1024], F32))
               for i in range(2)]
        ps = psT + [psB[0][:, 0:512], psB[0][:, 512:1024],
                    psB[1][:, 0:512], psB[1][:, 512:1024]]

        SBUF_TOP = 229376
        base0 = SBUF_TOP - nc.sbuf_bytes_remaining

        class Arena:
            def __init__(self, base, top):
                self.off = base
                self.top = top
                self.n = 0

            def alloc(self, shape, dtype):
                per = int(np.prod(shape[1:])) * mybir.dt.size(dtype)
                off = (self.off + 31) & ~31
                self.off = off + per
                assert self.off <= self.top, f"SBUF overflow: {self.off} > {self.top}"
                self.n += 1
                return nc.alloc_sbuf_tensor_at(f"ar{self.n}", shape, dtype, offset=off).ap()

            def mark(self):
                return self.off

            def reset(self, off):
                self.off = off

        ar = Arena(base0 + 32, SBUF_TOP - 64)
        m_pre = ar.mark()
        rotc = ar.alloc([128, CQ], F32)
        rots = ar.alloc([128, CQ], F32)
        msk = ar.alloc([128, 2 * 1024], BF16)

        sem_names = ["sXS", "sWQ", "sWK", "sWV", "sWO", "sW13", "sW2",
                     "sXT", "sWT", "sQKS", "sVS", "sKT", "sVG", "sYS", "sWOT",
                     "sAPS", "sCC", "sLD", "sXPS", "sXQS", "sXQT", "sW13T",
                     "sTMS", "sTMT", "sIQS", "sIQL", "sW2T", "sRS", "sACB",
                     "sOUT", "sPE", "sVE", "sSE", "sGE", "sIN", "sDBG", "sVG2", "sLD2", "sLD3", "sLD4", "sXS2"]
        sems = {n: stack.enter_context(nc.semaphore(n)) for n in sem_names}
        p = Prog()
        ALL_ENG = ["sync", "scalar", "vector", "tensor", "gpsimd"]
        ALLS = list(sem_names)

        def phase_barrier(engines=ALL_ENG):
            for e in engines:
                for sname in ALLS:
                    p.wait(e, sname)

        def ve(fn):
            p.wait("vector", "sVE")
            p.op("vector", fn, "sVE")

        def se(fn):
            p.wait("scalar", "sSE")
            p.op("scalar", fn, "sSE")

        def ge(fn):
            p.wait("gpsimd", "sGE")
            p.op("gpsimd", fn, "sGE")

        def pe(fn, inc=False):
            p.op("tensor", fn, "sPE" if inc else None)

        class Slot:
            def __init__(self):
                self.dep = []

            def require(self, eng):
                for sem, v in self.dep:
                    p.wait(eng, sem, p.cnt[sem] if v is None else v)

            def consumed(self, *sems, total=()):
                self.dep = [(x, p.cnt[x]) for x in sems] + [(x, None) for x in total]

        def tdma(fn, sem):
            p.dma("sync", fn, sem)
            p.wait("sync", sem)

        for i, v in enumerate(CONST_VALS):
            ve(lambda e, i=i, v=v: e.memset(consts[:, i:i + 1], v))
        ve(lambda e: e.memset(onesf32[0:1, :], 1.0))
        p.dma("gpsimd", lambda e: e.dma_start(out=rotc[:], in_=rotc_in), "sIN")
        p.dma("gpsimd", lambda e: e.dma_start(out=rots[:], in_=rots_in), "sIN")
        p.dma("gpsimd", lambda e: e.dma_start(
            out=msk[:].rearrange("p (c j) -> p c j", j=1024),
            in_=msk_in.rearrange("c p j -> p c j")), "sIN")
        p.dma("gpsimd", lambda e: e.dma_start(
            out=lamt[0:1, 0:1024], in_=lqk.rearrange("(o a) j -> o (a j)", o=1)), "sIN")

        p.dma("gpsimd", lambda e: e.dma_start(out=xscr[:, 0:D // 2], in_=xb[:, 0:D // 2]), "sXS")
        p.dma("gpsimd", lambda e: e.dma_start(out=wqscr, in_=wq), "sWQ")
        p.dma("gpsimd", lambda e: e.dma_start(out=xscr[:, D // 2:D], in_=xb[:, D // 2:D]), "sXS2")
        p.dma("gpsimd", lambda e: e.dma_start(out=wkscr, in_=wk), "sWK")
        p.dma("gpsimd", lambda e: e.dma_start(out=wvscr, in_=wv), "sWV")

        p.wait("vector", "sIN")
        L = lamt
        ve(lambda e: e.tensor_tensor(L[0:1, 1024:1280], L[0:1, 0:256], L[0:1, 256:512], ALU.mult))
        ve(lambda e: e.tensor_tensor(L[0:1, 1280:1536], L[0:1, 512:768], L[0:1, 768:1024], ALU.mult))
        p.wait("vector", "sVE")
        ve(lambda e: e.reduce_sum(L[0:1, 1536:1537], L[0:1, 1024:1280], axis=AX.X))
        ve(lambda e: e.reduce_sum(L[0:1, 1537:1538], L[0:1, 1280:1536], axis=AX.X))
        p.wait("scalar", "sVE")
        se(lambda e: e.activation(L[0:1, 1538:1540], L[0:1, 1536:1538], AF.Exp))
        p.wait("vector", "sSE")
        ve(lambda e: e.tensor_tensor(L[0:1, 1540:1541], L[0:1, 1538:1539], L[0:1, 1539:1540], ALU.subtract))
        p.wait("vector", "sVE")
        ve(lambda e: e.tensor_scalar(L[0:1, 1541:1542], L[0:1, 1540:1541], LAMBDA_INIT, -0.8, ALU.add, ALU.mult))
        p.wait("tensor", "sVE")
        p.wait("tensor", "sIN")
        pe(lambda e: e.matmul(ps[0][:, 0:1], onesf32[0:1, :], L[0:1, 1541:1542], start=True, stop=True), inc=True)
        p.wait("vector", "sPE")
        ve(lambda e: e.tensor_copy(neg08lam[:], ps[0][:, 0:1]))

        m0 = ar.mark()
        xT = [ar.alloc([128, T], BF16) for _ in range(NDC)]
        wTg2 = [[ar.alloc([128, 512], BF16) for _ in range(NDC)] for _ in range(2)]
        stg2 = [[ar.alloc([128, 512], F32) for _ in range(4)] for _ in range(2)]
        stg_bf = [ar.alloc([128, 512], BF16) for _ in range(2)]
        seg42 = [ar.alloc([128, 8], F32) for _ in range(2)]

        p.wait("sync", "sXS")
        for dc in range(NDC // 2):
            tdma(lambda e, dc=dc: e.dma_start_transpose(
                xT[dc][:], xscr[:, dc * 128:(dc + 1) * 128]), "sXT")
        p.wait("sync", "sXS2")
        for dc in range(NDC // 2, NDC):
            tdma(lambda e, dc=dc: e.dma_start_transpose(
                xT[dc][:], xscr[:, dc * 128:(dc + 1) * 128]), "sXT")

        groups = []
        for wsem, wscrt, dst_scr, is_qk, dsem in [
                ("sWQ", wqscr, qscr, True, "sQKS"),
                ("sWK", wkscr, kscr, True, "sQKS"),
                ("sWV", wvscr, vscr, False, "sVS")]:
            for cg in range(CQ // 512):
                groups.append((wsem, wscrt, dst_scr, is_qk, dsem, cg))

        wtg_slot = [Slot(), Slot()]
        wtg_sem = [0, 0]
        pb_slot = [Slot() for _ in range(4)]
        pb_slot[0].dep = [("sVE", p.cnt["sVE"])]
        sq_slot = [Slot() for _ in range(2)]
        qn_slot = [Slot() for _ in range(2)]
        t12_slot = [Slot() for _ in range(2)]
        sb_slot = [Slot() for _ in range(2)]
        seg_slot = [Slot() for _ in range(2)]

        def load_group(gi):
            wsem, wscrt, _, _, _, cg = groups[gi]
            buf = gi % 2
            p.wait("sync", wsem, 16)
            wtg_slot[buf].require("sync")
            for dc in range(NDC):
                tdma(lambda e, dc=dc, cg=cg, wscrt=wscrt, buf=buf: e.dma_start_transpose(
                    wTg2[buf][dc][:], wscrt[cg * 512:(cg + 1) * 512, dc * 128:(dc + 1) * 128]), "sWT")
            wtg_sem[buf] = p.cnt["sWT"]

        p.wait("gpsimd", "sXT")
        p.dma("gpsimd", lambda e: e.dma_start(out=woscr, in_=wo), "sWO")
        load_group(0)
        for gi, (wsem, wscrt, dst_scr, is_qk, dsem, cg) in enumerate(groups):
            buf = gi % 2
            if gi + 1 < len(groups):
                load_group(gi + 1)
            wTg = wTg2[buf]
            p.wait("tensor", "sWT", wtg_sem[buf])
            if gi == 0:
                p.wait("tensor", "sXT")
            for tc in range(TCA):
                bank = ps[tc % 4]
                pb_slot[tc % 4].require("tensor")
                for dc in range(NDC):
                    pe(lambda e, dc=dc, tc=tc, bank=bank, wTg=wTg: e.matmul(
                        bank[:, 0:512], xT[dc][:, tc * 128:(tc + 1) * 128],
                        wTg[dc][:, 0:512], start=(dc == 0), stop=(dc == NDC - 1)),
                        inc=(dc == NDC - 1))
                mm_done = p.cnt["sPE"]
                if not is_qk:
                    sb = stg_bf[tc % 2]
                    p.wait("scalar", "sPE", mm_done)
                    sb_slot[tc % 2].require("scalar")
                    se(lambda e, bank=bank, sb=sb: e.copy(sb[:], bank[:, 0:512]))
                    pb_slot[tc % 4].consumed("sSE")
                    p.wait("gpsimd", "sSE")
                    p.dma("gpsimd", lambda e, sb=sb, tc=tc, cg=cg, dst_scr=dst_scr: e.dma_start(
                        out=dst_scr[tc * 128:(tc + 1) * 128, cg * 512:(cg + 1) * 512],
                        in_=sb[:]), dsem)
                    sb_slot[tc % 2].consumed(total=(dsem,))
                else:
                    sq_t, qn_t, t1_t, t2_t = stg2[tc % 2]
                    seg4 = seg42[tc % 2]
                    sb = stg_bf[tc % 2]
                    p.wait("scalar", "sPE", mm_done)
                    sq_slot[tc % 2].require("scalar")
                    se(lambda e, bank=bank, sq_t=sq_t: e.activation(sq_t[:], bank[:, 0:512], AF.Square))
                    p.wait("vector", "sSE")
                    seg_slot[tc % 2].require("vector")
                    ve(lambda e, sq_t=sq_t, seg4=seg4: e.tensor_reduce(
                        seg4[:, 0:4].rearrange("p (a b) -> p a b", b=1),
                        sq_t[:].rearrange("p (a b) -> p a b", a=4), axis=AX.X, op=ALU.add))
                    sq_slot[tc % 2].consumed("sVE")
                    p.wait("scalar", "sVE")
                    se(lambda e, seg4=seg4: e.activation(seg4[:, 4:8], seg4[:, 0:4], AF.Sqrt,
                                                         bias=RMS_EPS, scale=1.0 / 128.0))
                    p.wait("vector", "sSE")
                    ve(lambda e, seg4=seg4: e.reciprocal(seg4[:, 0:4], seg4[:, 4:8]))
                    qn_slot[tc % 2].require("vector")
                    for sg in range(4):
                        ve(lambda e, bank=bank, qn_t=qn_t, sg=sg, seg4=seg4: e.tensor_scalar(
                            qn_t[:, sg * 128:(sg + 1) * 128], bank[:, sg * 128:(sg + 1) * 128],
                            seg4[:, sg:sg + 1], None, ALU.mult))
                    pb_slot[tc % 4].consumed("sSE", "sVE")
                    seg_slot[tc % 2].consumed("sVE")
                    t12_slot[tc % 2].require("vector")
                    ve(lambda e, qn_t=qn_t, t1_t=t1_t, cg=cg: e.tensor_tensor(
                        t1_t[:], qn_t[:], rotc[:, cg * 512:(cg + 1) * 512], ALU.mult))
                    qn_done = p.cnt["sVE"] - 1
                    p.wait("gpsimd", "sVE", qn_done)
                    t12_slot[tc % 2].require("gpsimd")
                    ge(lambda e, qn_t=qn_t, t2_t=t2_t, cg=cg: e.tensor_tensor(
                        t2_t[:].rearrange("p (a b c) -> p a b c", a=4, c=64),
                        qn_t[:].rearrange("p (a b c) -> p a b c", a=4, c=64)[:, :, ::-1, :],
                        rots[:, cg * 512:(cg + 1) * 512].rearrange("p (a b c) -> p a b c", a=4, c=64),
                        ALU.mult))
                    qn_slot[tc % 2].consumed("sVE", "sGE")
                    sb_slot[tc % 2].require("gpsimd")
                    t1_done = p.cnt["sVE"]
                    p.wait("gpsimd", "sVE", t1_done)
                    ge(lambda e, t1_t=t1_t, t2_t=t2_t, sb=sb: e.tensor_tensor(
                        sb[:], t1_t[:], t2_t[:], ALU.add))
                    t12_slot[tc % 2].consumed("sGE")
                    p.wait("gpsimd", "sGE")
                    p.dma("gpsimd", lambda e, sb=sb, tc=tc, cg=cg, dst_scr=dst_scr: e.dma_start(
                        out=dst_scr[tc * 128:(tc + 1) * 128, cg * 512:(cg + 1) * 512],
                        in_=sb[:]), dsem)
                    sb_slot[tc % 2].consumed(total=(dsem,))
            wtg_slot[buf].consumed("sPE")

        ar.reset(m0)
        woT = [ar.alloc([128, D], BF16) for _ in range(NCC)]
        m_wo = ar.mark()
        KQ2 = [[ar.alloc([128, T], BF16) for _ in range(4)] for _ in range(2)]
        Vt2 = [ar.alloc([128, TCA * 257], BF16) for _ in range(2)]
        Pt = [ar.alloc([128, 1024], BF16) for _ in range(2)]
        Ptm = [ar.alloc([128, 1024], BF16) for _ in range(2)]
        oe2 = [ar.alloc([128, 8 * 257], F32) for _ in range(2)]
        eps_t = ar.alloc([128, 280], F32)
        yst = [ar.alloc([128, 256], BF16) for _ in range(2)]

        phase_barrier()
        for vb in range(2):
            for c in range(TCA):
                ve(lambda e, vb=vb, c=c: e.memset(Vt2[vb][:, c * 257 + 256:c * 257 + 257], 1.0))
        p.wait("gpsimd", "sIN")
        p.wait("gpsimd", "sVE")

        psS2 = [psB[0], psB[1]]
        psO = [psT[0], psT[1], psT[2], psT[3]]
        kq_slot = [Slot(), Slot()]
        vt_slot = [Slot(), Slot()]
        kq_sem = [0, 0]
        vt_sem = [0, 0]
        psS_slot = [Slot(), Slot()]
        psS_slot[0].dep = [("sVE", p.cnt["sVE"])]
        pt_slot = [Slot(), Slot()]
        ptm_slot = [Slot(), Slot()]
        psO_slot = [Slot() for _ in range(4)]
        oe_slot = [Slot(), Slot()]
        ys_slot = [Slot(), Slot()]
        ep_jobs = []

        def emit_epilogue(h, qt, buf, sse_evac):
            oe = oe2[buf]
            c0 = h * HD
            p.wait("vector", "sSE", sse_evac)
            ve(lambda e, oe=oe: e.tensor_copy(
                eps_t[:, 0:8], oe[:, 256::257]))
            ve(lambda e: e.reciprocal(eps_t[:, 8:16], eps_t[:, 0:8]))
            ve(lambda e: e.tensor_scalar(eps_t[:, 8:12], eps_t[:, 8:12], 0.8, None, ALU.mult))
            ve(lambda e: e.tensor_scalar(
                eps_t[:, 12:16], eps_t[:, 12:16], neg08lam[:, 0:1], None, ALU.mult))
            for qs in range(4):
                ys = yst[qs % 2]
                ve(lambda e, qs=qs, oe=oe: e.tensor_scalar(
                    eps_t[:, 16:272], oe[:, qs * 257:qs * 257 + 256], eps_t[:, 8 + qs:9 + qs], None, ALU.mult))
                ys_slot[qs % 2].require("vector")
                ve(lambda e, qs=qs, ys=ys, oe=oe: e.scalar_tensor_tensor(
                    ys[:], oe[:, (4 + qs) * 257:(4 + qs) * 257 + 256], eps_t[:, 12 + qs:13 + qs],
                    eps_t[:, 16:272], ALU.mult, ALU.add))
                p.wait("gpsimd", "sVE")
                p.dma("gpsimd", lambda e, ys=ys, qt=qt, qs=qs, c0=c0: e.dma_start(
                    out=yscr[qt * 512 + qs * 128:qt * 512 + (qs + 1) * 128, c0:c0 + 256],
                    in_=ys[:]), "sYS")
                ys_slot[qs % 2].consumed(total=("sYS",))
            oe_slot[buf].consumed("sVE")

        def load_head(h):
            hb = h % 2
            c0 = h * HD
            p.wait("sync", "sQKS")
            kq_slot[hb].require("sync")
            for j, scr_c0 in enumerate([(kscr, c0), (kscr, c0 + 128), (qscr, c0), (qscr, c0 + 128)]):
                scr, cc0 = scr_c0
                tdma(lambda e, j=j, hb=hb, scr=scr, cc0=cc0: e.dma_start_transpose(
                    KQ2[hb][j][:], scr[:, cc0:cc0 + 128]), "sKT")
            kq_sem[hb] = p.cnt["sKT"]
            p.wait("gpsimd", "sVS")
            vt_slot[hb].require("gpsimd")
            vsem = "sVG" if hb == 0 else "sVG2"
            p.dma("gpsimd", lambda e, c0=c0, hb=hb: e.dma_start(
                out=Vt2[hb][:].rearrange("p (c j) -> p c j", j=257)[:, :, 0:256],
                in_=vscr[:, c0:c0 + 256].rearrange("(c p) j -> p c j", p=128)), vsem)
            vt_sem[hb] = (vsem, p.cnt[vsem])

        def emit_Shalf(Kh, Qh, m, half, qt):
            sb = psS2[m % 2]
            if half == 0:
                psS_slot[m % 2].require("tensor")
            pe(lambda e, Kh=Kh, Qh=Qh, m=m, half=half, qt=qt, sb=sb: e.matmul(
                sb[:, half * 512:(half + 1) * 512],
                Kh[:, (2 * m + half) * 128:(2 * m + half + 1) * 128],
                Qh[:, qt * 512:(qt + 1) * 512], start=True, stop=True), inc=True)
            return p.cnt["sPE"]

        load_head(0)
        for h in range(NHL):
            hb = h % 2
            if h + 1 < NHL:
                load_head(h + 1)
            if h == 1:
                p.wait("sync", "sWO", 16)
                for cc in range(NCC):
                    tdma(lambda e, cc=cc: e.dma_start_transpose(
                        woT[cc][:], woscr[:, cc * 128:(cc + 1) * 128]), "sWOT")
            p.wait("gpsimd", "sKT")
            CH13 = H13 // NHL
            p.dma("gpsimd", lambda e, h=h: e.dma_start(
                out=w13scr[h * CH13:(h + 1) * CH13, :], in_=w13[h * CH13:(h + 1) * CH13, :]), "sW13")
            K1, K2, Q1, Q2 = KQ2[hb]
            Vt = Vt2[hb]
            p.wait("tensor", "sKT", kq_sem[hb])
            p.wait("tensor", vt_sem[hb][0], vt_sem[hb][1])
            if h < 2:
                p.wait("tensor", "sVE")
            for qt in range(8):
                if len(ep_jobs) >= 2:
                    emit_epilogue(*ep_jobs.pop(0))
                mmax = (qt * 4 + 4) // 2
                for at in range(2):
                    Kh = K1 if at == 0 else K2
                    Qh = Q1 if at == 0 else Q2
                    s_done = [emit_Shalf(Kh, Qh, 0, 0, qt), emit_Shalf(Kh, Qh, 0, 1, qt)]
                    for m in range(mmax):
                        Pk = Pt[m % 2]
                        sb = psS2[m % 2]
                        exp_done = [0, 0]
                        for half in range(2):
                            p.wait("scalar", "sPE", s_done[half])
                            if half == 0:
                                pt_slot[m % 2].require("scalar")
                            se(lambda e, sb=sb, Pk=Pk, half=half: e.activation(
                                Pk[:, half * 512:(half + 1) * 512],
                                sb[:, half * 512:(half + 1) * 512], AF.Exp, scale=SCALE))
                            exp_done[half] = p.cnt["sSE"]
                        psS_slot[m % 2].consumed("sSE")
                        diag = m >= mmax - 2
                        dv2 = m - (mmax - 2) if diag else 0
                        msk_done = [0, 0]
                        if diag:
                            for half in range(2):
                                p.wait("vector", "sSE", exp_done[half])
                                if half == 0:
                                    ptm_slot[m % 2].require("vector")
                                ve(lambda e, Pk=Pk, dv2=dv2, m=m, half=half: e.tensor_tensor(
                                    Ptm[m % 2][:, half * 512:(half + 1) * 512],
                                    Pk[:, half * 512:(half + 1) * 512],
                                    msk[:, dv2 * 1024 + half * 512:dv2 * 1024 + (half + 1) * 512],
                                    ALU.mult))
                                msk_done[half] = p.cnt["sVE"]
                            lhsP = Ptm[m % 2]
                        else:
                            lhsP = Pk
                        for half in range(2):
                            kc = 2 * m + half
                            if half == 0 and m + 1 < mmax:
                                ns0 = emit_Shalf(Kh, Qh, m + 1, 0, qt)
                            if diag:
                                p.wait("tensor", "sVE", msk_done[half])
                            else:
                                p.wait("tensor", "sSE", exp_done[half])
                            if m == 0 and half == 0:
                                for qs in range(4):
                                    psO_slot[qs].require("tensor")
                            for qs in range(4):
                                pe(lambda e, lhsP=lhsP, qs=qs, kc=kc, half=half, mmax=mmax, Vt=Vt: e.matmul(
                                    psO[qs][:, 0:257],
                                    lhsP[:, half * 512 + qs * 128:half * 512 + (qs + 1) * 128],
                                    Vt[:, kc * 257:(kc + 1) * 257],
                                    start=(kc == 0), stop=(kc == 2 * mmax - 1)),
                                    inc=(qs == 3 and half == 1))
                            if half == 0 and m + 1 < mmax:
                                ns1 = emit_Shalf(Kh, Qh, m + 1, 1, qt)
                        if m + 1 < mmax:
                            s_done = [ns0, ns1]
                        if diag:
                            ptm_slot[m % 2].consumed("sPE")
                            pt_slot[m % 2].consumed("sPE", "sVE")
                        else:
                            pt_slot[m % 2].consumed("sPE")
                    pv_done = p.cnt["sPE"]
                    oe = oe2[qt % 2]
                    base = 0 if at == 0 else 4
                    p.wait("vector", "sPE", pv_done)
                    p.wait("scalar", "sPE", pv_done)
                    oe_slot[qt % 2].require("vector")
                    oe_slot[qt % 2].require("scalar")
                    for qs in range(4):
                        if qs % 2 == 0:
                            ve(lambda e, qs=qs, oe=oe, base=base: e.tensor_copy(
                                oe[:, (base + qs) * 257:(base + qs + 1) * 257], psO[qs][:, 0:257]))
                        else:
                            se(lambda e, qs=qs, oe=oe, base=base: e.copy(
                                oe[:, (base + qs) * 257:(base + qs + 1) * 257], psO[qs][:, 0:257]))
                    for qs in range(4):
                        psO_slot[qs].dep = [("sVE", p.cnt["sVE"]), ("sSE", p.cnt["sSE"])]
                ep_jobs.append((h, qt, qt % 2, p.cnt["sSE"]))
            kq_slot[hb].consumed("sPE")
            vt_slot[hb].consumed("sPE")
        while ep_jobs:
            emit_epilogue(*ep_jobs.pop(0))

        ar.reset(m_wo)
        yT = [ar.alloc([128, T], BF16) for _ in range(NCC)]
        apst = [ar.alloc([128, D], F32) for _ in range(2)]

        phase_barrier()
        p.dma("gpsimd", lambda e: e.dma_start(out=w2scr[:, 0:HF // 2], in_=w2[:, 0:HF // 2]), "sW2")
        p.dma("gpsimd", lambda e: e.dma_start(out=w2scr[:, HF // 2:HF], in_=w2[:, HF // 2:HF]), "sW2")
        p.wait("sync", "sYS")
        p.wait("sync", "sWO", 16)
        for cc in range(NCC):
            tdma(lambda e, cc=cc: e.dma_start_transpose(
                yT[cc][:], yscr[:, cc * 128:(cc + 1) * 128]), "sWOT")
        p.wait("tensor", "sWOT")
        wo_bank = [Slot(), Slot()]
        ap_slot = [Slot(), Slot()]
        for tc in range(TCA):
            banks = ps[0:4] if tc % 2 == 0 else ps[4:8]
            a_t = apst[tc % 2]
            wo_bank[tc % 2].require("tensor")
            for cc in range(NCC):
                for cs in range(4):
                    pe(lambda e, cc=cc, cs=cs, tc=tc, banks=banks: e.matmul(
                        banks[cs][:, 0:512], yT[cc][:, tc * 128:(tc + 1) * 128],
                        woT[cc][:, cs * 512:(cs + 1) * 512],
                        start=(cc == 0), stop=(cc == NCC - 1)),
                        inc=(cc == NCC - 1 and cs == 3))
            mm_done = p.cnt["sPE"]
            p.wait("vector", "sPE", mm_done)
            p.wait("scalar", "sPE", mm_done)
            ap_slot[tc % 2].require("vector")
            ap_slot[tc % 2].require("scalar")
            for cs in range(4):
                if cs % 2 == 0:
                    ve(lambda e, cs=cs, a_t=a_t, banks=banks: e.tensor_copy(
                        a_t[:, cs * 512:(cs + 1) * 512], banks[cs][:, 0:512]))
                else:
                    se(lambda e, cs=cs, a_t=a_t, banks=banks: e.copy(
                        a_t[:, cs * 512:(cs + 1) * 512], banks[cs][:, 0:512]))
            wo_bank[tc % 2].consumed("sVE", "sSE")
            p.wait("sync", "sVE")
            p.wait("sync", "sSE")
            p.dma("sync", lambda e, a_t=a_t, tc=tc: e.dma_start(
                out=apart[tc * 128:(tc + 1) * 128, :], in_=a_t[:]), "sAPS")
            ap_slot[tc % 2].consumed(total=("sAPS",))
            if tc % 8 == 7:
                ch = tc // 8
                p.wait("gpsimd", "sAPS")
                p.op("gpsimd", lambda e, ch=ch: e.collective_compute(
                    "ReduceScatter", ALU.add,
                    replica_groups=[[0, 1], [2, 3], [4, 5], [6, 7]],
                    ins=[apart[ch * 1024:(ch + 1) * 1024, :].opt()],
                    outs=[ascat[ch * 512:(ch + 1) * 512, :].opt()]), "sCC", 1)

        _sCC_save = p.cnt["sCC"]
        p.cnt["sCC"] = 0
        ar.reset(m_pre)
        NPR = 4
        r0t = [ar.alloc([128, D], F32) for _ in range(NPR)]
        xrt = [ar.alloc([128, D], F32) for _ in range(NPR)]
        xpt = [ar.alloc([128, D], F32) for _ in range(NPR)]
        tqt2 = [ar.alloc([128, D], F32) for _ in range(NPR)]
        zqt2 = [ar.alloc([128, D], I32) for _ in range(NPR)]
        xqt = [ar.alloc([128, D], BF16) for _ in range(NPR)]

        phase_barrier()
        p.cnt["sCC"] = _sCC_save
        ST = stat
        r_slot = [Slot() for _ in range(4)]
        xp_slot = [Slot() for _ in range(4)]
        xq_slot = [Slot() for _ in range(4)]
        for tc in range(TCM):
            pr = tc % 4
            SB0 = 12 * pr
            r_t, x_t, xp_t, xq_t = r0t[pr], xrt[pr], xpt[pr], xqt[pr]
            tq_t, zq_t = tqt2[pr], zqt2[pr]
            r_slot[pr].require("sync")
            p.wait("sync", "sCC", tc // 4 + 1)
            lsem = ["sLD", "sLD2", "sLD3", "sLD4"][pr]
            p.dma("sync", lambda e, r_t=r_t, tc=tc: e.dma_start(
                out=r_t[:], in_=ascat[tc * 128:(tc + 1) * 128, :]), lsem)
            p.dma("sync", lambda e, x_t=x_t, tc=tc: e.dma_start(
                out=x_t[:], in_=xmy[tc * 128:(tc + 1) * 128, :]), lsem)
            ld_done = p.cnt[lsem]
            p.wait("vector", lsem, ld_done)
            ve(lambda e, r_t=r_t, x_t=x_t: e.tensor_tensor(r_t[:], r_t[:], x_t[:], ALU.add))
            radd_done = p.cnt["sVE"]
            p.wait("scalar", "sVE", radd_done)
            se(lambda e, r_t=r_t, SB0=SB0: e.activation(trash_bf[:], r_t[:], AF.Copy, accum_out=ST[:, SB0:SB0 + 1]))
            se(lambda e, r_t=r_t, SB0=SB0: e.activation(trash_bf[:], r_t[:], AF.Square, accum_out=ST[:, SB0 + 1:SB0 + 2]))
            acc_done = p.cnt["sSE"]
            p.wait("vector", "sSE", acc_done)
            ve(lambda e, SB0=SB0: e.tensor_scalar(ST[:, SB0 + 2:SB0 + 3], ST[:, SB0:SB0 + 1], 1.0 / D, None, ALU.mult))
            ve(lambda e, SB0=SB0: e.tensor_scalar(ST[:, SB0 + 3:SB0 + 4], ST[:, SB0 + 1:SB0 + 2], 1.0 / D, None, ALU.mult))
            ve(lambda e, SB0=SB0: e.tensor_tensor(ST[:, SB0 + 4:SB0 + 5], ST[:, SB0 + 2:SB0 + 3], ST[:, SB0 + 2:SB0 + 3], ALU.mult))
            ve(lambda e, SB0=SB0: e.tensor_tensor(ST[:, SB0 + 5:SB0 + 6], ST[:, SB0 + 3:SB0 + 4], ST[:, SB0 + 4:SB0 + 5], ALU.subtract))
            var_done = p.cnt["sVE"]
            p.wait("scalar", "sVE", var_done)
            se(lambda e, SB0=SB0: e.activation(ST[:, SB0 + 6:SB0 + 7], ST[:, SB0 + 5:SB0 + 6], AF.Sqrt, bias=LN_EPS))
            p.wait("vector", "sSE")
            ve(lambda e, SB0=SB0: e.reciprocal(ST[:, SB0 + 7:SB0 + 8], ST[:, SB0 + 6:SB0 + 7]))
            xp_slot[pr].require("vector")
            ve(lambda e, r_t=r_t, xp_t=xp_t, SB0=SB0: e.tensor_scalar(
                xp_t[:], r_t[:], ST[:, SB0 + 2:SB0 + 3], ST[:, SB0 + 7:SB0 + 8], ALU.subtract, ALU.mult))
            r_slot[pr].consumed("sVE")
            xpw_done = p.cnt["sVE"]
            p.wait("gpsimd", "sVE", xpw_done)
            p.dma("gpsimd", lambda e, xp_t=xp_t, tc=tc: e.dma_start(
                out=xpscr[tc * 128:(tc + 1) * 128, :], in_=xp_t[:]), "sXPS")
            ve(lambda e, xp_t=xp_t, SB0=SB0: e.tensor_reduce(
                ST[:, SB0 + 8:SB0 + 9], xp_t[:], axis=AX.X, op=ALU.max, apply_absolute_value=True))
            ve(lambda e, SB0=SB0: e.tensor_scalar(ST[:, SB0 + 9:SB0 + 10], ST[:, SB0 + 8:SB0 + 9], 1e-5, 1.0 / 127.0, ALU.max, ALU.mult))
            ve(lambda e, SB0=SB0: e.reciprocal(ST[:, SB0 + 10:SB0 + 11], ST[:, SB0 + 9:SB0 + 10]))
            ve(lambda e, xp_t=xp_t, tq_t=tq_t, SB0=SB0: e.tensor_scalar(
                tq_t[:], xp_t[:], ST[:, SB0 + 10:SB0 + 11], None, ALU.mult))
            ve(lambda e, tq_t=tq_t, zq_t=zq_t: e.tensor_copy(zq_t[:], tq_t[:]))
            xp_slot[pr].consumed("sVE", total=("sXPS",))
            xq_slot[pr].require("vector")
            ve(lambda e, xq_t=xq_t, zq_t=zq_t, SB0=SB0: e.tensor_scalar(
                xq_t[:], zq_t[:], ST[:, SB0 + 9:SB0 + 10], None, ALU.mult))
            xqw_done = p.cnt["sVE"]
            p.wait("gpsimd", "sVE", xqw_done)
            p.dma("gpsimd", lambda e, xq_t=xq_t, tc=tc: e.dma_start(
                out=xqscr[tc * 128:(tc + 1) * 128, :], in_=xq_t[:]), "sXQS")
            xq_slot[pr].consumed(total=("sXQS",))

        ar.reset(m_pre)
        xqT = [ar.alloc([128, MYT], BF16) for _ in range(NDC)]
        w13g2 = [[ar.alloc([128, 512], BF16) for _ in range(2 * NDC)] for _ in range(2)]
        g42 = [[ar.alloc([128, 512], F32) for _ in range(3)] for _ in range(2)]
        tmp_bf = [ar.alloc([128, 512], BF16) for _ in range(2)]
        nwrow = ar.alloc([1, HF], F32)
        nwb = ar.alloc([128, HF], BF16)

        phase_barrier()
        p.wait("sync", "sXQS")
        for dc in range(NDC):
            tdma(lambda e, dc=dc: e.dma_start_transpose(
                xqT[dc][:], xqscr[:, dc * 128:(dc + 1) * 128]), "sXQT")
        p.dma("sync", lambda e: e.dma_start(out=nwrow[0:1, :], in_=nw_in), "sLD")
        p.wait("tensor", "sLD")
        for j in range(NPG):
            p.wait("tensor", "sVE")
            pe(lambda e, j=j: e.matmul(ps[6][:, 0:512], onesf32[0:1, :],
                                       nwrow[0:1, j * 512:(j + 1) * 512], start=True, stop=True), inc=True)
            p.wait("vector", "sPE")
            ve(lambda e, j=j: e.tensor_copy(nwb[:, j * 512:(j + 1) * 512], ps[6][:, 0:512]))

        FS = fstat
        nwb_done_sve = None
        w13_slot = [Slot(), Slot()]
        w13_sem = [0, 0]
        pA_slot = [Slot() for _ in range(4)]
        pB_slot = [Slot() for _ in range(4)]
        pB_slot[2].dep = [("sVE", p.cnt["sVE"])]
        gs_slot = [Slot(), Slot()]
        tb_slot = [Slot(), Slot()]

        def load_pg(pg):
            buf = pg % 2
            p.wait("sync", "sW13")
            w13_slot[buf].require("sync")
            for dc in range(NDC):
                tdma(lambda e, dc=dc, pg=pg, buf=buf: e.dma_start_transpose(
                    w13g2[buf][dc][:], w13scr[pg * 512:(pg + 1) * 512, dc * 128:(dc + 1) * 128]), "sW13T")
                tdma(lambda e, dc=dc, pg=pg, buf=buf: e.dma_start_transpose(
                    w13g2[buf][NDC + dc][:], w13scr[HF + pg * 512:HF + (pg + 1) * 512, dc * 128:(dc + 1) * 128]), "sW13T")
            w13_sem[buf] = p.cnt["sW13T"]

        load_pg(0)
        for pg in range(NPG):
            buf = pg % 2
            if pg + 1 < NPG:
                load_pg(pg + 1)
            w1g = w13g2[buf][0:NDC]
            w3g = w13g2[buf][NDC:2 * NDC]
            p.wait("tensor", "sW13T", w13_sem[buf])
            if pg == 0:
                p.wait("tensor", "sXQT")
            for tc in range(TCM):
                bA = ps[(tc % 4)]
                bB = ps[4 + (tc % 4)]
                pA_slot[tc % 4].require("tensor")
                for dc in range(NDC):
                    pe(lambda e, dc=dc, tc=tc, bA=bA, w1g=w1g: e.matmul(
                        bA[:, 0:512], xqT[dc][:, tc * 128:(tc + 1) * 128], w1g[dc][:],
                        start=(dc == 0), stop=(dc == NDC - 1)), inc=(dc == NDC - 1))
                pB_slot[tc % 4].require("tensor")
                for dc in range(NDC):
                    pe(lambda e, dc=dc, tc=tc, bB=bB, w3g=w3g: e.matmul(
                        bB[:, 0:512], xqT[dc][:, tc * 128:(tc + 1) * 128], w3g[dc][:],
                        start=(dc == 0), stop=(dc == NDC - 1)), inc=(dc == NDC - 1))
                mm_done = p.cnt["sPE"]
                r1_t, rr_t, g_t = g42[tc % 2]
                tb = tmp_bf[tc % 2]
                p.wait("scalar", "sPE", mm_done)
                gs_slot[tc % 2].require("scalar")
                se(lambda e, bA=bA, r1_t=r1_t: e.activation(r1_t[:], bA[:, 0:512], AF.Relu))
                pA_slot[tc % 4].consumed("sSE")
                se(lambda e, r1_t=r1_t, rr_t=rr_t: e.activation(rr_t[:], r1_t[:], AF.Square))
                sq_done = p.cnt["sSE"]
                p.wait("vector", "sSE", sq_done)
                ve(lambda e, rr_t=rr_t, g_t=g_t, bB=bB: e.tensor_tensor(
                    g_t[:], rr_t[:], bB[:, 0:512], ALU.mult))
                pB_slot[tc % 4].consumed("sVE")
                g_done = p.cnt["sVE"]
                p.wait("scalar", "sVE", g_done)
                se(lambda e, g_t=g_t, pg=pg, tc=tc: e.activation(
                    trash_bf[:, 0:512], g_t[:], AF.Square,
                    accum_out=FS[:, (pg * TCM + tc):(pg * TCM + tc) + 1]))
                p.wait("gpsimd", "sVE", g_done)
                tb_slot[tc % 2].require("gpsimd")
                ge(lambda e, g_t=g_t, tb=tb, pg=pg: e.tensor_tensor(
                    tb[:], g_t[:], nwb[:, pg * 512:(pg + 1) * 512], ALU.mult))
                gs_slot[tc % 2].consumed("sSE", "sGE")
                tt_done = p.cnt["sGE"]
                p.wait("vector", "sGE", tt_done)
                ve(lambda e, tb=tb, pg=pg, tc=tc: e.tensor_reduce(
                    FS[:, (NPG * TCM + pg * TCM + tc):(NPG * TCM + pg * TCM + tc) + 1],
                    tb[:], axis=AX.X, op=ALU.max, apply_absolute_value=True))
                p.wait("gpsimd", "sGE", tt_done)
                p.dma("gpsimd", lambda e, tb=tb, tc=tc, pg=pg: e.dma_start(
                    out=tmpscr[tc * 128:(tc + 1) * 128, pg * 512:(pg + 1) * 512],
                    in_=tb[:]), "sTMS")
                tb_slot[tc % 2].consumed("sVE", total=("sTMS",))
            w13_slot[buf].consumed("sPE")

        off5 = 2 * NPG * TCM
        p.wait("vector", "sSE")
        p.wait("vector", "sGE")
        for tc in range(TCM):
            ve(lambda e, tc=tc: e.tensor_reduce(
                FS[:, off5 + tc:off5 + tc + 1],
                FS[:, tc:NPG * TCM + tc:TCM], axis=AX.X, op=ALU.add))
            ve(lambda e, tc=tc: e.tensor_reduce(
                FS[:, off5 + TCM + tc:off5 + TCM + tc + 1],
                FS[:, NPG * TCM + tc:2 * NPG * TCM + tc:TCM], axis=AX.X, op=ALU.max))
        p.wait("scalar", "sVE")
        se(lambda e: e.activation(ST[:, 16:32], FS[:, off5:off5 + TCM], AF.Sqrt,
                                  bias=LN_EPS, scale=1.0 / HF))
        p.wait("vector", "sSE")
        ve(lambda e: e.reciprocal(ST[:, 16:32], ST[:, 16:32]))
        ve(lambda e: e.tensor_tensor(ST[:, 32:48], FS[:, off5 + TCM:off5 + 2 * TCM], ST[:, 16:32], ALU.mult))
        ve(lambda e: e.tensor_scalar(ST[:, 32:48], ST[:, 32:48], 1e-5, 1.0 / 127.0, ALU.max, ALU.mult))
        ve(lambda e: e.reciprocal(ST[:, 48:64], ST[:, 32:48]))
        ve(lambda e: e.tensor_tensor(ST[:, 16:32], ST[:, 16:32], ST[:, 48:64], ALU.mult))
        rsv_done = p.cnt["sVE"]
        p.wait("gpsimd", "sVE", rsv_done)
        for tc in range(TCM):
            p.dma("gpsimd", lambda e, tc=tc: e.dma_start(
                out=rsscr[0:1, tc * 128:(tc + 1) * 128].rearrange("o (j i) -> (o j) i", i=1),
                in_=ST[:, 16 + tc:17 + tc]), "sRS")
            p.dma("gpsimd", lambda e, tc=tc: e.dma_start(
                out=rsscr[1:2, tc * 128:(tc + 1) * 128].rearrange("o (j i) -> (o j) i", i=1),
                in_=ST[:, 32 + tc:33 + tc]), "sRS")

        ar.reset(m_pre)
        rs_row = ar.alloc([1, 2 * MYT], F32)
        rs_b = ar.alloc([128, MYT], BF16)
        inv2_b = ar.alloc([128, MYT], F32)
        t1_t52 = [ar.alloc([128, MYT], F32) for _ in range(2)]
        z52 = [ar.alloc([128, MYT], I32) for _ in range(2)]
        tmpf = [ar.alloc([128, MYT], BF16) for _ in range(2)]
        iq5 = [ar.alloc([128, MYT], BF16) for _ in range(2)]

        phase_barrier()
        p.wait("sync", "sRS")
        p.dma("sync", lambda e: e.dma_start(out=rs_row[0:1, :], in_=rsscr.rearrange("(o a) j -> o (a j)", o=1)), "sLD")
        p.wait("tensor", "sLD")
        p.wait("tensor", "sSE")
        for j in range(MYT // 512):
            p.wait("tensor", "sVE")
            pe(lambda e, j=j: e.matmul(ps[0][:, 0:512], onesf32[0:1, :],
                                       rs_row[0:1, j * 512:(j + 1) * 512], start=True, stop=True), inc=True)
            pe(lambda e, j=j: e.matmul(ps[1][:, 0:512], onesf32[0:1, :],
                                       rs_row[0:1, MYT + j * 512:MYT + (j + 1) * 512], start=True, stop=True), inc=True)
            p.wait("vector", "sPE")
            ve(lambda e, j=j: e.tensor_copy(rs_b[:, j * 512:(j + 1) * 512], ps[0][:, 0:512]))
            ve(lambda e, j=j: e.tensor_copy(inv2_b[:, j * 512:(j + 1) * 512], ps[1][:, 0:512]))
        p.wait("tensor", "sVE")

        p.wait("sync", "sTMS")
        tf_slot = [Slot(), Slot()]
        tz_slot = [Slot(), Slot()]
        iq_slot = [Slot(), Slot()]
        for hc in range(NHC):
            pr = hc % 2
            tf = tmpf[pr]
            iqt = iq5[pr]
            t1_t5, z5 = t1_t52[pr], z52[pr]
            tf_slot[pr].require("sync")
            tdma(lambda e, tf=tf, hc=hc: e.dma_start_transpose(
                tf[:], tmpscr[:, hc * 128:(hc + 1) * 128]), "sTMT")
            tmt_done = p.cnt["sTMT"]
            p.wait("vector", "sTMT", tmt_done)
            tz_slot[pr].require("vector")
            if pr == 0:
                ve(lambda e, tf=tf, t1_t5=t1_t5: e.tensor_tensor(t1_t5[:], tf[:], rs_b[:], ALU.mult))
                tf_slot[pr].consumed("sVE")
                p.wait("vector", "sVE")
            else:
                p.wait("gpsimd", "sTMT", tmt_done)
                tz_slot[pr].require("gpsimd")
                ge(lambda e, tf=tf, t1_t5=t1_t5: e.tensor_tensor(t1_t5[:], tf[:], rs_b[:], ALU.mult))
                tf_slot[pr].consumed("sGE")
                p.wait("vector", "sGE")
            ve(lambda e, t1_t5=t1_t5, z5=z5: e.tensor_copy(z5[:], t1_t5[:]))
            iq_slot[pr].require("vector")
            ve(lambda e, iqt=iqt, z5=z5: e.tensor_tensor(iqt[:], z5[:], inv2_b[:], ALU.mult))
            tz_slot[pr].consumed("sVE", "sGE")
            iqw = ("sVE", p.cnt["sVE"])
            p.wait("gpsimd", iqw[0], iqw[1])
            p.dma("gpsimd", lambda e, iqt=iqt, hc=hc: e.dma_start(
                out=iqscr[hc], in_=iqt[:]), "sIQS")
            iq_slot[pr].consumed(total=("sIQS",))

        ar.reset(m_pre)
        NG5 = 4
        GSZ = NHC // NG5
        w2T52 = [[ar.alloc([128, D], BF16) for _ in range(GSZ)] for _ in range(2)]
        iqg = [ar.alloc([128, 128 * (TCM // 2)], BF16) for _ in range(GSZ)]
        acc = [ar.alloc([128, D], F32) for _ in range(TCM // 2)]
        ot = [ar.alloc([128, D], F32) for _ in range(2)]

        w2t_slot = [Slot(), Slot()]
        w2t_sem = [0, 0]
        iqg_slot = Slot()
        acc_slot = [Slot() for _ in range(TCM // 2)]
        b5_slot = [Slot(), Slot()]
        ot_slot = [Slot(), Slot()]

        def load_w2t(gg):
            buf = gg % 2
            p.wait("sync", "sW2")
            w2t_slot[buf].require("sync")
            for i in range(GSZ):
                hc = (gg % NG5) * GSZ + i
                tdma(lambda e, i=i, hc=hc, buf=buf: e.dma_start_transpose(
                    w2T52[buf][i][:], w2scr[:, hc * 128:(hc + 1) * 128]), "sW2T")
            w2t_sem[buf] = p.cnt["sW2T"]

        def emit_ln2(th, i):
            tc = th * (TCM // 2) + i
            pr = i % 2
            SB0 = 12 * pr
            o_t = ot[pr]
            p.wait("scalar", "sVE")
            se(lambda e, i=i, SB0=SB0: e.activation(trash_bf[:], acc[i][:], AF.Copy, accum_out=ST[:, SB0:SB0 + 1]))
            se(lambda e, i=i, SB0=SB0: e.activation(trash_bf[:], acc[i][:], AF.Square, accum_out=ST[:, SB0 + 1:SB0 + 2]))
            acc_done2 = p.cnt["sSE"]
            p.wait("vector", "sSE", acc_done2)
            ve(lambda e, SB0=SB0: e.tensor_scalar(ST[:, SB0 + 2:SB0 + 3], ST[:, SB0:SB0 + 1], 1.0 / D, None, ALU.mult))
            ve(lambda e, SB0=SB0: e.tensor_scalar(ST[:, SB0 + 3:SB0 + 4], ST[:, SB0 + 1:SB0 + 2], 1.0 / D, None, ALU.mult))
            ve(lambda e, SB0=SB0: e.tensor_tensor(ST[:, SB0 + 4:SB0 + 5], ST[:, SB0 + 2:SB0 + 3], ST[:, SB0 + 2:SB0 + 3], ALU.mult))
            ve(lambda e, SB0=SB0: e.tensor_tensor(ST[:, SB0 + 5:SB0 + 6], ST[:, SB0 + 3:SB0 + 4], ST[:, SB0 + 4:SB0 + 5], ALU.subtract))
            var_done2 = p.cnt["sVE"]
            p.wait("scalar", "sVE", var_done2)
            se(lambda e, SB0=SB0: e.activation(ST[:, SB0 + 6:SB0 + 7], ST[:, SB0 + 5:SB0 + 6], AF.Sqrt, bias=LN_EPS))
            p.wait("vector", "sSE")
            ve(lambda e, SB0=SB0: e.reciprocal(ST[:, SB0 + 7:SB0 + 8], ST[:, SB0 + 6:SB0 + 7]))
            ot_slot[pr].require("vector")
            ve(lambda e, i=i, o_t=o_t, SB0=SB0: e.tensor_scalar(
                o_t[:], acc[i][:], ST[:, SB0 + 2:SB0 + 3], ST[:, SB0 + 7:SB0 + 8], ALU.subtract, ALU.mult))
            acc_slot[i].consumed("sVE")
            ow_done = p.cnt["sVE"]
            p.wait("sync", "sVE", ow_done)
            p.dma("sync", lambda e, o_t=o_t, tc=tc: e.dma_start(
                out=out_c[tc * 128:(tc + 1) * 128, :], in_=o_t[:]), "sOUT")
            ot_slot[pr].consumed(total=("sOUT",))

        first = True
        for th in range(2):
            if first:
                phase_barrier(["sync", "scalar"])
            p.wait("sync", "sXPS")
            p.wait("sync", "sVE")
            p.wait("sync", "sSE")
            for i in range(TCM // 2):
                tc = th * (TCM // 2) + i
                acc_slot[i].require("sync")
                p.dma("sync", lambda e, i=i, tc=tc: e.dma_start(
                    out=acc[i][:], in_=xpscr[tc * 128:(tc + 1) * 128, :]), "sACB")
            acb_done = p.cnt["sACB"]
            if first:
                load_w2t(0)
                first = False
            for g in range(NG5):
                gg = th * NG5 + g
                buf = gg % 2
                if gg + 1 < 2 * NG5:
                    load_w2t(gg + 1)
                p.wait("scalar", "sIQS")
                iqg_slot.require("scalar")
                for i in range(GSZ):
                    hc = g * GSZ + i
                    p.dma("scalar", lambda e, i=i, hc=hc, th=th: e.dma_start(
                        out=iqg[i][:], in_=iqscr[hc][:, th * (MYT // 2):(th + 1) * (MYT // 2)]), "sIQL")
                iql_done = p.cnt["sIQL"]
                p.wait("tensor", "sW2T", w2t_sem[buf])
                p.wait("tensor", "sIQL", iql_done)
                p.wait("tensor", "sACB", acb_done)
                for i in range(TCM // 2):
                    banks = ps[0:4] if i % 2 == 0 else ps[4:8]
                    b5_slot[i % 2].require("tensor")
                    for j in range(GSZ):
                        for cs in range(4):
                            pe(lambda e, i=i, j=j, cs=cs, banks=banks, buf=buf: e.matmul(
                                banks[cs][:, 0:512], iqg[j][:, i * 128:(i + 1) * 128],
                                w2T52[buf][j][:, cs * 512:(cs + 1) * 512],
                                start=(j == 0), stop=(j == GSZ - 1)),
                                inc=(j == GSZ - 1 and cs == 3))
                    mm_done = p.cnt["sPE"]
                    p.wait("vector", "sPE", mm_done)
                    p.wait("vector", "sACB", acb_done)
                    acc_slot[i].require("vector")
                    for cs in range(4):
                        ve(lambda e, i=i, cs=cs, banks=banks: e.tensor_tensor(
                            acc[i][:, cs * 512:(cs + 1) * 512], acc[i][:, cs * 512:(cs + 1) * 512],
                            banks[cs][:, 0:512], ALU.add))
                    b5_slot[i % 2].consumed("sVE")
                    if g == NG5 - 1:
                        emit_ln2(th, i)
                iqg_slot.consumed("sPE")
                w2t_slot[buf].consumed("sPE")
        if DEBUG:
            p.barrier(["gpsimd"], ["sQKS", "sVS", "sYS", "sAPS", "sCC", "sXPS",
                                   "sXQS", "sTMS", "sOUT"])
            for name, (dst, src) in dbg.items():
                p.dma("gpsimd", lambda e, dst=dst, src=src: e.dma_start(out=dst, in_=src), "sDBG")
            p.wait("gpsimd", "sDBG")

        for eng in ALL_ENG:
            p.wait(eng, "sOUT")

        with nc.Block() as block:
            def mk(engname):
                def run(e):
                    for it in p.items:
                        if it[0] != engname:
                            continue
                        if it[1] == "wait":
                            e.wait_ge(sems[it[2]], it[3])
                        else:
                            instr = it[2](e)
                            if it[3] is not None:
                                instr.then_inc(sems[it[3]], it[4])
                return run
            block.sync(mk("sync"))
            block.scalar(mk("scalar"))
            block.vector(mk("vector"))
            block.tensor(mk("tensor"))
            block.gpsimd(mk("gpsimd"))

    return nc


def _host_consts(hg):
    inv_freq = ((np.float32(1.0) / np.float32(ROPE_BASE)) **
                (np.arange(0, HH, 2, dtype=np.float32) / np.float32(HH))).astype(np.float64)
    rotc = np.zeros((128, CQ), np.float32)
    rots = np.zeros((128, CQ), np.float32)
    for hl in range(NHL):
        hgl = hg * NHL + hl
        c = np.cos(np.float64(hgl) * inv_freq).astype(np.float32)
        s = np.sin(np.float64(hgl) * inv_freq).astype(np.float32)
        for half in range(2):
            base = hl * HD + half * HH
            rotc[:, base:base + 64] = c
            rotc[:, base + 64:base + 128] = c
            rots[:, base:base + 64] = s
            rots[:, base + 64:base + 128] = -s
    ki = np.arange(128)[:, None]
    qj = np.arange(512)[None, :]
    m4 = np.zeros((4, 128, 512), np.float32)
    for d in range(4):
        m4[d] = (qj >= ki + d * 128).astype(np.float32)
    mskv = np.zeros((2, 128, 1024), np.float32)
    mskv[0, :, 0:512] = m4[0]
    mskv[0, :, 512:1024] = m4[1]
    mskv[1, :, 0:512] = m4[2]
    mskv[1, :, 512:1024] = m4[3]
    return rotc, rots, mskv.astype(ml_dtypes.bfloat16)




def my_token_ids(hg):
    return np.concatenate([np.arange(c * 1024 + hg * 512, c * 1024 + (hg + 1) * 512)
                           for c in range(T // 1024)])


def make_in_maps(x, wqf, wkf, wvf, wof, w13f, w2f, nwf, lqkf):
    in_maps = []
    for c in range(8):
        b, hg = c // 2, c % 2
        rotcv, rotsv, mskv = _host_consts(hg)
        in_maps.append(dict(
            xb=x[b],
            xmy=np.ascontiguousarray(x[b, my_token_ids(hg)]),
            wq=np.ascontiguousarray(wqf[hg * CQ:(hg + 1) * CQ]),
            wk=np.ascontiguousarray(wkf[hg * CQ:(hg + 1) * CQ]),
            wv=np.ascontiguousarray(wvf[hg * CQ:(hg + 1) * CQ]),
            wo=np.ascontiguousarray(wof[:, hg * CQ:(hg + 1) * CQ]),
            w13=w13f, w2=w2f, nw_in=nwf, lqk=lqkf,
            rotc_in=rotcv, rots_in=rotsv, msk_in=mskv,
        ))
    return in_maps

def kernel(L0_attn_x, L0_attn_wq, L0_attn_wk, L0_attn_wv, L0_attn_wo,
           L0_attn_lq1, L0_attn_lk1, L0_attn_lq2, L0_attn_lk2,
           L0_ffn_w13, L0_ffn_w2, L0_ffn_norm_w):
    x = np.ascontiguousarray(np.asarray(L0_attn_x, dtype=np.float32))
    wqf = np.ascontiguousarray(np.asarray(L0_attn_wq, dtype=np.float32))
    wkf = np.ascontiguousarray(np.asarray(L0_attn_wk, dtype=np.float32))
    wvf = np.ascontiguousarray(np.asarray(L0_attn_wv, dtype=np.float32))
    wof = np.ascontiguousarray(np.asarray(L0_attn_wo, dtype=np.float32))
    w13f = np.ascontiguousarray(np.asarray(L0_ffn_w13, dtype=np.float32))
    w2f = np.ascontiguousarray(np.asarray(L0_ffn_w2, dtype=np.float32))
    nwf = np.ascontiguousarray(np.asarray(L0_ffn_norm_w, dtype=np.float32)).reshape(1, HF)
    lqkf = np.ascontiguousarray(np.stack([np.asarray(v, dtype=np.float32) for v in
                                          (L0_attn_lq1, L0_attn_lk1, L0_attn_lq2, L0_attn_lk2)]))

    nc = build_nc()
    in_maps = make_in_maps(x, wqf, wkf, wvf, wof, w13f, w2f, nwf, lqkf)
    res = run_bass_kernel_spmd(nc, in_maps, core_ids=list(range(8)))
    out = np.empty((B, T, D), dtype=np.float32)
    for c in range(8):
        b, hg = c // 2, c % 2
        out[b, my_token_ids(hg)] = res.results[c]["out_c"]
    return out


# version 13
# speedup vs baseline: 1.0610x; 1.0610x over previous
import math
from collections import Counter
from contextlib import ExitStack

import numpy as np
import ml_dtypes

import concourse.bass as bass
import concourse.mybir as mybir
from concourse.bass_utils import run_bass_kernel_spmd

F32 = mybir.dt.float32
BF16 = mybir.dt.bfloat16
I32 = mybir.dt.int32
AF = mybir.ActivationFunctionType
ALU = mybir.AluOpType
AX = mybir.AxisListType

B, T, D = 4, 4096, 2048
NHL = 4
HD, HH = 256, 128
CQ = NHL * HD
HF = 5632
H13 = 2 * HF
MYT = 2048
TCA = T // 128
TCM = MYT // 128
NDC = D // 128
NCC = CQ // 128
NPG = HF // 512
NHC = HF // 128
LAMBDA_INIT = 0.2
RMS_EPS = 1.1920929e-07
LN_EPS = 1e-05
ROPE_BASE = 10000.0
SCALE = 1.0 / math.sqrt(128.0)

DEBUG = False


class Prog:
    def __init__(self):
        self.items = []
        self.cnt = Counter()

    def op(self, eng, fn, inc=None, n=1):
        if inc is not None:
            self.cnt[inc] += n
        self.items.append((eng, "op", fn, inc, n))

    def dma(self, eng, fn, inc):
        self.op(eng, fn, inc, 16)

    def wait(self, eng, sem, val=None):
        v = self.cnt[sem] if val is None else val
        if v > 0:
            self.items.append((eng, "wait", sem, v))

    def barrier(self, engines, sems):
        for e in engines:
            for s in sems:
                self.wait(e, s)


def build_nc():
    nc = bass.Bass("TRN2", target_bir_lowering=False)

    xb = nc.dram_tensor("xb", [T, D], F32, kind="ExternalInput").ap()
    xmy = nc.dram_tensor("xmy", [MYT, D], F32, kind="ExternalInput").ap()
    wq = nc.dram_tensor("wq", [CQ, D], F32, kind="ExternalInput").ap()
    wk = nc.dram_tensor("wk", [CQ, D], F32, kind="ExternalInput").ap()
    wv = nc.dram_tensor("wv", [CQ, D], F32, kind="ExternalInput").ap()
    wo = nc.dram_tensor("wo", [D, CQ], F32, kind="ExternalInput").ap()
    w13 = nc.dram_tensor("w13", [H13, D], F32, kind="ExternalInput").ap()
    w2 = nc.dram_tensor("w2", [D, HF], F32, kind="ExternalInput").ap()
    nw_in = nc.dram_tensor("nw_in", [1, HF], F32, kind="ExternalInput").ap()
    lqk = nc.dram_tensor("lqk", [4, HD], F32, kind="ExternalInput").ap()
    rotc_in = nc.dram_tensor("rotc_in", [128, CQ], F32, kind="ExternalInput").ap()
    rots_in = nc.dram_tensor("rots_in", [128, CQ], F32, kind="ExternalInput").ap()
    msk_in = nc.dram_tensor("msk_in", [2, 128, 1024], BF16, kind="ExternalInput").ap()

    out_c = nc.dram_tensor("out_c", [MYT, D], F32, kind="ExternalOutput").ap()

    xscr = nc.dram_tensor("xscr", [T, D], BF16, kind="Internal").ap()
    wqscr = nc.dram_tensor("wqscr", [CQ, D], BF16, kind="Internal").ap()
    wkscr = nc.dram_tensor("wkscr", [CQ, D], BF16, kind="Internal").ap()
    wvscr = nc.dram_tensor("wvscr", [CQ, D], BF16, kind="Internal").ap()
    woscr = nc.dram_tensor("woscr", [D, CQ], BF16, kind="Internal").ap()
    w13scr = nc.dram_tensor("w13scr", [H13, D], BF16, kind="Internal").ap()
    w2scr = nc.dram_tensor("w2scr", [D, HF], BF16, kind="Internal").ap()
    qscr = nc.dram_tensor("qscr", [T, CQ], BF16, kind="Internal").ap()
    kscr = nc.dram_tensor("kscr", [T, CQ], BF16, kind="Internal").ap()
    vscr = nc.dram_tensor("vscr", [T, CQ], BF16, kind="Internal").ap()
    yscr = nc.dram_tensor("yscr", [T, CQ], BF16, kind="Internal").ap()
    apart = nc.dram_tensor("apart", [T, D], F32, kind="Internal").ap()
    ascat = nc.dram_tensor("ascat", [MYT, D], F32, kind="Internal").ap()
    xpscr = nc.dram_tensor("xpscr", [MYT, D], F32, kind="Internal").ap()
    xqscr = nc.dram_tensor("xqscr", [MYT, D], BF16, kind="Internal").ap()
    tmpscr = nc.dram_tensor("tmpscr", [MYT, HF], BF16, kind="Internal").ap()
    iqscr = nc.dram_tensor("iqscr", [NHC, 128, MYT], BF16, kind="Internal").ap()
    rsscr = nc.dram_tensor("rsscr", [2, MYT], F32, kind="Internal").ap()

    dbg = {}
    if DEBUG:
        for name, src in [("d_qscr", qscr), ("d_kscr", kscr), ("d_vscr", vscr),
                          ("d_yscr", yscr), ("d_apart", apart), ("d_ascat", ascat),
                          ("d_xpscr", xpscr), ("d_xqscr", xqscr), ("d_tmpscr", tmpscr),
                          ("d_iqscr", iqscr)]:
            dbg[name] = (nc.dram_tensor(name, list(src.shape), src.dtype,
                                        kind="ExternalOutput").ap(), src)

    stack = ExitStack()
    with stack:
        consts = stack.enter_context(nc.sbuf_tensor("consts", [128, 8], F32))
        CONST_VALS = [SCALE, 1.0 / 128.0, RMS_EPS, LN_EPS, 256.5]
        for i, v in enumerate(CONST_VALS):
            nc.const_aps.register(F32, v, consts[:, i:i + 1])
        onesf32 = stack.enter_context(nc.sbuf_tensor("onesf32", [1, 128], F32))
        lamt = stack.enter_context(nc.sbuf_tensor("lamt", [1, 1544], F32))
        neg08lam = stack.enter_context(nc.sbuf_tensor("neg08lam", [128, 1], F32))
        stat = stack.enter_context(nc.sbuf_tensor("stat", [128, 64], F32))
        fstat = stack.enter_context(nc.sbuf_tensor("fstat", [128, 2 * TCM * NPG + 2 * TCM], F32))
        trash_bf = stack.enter_context(nc.sbuf_tensor("trash_bf", [128, D], BF16))

        psT = [stack.enter_context(nc.psum_tensor(f"ps{i}", [128, 512], F32))
               for i in range(4)]
        psB = [stack.enter_context(nc.psum_tensor(f"psb{i}", [128, 1024], F32))
               for i in range(2)]
        ps = psT + [psB[0][:, 0:512], psB[0][:, 512:1024],
                    psB[1][:, 0:512], psB[1][:, 512:1024]]

        SBUF_TOP = 229376
        base0 = SBUF_TOP - nc.sbuf_bytes_remaining

        class Arena:
            def __init__(self, base, top):
                self.off = base
                self.top = top
                self.n = 0

            def alloc(self, shape, dtype):
                per = int(np.prod(shape[1:])) * mybir.dt.size(dtype)
                off = (self.off + 31) & ~31
                self.off = off + per
                assert self.off <= self.top, f"SBUF overflow: {self.off} > {self.top}"
                self.n += 1
                return nc.alloc_sbuf_tensor_at(f"ar{self.n}", shape, dtype, offset=off).ap()

            def mark(self):
                return self.off

            def reset(self, off):
                self.off = off

        ar = Arena(base0 + 32, SBUF_TOP - 64)
        m_pre = ar.mark()
        rotc = ar.alloc([128, CQ], F32)
        rots = ar.alloc([128, CQ], F32)
        msk = ar.alloc([128, 2 * 1024], BF16)

        sem_names = ["sXS", "sWQ", "sWK", "sWV", "sWO", "sW13", "sW2",
                     "sXT", "sWT", "sQKS", "sVS", "sKT", "sVG", "sYS", "sWOT",
                     "sAPS", "sCC", "sLD", "sXPS", "sXQS", "sXQT", "sW13T",
                     "sTMS", "sTMT", "sIQS", "sIQL", "sW2T", "sRS", "sACB",
                     "sOUT", "sPE", "sVE", "sSE", "sGE", "sIN", "sDBG", "sVG2", "sLD2", "sLD3", "sLD4", "sXS2"]
        sems = {n: stack.enter_context(nc.semaphore(n)) for n in sem_names}
        p = Prog()
        ALL_ENG = ["sync", "scalar", "vector", "tensor", "gpsimd"]
        ALLS = list(sem_names)

        def phase_barrier(engines=ALL_ENG):
            for e in engines:
                for sname in ALLS:
                    p.wait(e, sname)

        def ve(fn):
            p.wait("vector", "sVE")
            p.op("vector", fn, "sVE")

        def se(fn):
            p.wait("scalar", "sSE")
            p.op("scalar", fn, "sSE")

        def ge(fn):
            p.wait("gpsimd", "sGE")
            p.op("gpsimd", fn, "sGE")

        def pe(fn, inc=False):
            p.op("tensor", fn, "sPE" if inc else None)

        class Slot:
            def __init__(self):
                self.dep = []

            def require(self, eng):
                for sem, v in self.dep:
                    p.wait(eng, sem, p.cnt[sem] if v is None else v)

            def consumed(self, *sems, total=()):
                self.dep = [(x, p.cnt[x]) for x in sems] + [(x, None) for x in total]

        def tdma(fn, sem):
            p.dma("sync", fn, sem)
            p.wait("sync", sem)

        for i, v in enumerate(CONST_VALS):
            ve(lambda e, i=i, v=v: e.memset(consts[:, i:i + 1], v))
        ve(lambda e: e.memset(onesf32[0:1, :], 1.0))
        p.dma("gpsimd", lambda e: e.dma_start(out=rotc[:], in_=rotc_in), "sIN")
        p.dma("gpsimd", lambda e: e.dma_start(out=rots[:], in_=rots_in), "sIN")
        p.dma("gpsimd", lambda e: e.dma_start(
            out=msk[:].rearrange("p (c j) -> p c j", j=1024),
            in_=msk_in.rearrange("c p j -> p c j")), "sIN")
        p.dma("gpsimd", lambda e: e.dma_start(
            out=lamt[0:1, 0:1024], in_=lqk.rearrange("(o a) j -> o (a j)", o=1)), "sIN")

        p.dma("gpsimd", lambda e: e.dma_start(out=xscr[:, 0:D // 2], in_=xb[:, 0:D // 2]), "sXS")
        p.dma("gpsimd", lambda e: e.dma_start(out=wqscr, in_=wq), "sWQ")
        p.dma("gpsimd", lambda e: e.dma_start(out=xscr[:, D // 2:D], in_=xb[:, D // 2:D]), "sXS2")
        p.dma("gpsimd", lambda e: e.dma_start(out=wkscr, in_=wk), "sWK")
        p.dma("gpsimd", lambda e: e.dma_start(out=wvscr, in_=wv), "sWV")

        p.wait("vector", "sIN")
        L = lamt
        ve(lambda e: e.tensor_tensor(L[0:1, 1024:1280], L[0:1, 0:256], L[0:1, 256:512], ALU.mult))
        ve(lambda e: e.tensor_tensor(L[0:1, 1280:1536], L[0:1, 512:768], L[0:1, 768:1024], ALU.mult))
        p.wait("vector", "sVE")
        ve(lambda e: e.reduce_sum(L[0:1, 1536:1537], L[0:1, 1024:1280], axis=AX.X))
        ve(lambda e: e.reduce_sum(L[0:1, 1537:1538], L[0:1, 1280:1536], axis=AX.X))
        p.wait("scalar", "sVE")
        se(lambda e: e.activation(L[0:1, 1538:1540], L[0:1, 1536:1538], AF.Exp))
        p.wait("vector", "sSE")
        ve(lambda e: e.tensor_tensor(L[0:1, 1540:1541], L[0:1, 1538:1539], L[0:1, 1539:1540], ALU.subtract))
        p.wait("vector", "sVE")
        ve(lambda e: e.tensor_scalar(L[0:1, 1541:1542], L[0:1, 1540:1541], LAMBDA_INIT, -0.8, ALU.add, ALU.mult))
        p.wait("tensor", "sVE")
        p.wait("tensor", "sIN")
        pe(lambda e: e.matmul(ps[0][:, 0:1], onesf32[0:1, :], L[0:1, 1541:1542], start=True, stop=True), inc=True)
        p.wait("vector", "sPE")
        ve(lambda e: e.tensor_copy(neg08lam[:], ps[0][:, 0:1]))

        m0 = ar.mark()
        xT = [ar.alloc([128, T], BF16) for _ in range(NDC)]
        wTg2 = [[ar.alloc([128, 512], BF16) for _ in range(NDC)] for _ in range(2)]
        stg2 = [[ar.alloc([128, 512], F32) for _ in range(4)] for _ in range(2)]
        stg_bf = [ar.alloc([128, 512], BF16) for _ in range(2)]
        seg42 = [ar.alloc([128, 8], F32) for _ in range(2)]

        p.wait("sync", "sXS")
        for dc in range(NDC // 2):
            tdma(lambda e, dc=dc: e.dma_start_transpose(
                xT[dc][:], xscr[:, dc * 128:(dc + 1) * 128]), "sXT")
        p.wait("sync", "sXS2")
        for dc in range(NDC // 2, NDC):
            tdma(lambda e, dc=dc: e.dma_start_transpose(
                xT[dc][:], xscr[:, dc * 128:(dc + 1) * 128]), "sXT")

        groups = []
        for cg in range(CQ // 512):
            for wsem, wscrt, dst_scr, is_qk, dsem in [
                    ("sWQ", wqscr, qscr, True, "sQKS"),
                    ("sWV", wvscr, vscr, False, "sVS"),
                    ("sWK", wkscr, kscr, True, "sQKS")]:
                groups.append((wsem, wscrt, dst_scr, is_qk, dsem, cg))

        wtg_slot = [Slot(), Slot()]
        wtg_sem = [0, 0]
        pb_slot = [Slot() for _ in range(4)]
        pb_slot[0].dep = [("sVE", p.cnt["sVE"])]
        sq_slot = [Slot() for _ in range(2)]
        qn_slot = [Slot() for _ in range(2)]
        t12_slot = [Slot() for _ in range(2)]
        sb_slot = [Slot() for _ in range(2)]
        seg_slot = [Slot() for _ in range(2)]

        def load_group(gi):
            wsem, wscrt, _, _, _, cg = groups[gi]
            buf = gi % 2
            p.wait("sync", wsem, 16)
            wtg_slot[buf].require("sync")
            for dc in range(NDC):
                tdma(lambda e, dc=dc, cg=cg, wscrt=wscrt, buf=buf: e.dma_start_transpose(
                    wTg2[buf][dc][:], wscrt[cg * 512:(cg + 1) * 512, dc * 128:(dc + 1) * 128]), "sWT")
            wtg_sem[buf] = p.cnt["sWT"]

        p.wait("gpsimd", "sXT")
        p.dma("gpsimd", lambda e: e.dma_start(out=woscr, in_=wo), "sWO")
        load_group(0)
        for gi, (wsem, wscrt, dst_scr, is_qk, dsem, cg) in enumerate(groups):
            buf = gi % 2
            if gi + 1 < len(groups):
                load_group(gi + 1)
            wTg = wTg2[buf]
            p.wait("tensor", "sWT", wtg_sem[buf])
            if gi == 0:
                p.wait("tensor", "sXT")
            for tc in range(TCA):
                bank = ps[tc % 4]
                pb_slot[tc % 4].require("tensor")
                for dc in range(NDC):
                    pe(lambda e, dc=dc, tc=tc, bank=bank, wTg=wTg: e.matmul(
                        bank[:, 0:512], xT[dc][:, tc * 128:(tc + 1) * 128],
                        wTg[dc][:, 0:512], start=(dc == 0), stop=(dc == NDC - 1)),
                        inc=(dc == NDC - 1))
                mm_done = p.cnt["sPE"]
                if not is_qk:
                    sb = stg_bf[tc % 2]
                    p.wait("scalar", "sPE", mm_done)
                    sb_slot[tc % 2].require("scalar")
                    se(lambda e, bank=bank, sb=sb: e.copy(sb[:], bank[:, 0:512]))
                    pb_slot[tc % 4].consumed("sSE")
                    p.wait("gpsimd", "sSE")
                    p.dma("gpsimd", lambda e, sb=sb, tc=tc, cg=cg, dst_scr=dst_scr: e.dma_start(
                        out=dst_scr[tc * 128:(tc + 1) * 128, cg * 512:(cg + 1) * 512],
                        in_=sb[:]), dsem)
                    sb_slot[tc % 2].consumed(total=(dsem,))
                else:
                    sq_t, qn_t, t1_t, t2_t = stg2[tc % 2]
                    seg4 = seg42[tc % 2]
                    sb = stg_bf[tc % 2]
                    p.wait("scalar", "sPE", mm_done)
                    sq_slot[tc % 2].require("scalar")
                    se(lambda e, bank=bank, sq_t=sq_t: e.activation(sq_t[:], bank[:, 0:512], AF.Square))
                    p.wait("vector", "sSE")
                    seg_slot[tc % 2].require("vector")
                    ve(lambda e, sq_t=sq_t, seg4=seg4: e.tensor_reduce(
                        seg4[:, 0:4].rearrange("p (a b) -> p a b", b=1),
                        sq_t[:].rearrange("p (a b) -> p a b", a=4), axis=AX.X, op=ALU.add))
                    sq_slot[tc % 2].consumed("sVE")
                    p.wait("scalar", "sVE")
                    se(lambda e, seg4=seg4: e.activation(seg4[:, 4:8], seg4[:, 0:4], AF.Sqrt,
                                                         bias=RMS_EPS, scale=1.0 / 128.0))
                    p.wait("vector", "sSE")
                    ve(lambda e, seg4=seg4: e.reciprocal(seg4[:, 0:4], seg4[:, 4:8]))
                    qn_slot[tc % 2].require("vector")
                    for sg in range(4):
                        ve(lambda e, bank=bank, qn_t=qn_t, sg=sg, seg4=seg4: e.tensor_scalar(
                            qn_t[:, sg * 128:(sg + 1) * 128], bank[:, sg * 128:(sg + 1) * 128],
                            seg4[:, sg:sg + 1], None, ALU.mult))
                    pb_slot[tc % 4].consumed("sSE", "sVE")
                    seg_slot[tc % 2].consumed("sVE")
                    t12_slot[tc % 2].require("vector")
                    ve(lambda e, qn_t=qn_t, t1_t=t1_t, cg=cg: e.tensor_tensor(
                        t1_t[:], qn_t[:], rotc[:, cg * 512:(cg + 1) * 512], ALU.mult))
                    qn_done = p.cnt["sVE"] - 1
                    p.wait("gpsimd", "sVE", qn_done)
                    t12_slot[tc % 2].require("gpsimd")
                    ge(lambda e, qn_t=qn_t, t2_t=t2_t, cg=cg: e.tensor_tensor(
                        t2_t[:].rearrange("p (a b c) -> p a b c", a=4, c=64),
                        qn_t[:].rearrange("p (a b c) -> p a b c", a=4, c=64)[:, :, ::-1, :],
                        rots[:, cg * 512:(cg + 1) * 512].rearrange("p (a b c) -> p a b c", a=4, c=64),
                        ALU.mult))
                    qn_slot[tc % 2].consumed("sVE", "sGE")
                    sb_slot[tc % 2].require("gpsimd")
                    t1_done = p.cnt["sVE"]
                    p.wait("gpsimd", "sVE", t1_done)
                    ge(lambda e, t1_t=t1_t, t2_t=t2_t, sb=sb: e.tensor_tensor(
                        sb[:], t1_t[:], t2_t[:], ALU.add))
                    t12_slot[tc % 2].consumed("sGE")
                    p.wait("gpsimd", "sGE")
                    p.dma("gpsimd", lambda e, sb=sb, tc=tc, cg=cg, dst_scr=dst_scr: e.dma_start(
                        out=dst_scr[tc * 128:(tc + 1) * 128, cg * 512:(cg + 1) * 512],
                        in_=sb[:]), dsem)
                    sb_slot[tc % 2].consumed(total=(dsem,))
            wtg_slot[buf].consumed("sPE")

        ar.reset(m0)
        woT = [ar.alloc([128, D], BF16) for _ in range(NCC)]
        m_wo = ar.mark()
        KQ2 = [[ar.alloc([128, T], BF16) for _ in range(4)] for _ in range(2)]
        Vt2 = [ar.alloc([128, TCA * 257], BF16) for _ in range(2)]
        Pt = [ar.alloc([128, 1024], BF16) for _ in range(2)]
        Ptm = [ar.alloc([128, 1024], BF16) for _ in range(2)]
        oe2 = [ar.alloc([128, 8 * 257], F32) for _ in range(2)]
        eps_t = ar.alloc([128, 280], F32)
        yst = [ar.alloc([128, 256], BF16) for _ in range(2)]

        phase_barrier()
        for vb in range(2):
            for c in range(TCA):
                ve(lambda e, vb=vb, c=c: e.memset(Vt2[vb][:, c * 257 + 256:c * 257 + 257], 1.0))
        p.wait("gpsimd", "sIN")
        p.wait("gpsimd", "sVE")

        psS2 = [psB[0], psB[1]]
        psO = [psT[0], psT[1], psT[2], psT[3]]
        kq_slot = [Slot(), Slot()]
        vt_slot = [Slot(), Slot()]
        kq_sem = [0, 0]
        vt_sem = [0, 0]
        psS_slot = [Slot(), Slot()]
        psS_slot[0].dep = [("sVE", p.cnt["sVE"])]
        pt_slot = [Slot(), Slot()]
        ptm_slot = [Slot(), Slot()]
        psO_slot = [Slot() for _ in range(4)]
        oe_slot = [Slot(), Slot()]
        ys_slot = [Slot(), Slot()]
        ep_jobs = []

        def emit_epilogue(h, qt, buf, sse_evac):
            oe = oe2[buf]
            c0 = h * HD
            p.wait("vector", "sSE", sse_evac)
            ve(lambda e, oe=oe: e.tensor_copy(
                eps_t[:, 0:8], oe[:, 256::257]))
            ve(lambda e: e.reciprocal(eps_t[:, 8:16], eps_t[:, 0:8]))
            ve(lambda e: e.tensor_scalar(eps_t[:, 8:12], eps_t[:, 8:12], 0.8, None, ALU.mult))
            ve(lambda e: e.tensor_scalar(
                eps_t[:, 12:16], eps_t[:, 12:16], neg08lam[:, 0:1], None, ALU.mult))
            for qs in range(4):
                ys = yst[qs % 2]
                ve(lambda e, qs=qs, oe=oe: e.tensor_scalar(
                    eps_t[:, 16:272], oe[:, qs * 257:qs * 257 + 256], eps_t[:, 8 + qs:9 + qs], None, ALU.mult))
                ys_slot[qs % 2].require("vector")
                ve(lambda e, qs=qs, ys=ys, oe=oe: e.scalar_tensor_tensor(
                    ys[:], oe[:, (4 + qs) * 257:(4 + qs) * 257 + 256], eps_t[:, 12 + qs:13 + qs],
                    eps_t[:, 16:272], ALU.mult, ALU.add))
                p.wait("gpsimd", "sVE")
                p.dma("gpsimd", lambda e, ys=ys, qt=qt, qs=qs, c0=c0: e.dma_start(
                    out=yscr[qt * 512 + qs * 128:qt * 512 + (qs + 1) * 128, c0:c0 + 256],
                    in_=ys[:]), "sYS")
                ys_slot[qs % 2].consumed(total=("sYS",))
            oe_slot[buf].consumed("sVE")

        def load_head(h):
            hb = h % 2
            c0 = h * HD
            p.wait("sync", "sQKS")
            kq_slot[hb].require("sync")
            for j, scr_c0 in enumerate([(kscr, c0), (kscr, c0 + 128), (qscr, c0), (qscr, c0 + 128)]):
                scr, cc0 = scr_c0
                tdma(lambda e, j=j, hb=hb, scr=scr, cc0=cc0: e.dma_start_transpose(
                    KQ2[hb][j][:], scr[:, cc0:cc0 + 128]), "sKT")
            kq_sem[hb] = p.cnt["sKT"]
            p.wait("gpsimd", "sVS")
            vt_slot[hb].require("gpsimd")
            vsem = "sVG" if hb == 0 else "sVG2"
            p.dma("gpsimd", lambda e, c0=c0, hb=hb: e.dma_start(
                out=Vt2[hb][:].rearrange("p (c j) -> p c j", j=257)[:, :, 0:256],
                in_=vscr[:, c0:c0 + 256].rearrange("(c p) j -> p c j", p=128)), vsem)
            vt_sem[hb] = (vsem, p.cnt[vsem])

        def emit_Shalf(Kh, Qh, m, half, qt):
            sb = psS2[m % 2]
            if half == 0:
                psS_slot[m % 2].require("tensor")
            pe(lambda e, Kh=Kh, Qh=Qh, m=m, half=half, qt=qt, sb=sb: e.matmul(
                sb[:, half * 512:(half + 1) * 512],
                Kh[:, (2 * m + half) * 128:(2 * m + half + 1) * 128],
                Qh[:, qt * 512:(qt + 1) * 512], start=True, stop=True), inc=True)
            return p.cnt["sPE"]

        load_head(0)
        for h in range(NHL):
            hb = h % 2
            if h + 1 < NHL:
                load_head(h + 1)
            if h == 1:
                p.wait("sync", "sWO", 16)
                for cc in range(NCC):
                    tdma(lambda e, cc=cc: e.dma_start_transpose(
                        woT[cc][:], woscr[:, cc * 128:(cc + 1) * 128]), "sWOT")
            p.wait("gpsimd", "sKT")
            CH13 = H13 // NHL
            p.dma("gpsimd", lambda e, h=h: e.dma_start(
                out=w13scr[h * CH13:(h + 1) * CH13, :], in_=w13[h * CH13:(h + 1) * CH13, :]), "sW13")
            K1, K2, Q1, Q2 = KQ2[hb]
            Vt = Vt2[hb]
            p.wait("tensor", "sKT", kq_sem[hb])
            p.wait("tensor", vt_sem[hb][0], vt_sem[hb][1])
            if h < 2:
                p.wait("tensor", "sVE")
            for qt in range(7, -1, -1):
                if len(ep_jobs) >= 2:
                    emit_epilogue(*ep_jobs.pop(0))
                mmax = (qt * 4 + 4) // 2
                for at in range(2):
                    Kh = K1 if at == 0 else K2
                    Qh = Q1 if at == 0 else Q2
                    s_done = [emit_Shalf(Kh, Qh, 0, 0, qt), emit_Shalf(Kh, Qh, 0, 1, qt)]
                    for m in range(mmax):
                        Pk = Pt[m % 2]
                        sb = psS2[m % 2]
                        exp_done = [0, 0]
                        for half in range(2):
                            p.wait("scalar", "sPE", s_done[half])
                            if half == 0:
                                pt_slot[m % 2].require("scalar")
                            se(lambda e, sb=sb, Pk=Pk, half=half: e.activation(
                                Pk[:, half * 512:(half + 1) * 512],
                                sb[:, half * 512:(half + 1) * 512], AF.Exp, scale=SCALE))
                            exp_done[half] = p.cnt["sSE"]
                        psS_slot[m % 2].consumed("sSE")
                        diag = m >= mmax - 2
                        dv2 = m - (mmax - 2) if diag else 0
                        msk_done = [0, 0]
                        if diag:
                            for half in range(2):
                                p.wait("vector", "sSE", exp_done[half])
                                if half == 0:
                                    ptm_slot[m % 2].require("vector")
                                ve(lambda e, Pk=Pk, dv2=dv2, m=m, half=half: e.tensor_tensor(
                                    Ptm[m % 2][:, half * 512:(half + 1) * 512],
                                    Pk[:, half * 512:(half + 1) * 512],
                                    msk[:, dv2 * 1024 + half * 512:dv2 * 1024 + (half + 1) * 512],
                                    ALU.mult))
                                msk_done[half] = p.cnt["sVE"]
                            lhsP = Ptm[m % 2]
                        else:
                            lhsP = Pk
                        for half in range(2):
                            kc = 2 * m + half
                            if half == 0 and m + 1 < mmax:
                                ns0 = emit_Shalf(Kh, Qh, m + 1, 0, qt)
                            if diag:
                                p.wait("tensor", "sVE", msk_done[half])
                            else:
                                p.wait("tensor", "sSE", exp_done[half])
                            if m == 0 and half == 0:
                                for qs in range(4):
                                    psO_slot[qs].require("tensor")
                            for qs in range(4):
                                pe(lambda e, lhsP=lhsP, qs=qs, kc=kc, half=half, mmax=mmax, Vt=Vt: e.matmul(
                                    psO[qs][:, 0:257],
                                    lhsP[:, half * 512 + qs * 128:half * 512 + (qs + 1) * 128],
                                    Vt[:, kc * 257:(kc + 1) * 257],
                                    start=(kc == 0), stop=(kc == 2 * mmax - 1)),
                                    inc=(qs == 3 and half == 1))
                            if half == 0 and m + 1 < mmax:
                                ns1 = emit_Shalf(Kh, Qh, m + 1, 1, qt)
                        if m + 1 < mmax:
                            s_done = [ns0, ns1]
                        if diag:
                            ptm_slot[m % 2].consumed("sPE")
                            pt_slot[m % 2].consumed("sPE", "sVE")
                        else:
                            pt_slot[m % 2].consumed("sPE")
                    pv_done = p.cnt["sPE"]
                    oe = oe2[qt % 2]
                    base = 0 if at == 0 else 4
                    p.wait("vector", "sPE", pv_done)
                    p.wait("scalar", "sPE", pv_done)
                    oe_slot[qt % 2].require("vector")
                    oe_slot[qt % 2].require("scalar")
                    for qs in range(4):
                        if qs % 2 == 0:
                            ve(lambda e, qs=qs, oe=oe, base=base: e.tensor_copy(
                                oe[:, (base + qs) * 257:(base + qs + 1) * 257], psO[qs][:, 0:257]))
                        else:
                            se(lambda e, qs=qs, oe=oe, base=base: e.copy(
                                oe[:, (base + qs) * 257:(base + qs + 1) * 257], psO[qs][:, 0:257]))
                    for qs in range(4):
                        psO_slot[qs].dep = [("sVE", p.cnt["sVE"]), ("sSE", p.cnt["sSE"])]
                ep_jobs.append((h, qt, qt % 2, p.cnt["sSE"]))
            kq_slot[hb].consumed("sPE")
            vt_slot[hb].consumed("sPE")
        while ep_jobs:
            emit_epilogue(*ep_jobs.pop(0))

        ar.reset(m_wo)
        yT = [ar.alloc([128, T], BF16) for _ in range(NCC)]
        apst = [ar.alloc([128, D], F32) for _ in range(2)]

        phase_barrier()
        p.dma("gpsimd", lambda e: e.dma_start(out=w2scr[:, 0:HF // 2], in_=w2[:, 0:HF // 2]), "sW2")
        p.dma("gpsimd", lambda e: e.dma_start(out=w2scr[:, HF // 2:HF], in_=w2[:, HF // 2:HF]), "sW2")
        p.wait("sync", "sYS")
        p.wait("sync", "sWO", 16)
        for cc in range(NCC):
            tdma(lambda e, cc=cc: e.dma_start_transpose(
                yT[cc][:], yscr[:, cc * 128:(cc + 1) * 128]), "sWOT")
        p.wait("tensor", "sWOT")
        wo_bank = [Slot(), Slot()]
        ap_slot = [Slot(), Slot()]
        for tc in range(TCA):
            banks = ps[0:4] if tc % 2 == 0 else ps[4:8]
            a_t = apst[tc % 2]
            wo_bank[tc % 2].require("tensor")
            for cc in range(NCC):
                for cs in range(4):
                    pe(lambda e, cc=cc, cs=cs, tc=tc, banks=banks: e.matmul(
                        banks[cs][:, 0:512], yT[cc][:, tc * 128:(tc + 1) * 128],
                        woT[cc][:, cs * 512:(cs + 1) * 512],
                        start=(cc == 0), stop=(cc == NCC - 1)),
                        inc=(cc == NCC - 1 and cs == 3))
            mm_done = p.cnt["sPE"]
            p.wait("vector", "sPE", mm_done)
            p.wait("scalar", "sPE", mm_done)
            ap_slot[tc % 2].require("vector")
            ap_slot[tc % 2].require("scalar")
            for cs in range(4):
                if cs % 2 == 0:
                    ve(lambda e, cs=cs, a_t=a_t, banks=banks: e.tensor_copy(
                        a_t[:, cs * 512:(cs + 1) * 512], banks[cs][:, 0:512]))
                else:
                    se(lambda e, cs=cs, a_t=a_t, banks=banks: e.copy(
                        a_t[:, cs * 512:(cs + 1) * 512], banks[cs][:, 0:512]))
            wo_bank[tc % 2].consumed("sVE", "sSE")
            p.wait("sync", "sVE")
            p.wait("sync", "sSE")
            p.dma("sync", lambda e, a_t=a_t, tc=tc: e.dma_start(
                out=apart[tc * 128:(tc + 1) * 128, :], in_=a_t[:]), "sAPS")
            ap_slot[tc % 2].consumed(total=("sAPS",))
            if tc % 8 == 7:
                ch = tc // 8
                p.wait("gpsimd", "sAPS")
                p.op("gpsimd", lambda e, ch=ch: e.collective_compute(
                    "ReduceScatter", ALU.add,
                    replica_groups=[[0, 1], [2, 3], [4, 5], [6, 7]],
                    ins=[apart[ch * 1024:(ch + 1) * 1024, :].opt()],
                    outs=[ascat[ch * 512:(ch + 1) * 512, :].opt()]), "sCC", 1)

        _sCC_save = p.cnt["sCC"]
        p.cnt["sCC"] = 0
        ar.reset(m_pre)
        NPR = 4
        r0t = [ar.alloc([128, D], F32) for _ in range(NPR)]
        xrt = [ar.alloc([128, D], F32) for _ in range(NPR)]
        xpt = [ar.alloc([128, D], F32) for _ in range(NPR)]
        tqt2 = [ar.alloc([128, D], F32) for _ in range(NPR)]
        zqt2 = [ar.alloc([128, D], I32) for _ in range(NPR)]
        xqt = [ar.alloc([128, D], BF16) for _ in range(NPR)]

        phase_barrier()
        p.cnt["sCC"] = _sCC_save
        ST = stat
        r_slot = [Slot() for _ in range(4)]
        xp_slot = [Slot() for _ in range(4)]
        xq_slot = [Slot() for _ in range(4)]
        for tc in range(TCM):
            pr = tc % 4
            SB0 = 12 * pr
            r_t, x_t, xp_t, xq_t = r0t[pr], xrt[pr], xpt[pr], xqt[pr]
            tq_t, zq_t = tqt2[pr], zqt2[pr]
            r_slot[pr].require("sync")
            p.wait("sync", "sCC", tc // 4 + 1)
            lsem = ["sLD", "sLD2", "sLD3", "sLD4"][pr]
            p.dma("sync", lambda e, r_t=r_t, tc=tc: e.dma_start(
                out=r_t[:], in_=ascat[tc * 128:(tc + 1) * 128, :]), lsem)
            p.dma("sync", lambda e, x_t=x_t, tc=tc: e.dma_start(
                out=x_t[:], in_=xmy[tc * 128:(tc + 1) * 128, :]), lsem)
            ld_done = p.cnt[lsem]
            p.wait("vector", lsem, ld_done)
            ve(lambda e, r_t=r_t, x_t=x_t: e.tensor_tensor(r_t[:], r_t[:], x_t[:], ALU.add))
            radd_done = p.cnt["sVE"]
            p.wait("scalar", "sVE", radd_done)
            se(lambda e, r_t=r_t, SB0=SB0: e.activation(trash_bf[:], r_t[:], AF.Copy, accum_out=ST[:, SB0:SB0 + 1]))
            se(lambda e, r_t=r_t, SB0=SB0: e.activation(trash_bf[:], r_t[:], AF.Square, accum_out=ST[:, SB0 + 1:SB0 + 2]))
            acc_done = p.cnt["sSE"]
            p.wait("vector", "sSE", acc_done)
            ve(lambda e, SB0=SB0: e.tensor_scalar(ST[:, SB0 + 2:SB0 + 3], ST[:, SB0:SB0 + 1], 1.0 / D, None, ALU.mult))
            ve(lambda e, SB0=SB0: e.tensor_scalar(ST[:, SB0 + 3:SB0 + 4], ST[:, SB0 + 1:SB0 + 2], 1.0 / D, None, ALU.mult))
            ve(lambda e, SB0=SB0: e.tensor_tensor(ST[:, SB0 + 4:SB0 + 5], ST[:, SB0 + 2:SB0 + 3], ST[:, SB0 + 2:SB0 + 3], ALU.mult))
            ve(lambda e, SB0=SB0: e.tensor_tensor(ST[:, SB0 + 5:SB0 + 6], ST[:, SB0 + 3:SB0 + 4], ST[:, SB0 + 4:SB0 + 5], ALU.subtract))
            var_done = p.cnt["sVE"]
            p.wait("scalar", "sVE", var_done)
            se(lambda e, SB0=SB0: e.activation(ST[:, SB0 + 6:SB0 + 7], ST[:, SB0 + 5:SB0 + 6], AF.Sqrt, bias=LN_EPS))
            p.wait("vector", "sSE")
            ve(lambda e, SB0=SB0: e.reciprocal(ST[:, SB0 + 7:SB0 + 8], ST[:, SB0 + 6:SB0 + 7]))
            xp_slot[pr].require("vector")
            ve(lambda e, r_t=r_t, xp_t=xp_t, SB0=SB0: e.tensor_scalar(
                xp_t[:], r_t[:], ST[:, SB0 + 2:SB0 + 3], ST[:, SB0 + 7:SB0 + 8], ALU.subtract, ALU.mult))
            r_slot[pr].consumed("sVE")
            xpw_done = p.cnt["sVE"]
            p.wait("gpsimd", "sVE", xpw_done)
            p.dma("gpsimd", lambda e, xp_t=xp_t, tc=tc: e.dma_start(
                out=xpscr[tc * 128:(tc + 1) * 128, :], in_=xp_t[:]), "sXPS")
            ve(lambda e, xp_t=xp_t, SB0=SB0: e.tensor_reduce(
                ST[:, SB0 + 8:SB0 + 9], xp_t[:], axis=AX.X, op=ALU.max, apply_absolute_value=True))
            ve(lambda e, SB0=SB0: e.tensor_scalar(ST[:, SB0 + 9:SB0 + 10], ST[:, SB0 + 8:SB0 + 9], 1e-5, 1.0 / 127.0, ALU.max, ALU.mult))
            ve(lambda e, SB0=SB0: e.reciprocal(ST[:, SB0 + 10:SB0 + 11], ST[:, SB0 + 9:SB0 + 10]))
            ve(lambda e, xp_t=xp_t, tq_t=tq_t, SB0=SB0: e.tensor_scalar(
                tq_t[:], xp_t[:], ST[:, SB0 + 10:SB0 + 11], None, ALU.mult))
            ve(lambda e, tq_t=tq_t, zq_t=zq_t: e.tensor_copy(zq_t[:], tq_t[:]))
            xp_slot[pr].consumed("sVE", total=("sXPS",))
            xq_slot[pr].require("vector")
            ve(lambda e, xq_t=xq_t, zq_t=zq_t, SB0=SB0: e.tensor_scalar(
                xq_t[:], zq_t[:], ST[:, SB0 + 9:SB0 + 10], None, ALU.mult))
            xqw_done = p.cnt["sVE"]
            p.wait("gpsimd", "sVE", xqw_done)
            p.dma("gpsimd", lambda e, xq_t=xq_t, tc=tc: e.dma_start(
                out=xqscr[tc * 128:(tc + 1) * 128, :], in_=xq_t[:]), "sXQS")
            xq_slot[pr].consumed(total=("sXQS",))

        ar.reset(m_pre)
        xqT = [ar.alloc([128, MYT], BF16) for _ in range(NDC)]
        w13g2 = [[ar.alloc([128, 512], BF16) for _ in range(2 * NDC)] for _ in range(2)]
        g42 = [[ar.alloc([128, 512], F32) for _ in range(3)] for _ in range(2)]
        tmp_bf = [ar.alloc([128, 512], BF16) for _ in range(2)]
        nwrow = ar.alloc([1, HF], F32)
        nwb = ar.alloc([128, HF], BF16)

        phase_barrier()
        p.wait("sync", "sXQS")
        for dc in range(NDC):
            tdma(lambda e, dc=dc: e.dma_start_transpose(
                xqT[dc][:], xqscr[:, dc * 128:(dc + 1) * 128]), "sXQT")
        p.dma("sync", lambda e: e.dma_start(out=nwrow[0:1, :], in_=nw_in), "sLD")
        p.wait("tensor", "sLD")
        for j in range(NPG):
            p.wait("tensor", "sVE")
            pe(lambda e, j=j: e.matmul(ps[6][:, 0:512], onesf32[0:1, :],
                                       nwrow[0:1, j * 512:(j + 1) * 512], start=True, stop=True), inc=True)
            p.wait("vector", "sPE")
            ve(lambda e, j=j: e.tensor_copy(nwb[:, j * 512:(j + 1) * 512], ps[6][:, 0:512]))

        FS = fstat
        nwb_done_sve = None
        w13_slot = [Slot(), Slot()]
        w13_sem = [0, 0]
        pA_slot = [Slot() for _ in range(4)]
        pB_slot = [Slot() for _ in range(4)]
        pB_slot[2].dep = [("sVE", p.cnt["sVE"])]
        gs_slot = [Slot(), Slot()]
        tb_slot = [Slot(), Slot()]

        def load_pg(pg):
            buf = pg % 2
            p.wait("sync", "sW13")
            w13_slot[buf].require("sync")
            for dc in range(NDC):
                tdma(lambda e, dc=dc, pg=pg, buf=buf: e.dma_start_transpose(
                    w13g2[buf][dc][:], w13scr[pg * 512:(pg + 1) * 512, dc * 128:(dc + 1) * 128]), "sW13T")
                tdma(lambda e, dc=dc, pg=pg, buf=buf: e.dma_start_transpose(
                    w13g2[buf][NDC + dc][:], w13scr[HF + pg * 512:HF + (pg + 1) * 512, dc * 128:(dc + 1) * 128]), "sW13T")
            w13_sem[buf] = p.cnt["sW13T"]

        load_pg(0)
        for pg in range(NPG):
            buf = pg % 2
            if pg + 1 < NPG:
                load_pg(pg + 1)
            w1g = w13g2[buf][0:NDC]
            w3g = w13g2[buf][NDC:2 * NDC]
            p.wait("tensor", "sW13T", w13_sem[buf])
            if pg == 0:
                p.wait("tensor", "sXQT")
            for tc in range(TCM):
                bA = ps[(tc % 4)]
                bB = ps[4 + (tc % 4)]
                pA_slot[tc % 4].require("tensor")
                for dc in range(NDC):
                    pe(lambda e, dc=dc, tc=tc, bA=bA, w1g=w1g: e.matmul(
                        bA[:, 0:512], xqT[dc][:, tc * 128:(tc + 1) * 128], w1g[dc][:],
                        start=(dc == 0), stop=(dc == NDC - 1)), inc=(dc == NDC - 1))
                pB_slot[tc % 4].require("tensor")
                for dc in range(NDC):
                    pe(lambda e, dc=dc, tc=tc, bB=bB, w3g=w3g: e.matmul(
                        bB[:, 0:512], xqT[dc][:, tc * 128:(tc + 1) * 128], w3g[dc][:],
                        start=(dc == 0), stop=(dc == NDC - 1)), inc=(dc == NDC - 1))
                mm_done = p.cnt["sPE"]
                r1_t, rr_t, g_t = g42[tc % 2]
                tb = tmp_bf[tc % 2]
                p.wait("scalar", "sPE", mm_done)
                gs_slot[tc % 2].require("scalar")
                se(lambda e, bA=bA, r1_t=r1_t: e.activation(r1_t[:], bA[:, 0:512], AF.Relu))
                pA_slot[tc % 4].consumed("sSE")
                se(lambda e, r1_t=r1_t, rr_t=rr_t: e.activation(rr_t[:], r1_t[:], AF.Square))
                sq_done = p.cnt["sSE"]
                p.wait("vector", "sSE", sq_done)
                ve(lambda e, rr_t=rr_t, g_t=g_t, bB=bB: e.tensor_tensor(
                    g_t[:], rr_t[:], bB[:, 0:512], ALU.mult))
                pB_slot[tc % 4].consumed("sVE")
                g_done = p.cnt["sVE"]
                p.wait("scalar", "sVE", g_done)
                se(lambda e, g_t=g_t, pg=pg, tc=tc: e.activation(
                    trash_bf[:, 0:512], g_t[:], AF.Square,
                    accum_out=FS[:, (pg * TCM + tc):(pg * TCM + tc) + 1]))
                p.wait("gpsimd", "sVE", g_done)
                tb_slot[tc % 2].require("gpsimd")
                ge(lambda e, g_t=g_t, tb=tb, pg=pg: e.tensor_tensor(
                    tb[:], g_t[:], nwb[:, pg * 512:(pg + 1) * 512], ALU.mult))
                gs_slot[tc % 2].consumed("sSE", "sGE")
                tt_done = p.cnt["sGE"]
                p.wait("vector", "sGE", tt_done)
                ve(lambda e, tb=tb, pg=pg, tc=tc: e.tensor_reduce(
                    FS[:, (NPG * TCM + pg * TCM + tc):(NPG * TCM + pg * TCM + tc) + 1],
                    tb[:], axis=AX.X, op=ALU.max, apply_absolute_value=True))
                p.wait("gpsimd", "sGE", tt_done)
                p.dma("gpsimd", lambda e, tb=tb, tc=tc, pg=pg: e.dma_start(
                    out=tmpscr[tc * 128:(tc + 1) * 128, pg * 512:(pg + 1) * 512],
                    in_=tb[:]), "sTMS")
                tb_slot[tc % 2].consumed("sVE", total=("sTMS",))
            w13_slot[buf].consumed("sPE")

        off5 = 2 * NPG * TCM
        p.wait("vector", "sSE")
        p.wait("vector", "sGE")
        for tc in range(TCM):
            ve(lambda e, tc=tc: e.tensor_reduce(
                FS[:, off5 + tc:off5 + tc + 1],
                FS[:, tc:NPG * TCM + tc:TCM], axis=AX.X, op=ALU.add))
            ve(lambda e, tc=tc: e.tensor_reduce(
                FS[:, off5 + TCM + tc:off5 + TCM + tc + 1],
                FS[:, NPG * TCM + tc:2 * NPG * TCM + tc:TCM], axis=AX.X, op=ALU.max))
        p.wait("scalar", "sVE")
        se(lambda e: e.activation(ST[:, 16:32], FS[:, off5:off5 + TCM], AF.Sqrt,
                                  bias=LN_EPS, scale=1.0 / HF))
        p.wait("vector", "sSE")
        ve(lambda e: e.reciprocal(ST[:, 16:32], ST[:, 16:32]))
        ve(lambda e: e.tensor_tensor(ST[:, 32:48], FS[:, off5 + TCM:off5 + 2 * TCM], ST[:, 16:32], ALU.mult))
        ve(lambda e: e.tensor_scalar(ST[:, 32:48], ST[:, 32:48], 1e-5, 1.0 / 127.0, ALU.max, ALU.mult))
        ve(lambda e: e.reciprocal(ST[:, 48:64], ST[:, 32:48]))
        ve(lambda e: e.tensor_tensor(ST[:, 16:32], ST[:, 16:32], ST[:, 48:64], ALU.mult))
        rsv_done = p.cnt["sVE"]
        p.wait("gpsimd", "sVE", rsv_done)
        for tc in range(TCM):
            p.dma("gpsimd", lambda e, tc=tc: e.dma_start(
                out=rsscr[0:1, tc * 128:(tc + 1) * 128].rearrange("o (j i) -> (o j) i", i=1),
                in_=ST[:, 16 + tc:17 + tc]), "sRS")
            p.dma("gpsimd", lambda e, tc=tc: e.dma_start(
                out=rsscr[1:2, tc * 128:(tc + 1) * 128].rearrange("o (j i) -> (o j) i", i=1),
                in_=ST[:, 32 + tc:33 + tc]), "sRS")

        ar.reset(m_pre)
        rs_row = ar.alloc([1, 2 * MYT], F32)
        rs_b = ar.alloc([128, MYT], BF16)
        inv2_b = ar.alloc([128, MYT], F32)
        t1_t52 = [ar.alloc([128, MYT], F32) for _ in range(2)]
        z52 = [ar.alloc([128, MYT], I32) for _ in range(2)]
        tmpf = [ar.alloc([128, MYT], BF16) for _ in range(2)]
        iq5 = [ar.alloc([128, MYT], BF16) for _ in range(2)]

        phase_barrier()
        p.wait("sync", "sRS")
        p.dma("sync", lambda e: e.dma_start(out=rs_row[0:1, :], in_=rsscr.rearrange("(o a) j -> o (a j)", o=1)), "sLD")
        p.wait("tensor", "sLD")
        p.wait("tensor", "sSE")
        for j in range(MYT // 512):
            p.wait("tensor", "sVE")
            pe(lambda e, j=j: e.matmul(ps[0][:, 0:512], onesf32[0:1, :],
                                       rs_row[0:1, j * 512:(j + 1) * 512], start=True, stop=True), inc=True)
            pe(lambda e, j=j: e.matmul(ps[1][:, 0:512], onesf32[0:1, :],
                                       rs_row[0:1, MYT + j * 512:MYT + (j + 1) * 512], start=True, stop=True), inc=True)
            p.wait("vector", "sPE")
            ve(lambda e, j=j: e.tensor_copy(rs_b[:, j * 512:(j + 1) * 512], ps[0][:, 0:512]))
            ve(lambda e, j=j: e.tensor_copy(inv2_b[:, j * 512:(j + 1) * 512], ps[1][:, 0:512]))
        p.wait("tensor", "sVE")

        p.wait("sync", "sTMS")
        tf_slot = [Slot(), Slot()]
        tz_slot = [Slot(), Slot()]
        iq_slot = [Slot(), Slot()]
        for hc in range(NHC):
            pr = hc % 2
            tf = tmpf[pr]
            iqt = iq5[pr]
            t1_t5, z5 = t1_t52[pr], z52[pr]
            tf_slot[pr].require("sync")
            tdma(lambda e, tf=tf, hc=hc: e.dma_start_transpose(
                tf[:], tmpscr[:, hc * 128:(hc + 1) * 128]), "sTMT")
            tmt_done = p.cnt["sTMT"]
            p.wait("vector", "sTMT", tmt_done)
            tz_slot[pr].require("vector")
            if pr == 0:
                ve(lambda e, tf=tf, t1_t5=t1_t5: e.tensor_tensor(t1_t5[:], tf[:], rs_b[:], ALU.mult))
                tf_slot[pr].consumed("sVE")
                p.wait("vector", "sVE")
            else:
                p.wait("gpsimd", "sTMT", tmt_done)
                tz_slot[pr].require("gpsimd")
                ge(lambda e, tf=tf, t1_t5=t1_t5: e.tensor_tensor(t1_t5[:], tf[:], rs_b[:], ALU.mult))
                tf_slot[pr].consumed("sGE")
                p.wait("vector", "sGE")
            ve(lambda e, t1_t5=t1_t5, z5=z5: e.tensor_copy(z5[:], t1_t5[:]))
            iq_slot[pr].require("vector")
            ve(lambda e, iqt=iqt, z5=z5: e.tensor_tensor(iqt[:], z5[:], inv2_b[:], ALU.mult))
            tz_slot[pr].consumed("sVE", "sGE")
            iqw = ("sVE", p.cnt["sVE"])
            p.wait("gpsimd", iqw[0], iqw[1])
            p.dma("gpsimd", lambda e, iqt=iqt, hc=hc: e.dma_start(
                out=iqscr[hc], in_=iqt[:]), "sIQS")
            iq_slot[pr].consumed(total=("sIQS",))

        ar.reset(m_pre)
        NG5 = 4
        GSZ = NHC // NG5
        w2T52 = [[ar.alloc([128, D], BF16) for _ in range(GSZ)] for _ in range(2)]
        iqg = [ar.alloc([128, 128 * (TCM // 2)], BF16) for _ in range(GSZ)]
        acc = [ar.alloc([128, D], F32) for _ in range(TCM // 2)]
        ot = [ar.alloc([128, D], F32) for _ in range(2)]

        w2t_slot = [Slot(), Slot()]
        w2t_sem = [0, 0]
        iqg_slot = Slot()
        acc_slot = [Slot() for _ in range(TCM // 2)]
        b5_slot = [Slot(), Slot()]
        ot_slot = [Slot(), Slot()]

        def load_w2t(gg):
            buf = gg % 2
            p.wait("sync", "sW2")
            w2t_slot[buf].require("sync")
            for i in range(GSZ):
                hc = (gg % NG5) * GSZ + i
                tdma(lambda e, i=i, hc=hc, buf=buf: e.dma_start_transpose(
                    w2T52[buf][i][:], w2scr[:, hc * 128:(hc + 1) * 128]), "sW2T")
            w2t_sem[buf] = p.cnt["sW2T"]

        def emit_ln2(th, i):
            tc = th * (TCM // 2) + i
            pr = i % 2
            SB0 = 12 * pr
            o_t = ot[pr]
            p.wait("scalar", "sVE")
            se(lambda e, i=i, SB0=SB0: e.activation(trash_bf[:], acc[i][:], AF.Copy, accum_out=ST[:, SB0:SB0 + 1]))
            se(lambda e, i=i, SB0=SB0: e.activation(trash_bf[:], acc[i][:], AF.Square, accum_out=ST[:, SB0 + 1:SB0 + 2]))
            acc_done2 = p.cnt["sSE"]
            p.wait("vector", "sSE", acc_done2)
            ve(lambda e, SB0=SB0: e.tensor_scalar(ST[:, SB0 + 2:SB0 + 3], ST[:, SB0:SB0 + 1], 1.0 / D, None, ALU.mult))
            ve(lambda e, SB0=SB0: e.tensor_scalar(ST[:, SB0 + 3:SB0 + 4], ST[:, SB0 + 1:SB0 + 2], 1.0 / D, None, ALU.mult))
            ve(lambda e, SB0=SB0: e.tensor_tensor(ST[:, SB0 + 4:SB0 + 5], ST[:, SB0 + 2:SB0 + 3], ST[:, SB0 + 2:SB0 + 3], ALU.mult))
            ve(lambda e, SB0=SB0: e.tensor_tensor(ST[:, SB0 + 5:SB0 + 6], ST[:, SB0 + 3:SB0 + 4], ST[:, SB0 + 4:SB0 + 5], ALU.subtract))
            var_done2 = p.cnt["sVE"]
            p.wait("scalar", "sVE", var_done2)
            se(lambda e, SB0=SB0: e.activation(ST[:, SB0 + 6:SB0 + 7], ST[:, SB0 + 5:SB0 + 6], AF.Sqrt, bias=LN_EPS))
            p.wait("vector", "sSE")
            ve(lambda e, SB0=SB0: e.reciprocal(ST[:, SB0 + 7:SB0 + 8], ST[:, SB0 + 6:SB0 + 7]))
            ot_slot[pr].require("vector")
            ve(lambda e, i=i, o_t=o_t, SB0=SB0: e.tensor_scalar(
                o_t[:], acc[i][:], ST[:, SB0 + 2:SB0 + 3], ST[:, SB0 + 7:SB0 + 8], ALU.subtract, ALU.mult))
            acc_slot[i].consumed("sVE")
            ow_done = p.cnt["sVE"]
            p.wait("sync", "sVE", ow_done)
            p.dma("sync", lambda e, o_t=o_t, tc=tc: e.dma_start(
                out=out_c[tc * 128:(tc + 1) * 128, :], in_=o_t[:]), "sOUT")
            ot_slot[pr].consumed(total=("sOUT",))

        first = True
        for th in range(2):
            if first:
                phase_barrier(["sync", "scalar"])
            p.wait("sync", "sXPS")
            p.wait("sync", "sVE")
            p.wait("sync", "sSE")
            for i in range(TCM // 2):
                tc = th * (TCM // 2) + i
                acc_slot[i].require("sync")
                p.dma("sync", lambda e, i=i, tc=tc: e.dma_start(
                    out=acc[i][:], in_=xpscr[tc * 128:(tc + 1) * 128, :]), "sACB")
            acb_done = p.cnt["sACB"]
            if first:
                load_w2t(0)
                first = False
            for g in range(NG5):
                gg = th * NG5 + g
                buf = gg % 2
                if gg + 1 < 2 * NG5:
                    load_w2t(gg + 1)
                p.wait("scalar", "sIQS")
                iqg_slot.require("scalar")
                for i in range(GSZ):
                    hc = g * GSZ + i
                    p.dma("scalar", lambda e, i=i, hc=hc, th=th: e.dma_start(
                        out=iqg[i][:], in_=iqscr[hc][:, th * (MYT // 2):(th + 1) * (MYT // 2)]), "sIQL")
                iql_done = p.cnt["sIQL"]
                p.wait("tensor", "sW2T", w2t_sem[buf])
                p.wait("tensor", "sIQL", iql_done)
                p.wait("tensor", "sACB", acb_done)
                for i in range(TCM // 2):
                    banks = ps[0:4] if i % 2 == 0 else ps[4:8]
                    b5_slot[i % 2].require("tensor")
                    for j in range(GSZ):
                        for cs in range(4):
                            pe(lambda e, i=i, j=j, cs=cs, banks=banks, buf=buf: e.matmul(
                                banks[cs][:, 0:512], iqg[j][:, i * 128:(i + 1) * 128],
                                w2T52[buf][j][:, cs * 512:(cs + 1) * 512],
                                start=(j == 0), stop=(j == GSZ - 1)),
                                inc=(j == GSZ - 1 and cs == 3))
                    mm_done = p.cnt["sPE"]
                    p.wait("vector", "sPE", mm_done)
                    p.wait("vector", "sACB", acb_done)
                    acc_slot[i].require("vector")
                    for cs in range(4):
                        ve(lambda e, i=i, cs=cs, banks=banks: e.tensor_tensor(
                            acc[i][:, cs * 512:(cs + 1) * 512], acc[i][:, cs * 512:(cs + 1) * 512],
                            banks[cs][:, 0:512], ALU.add))
                    b5_slot[i % 2].consumed("sVE")
                    if g == NG5 - 1:
                        emit_ln2(th, i)
                iqg_slot.consumed("sPE")
                w2t_slot[buf].consumed("sPE")
        if DEBUG:
            p.barrier(["gpsimd"], ["sQKS", "sVS", "sYS", "sAPS", "sCC", "sXPS",
                                   "sXQS", "sTMS", "sOUT"])
            for name, (dst, src) in dbg.items():
                p.dma("gpsimd", lambda e, dst=dst, src=src: e.dma_start(out=dst, in_=src), "sDBG")
            p.wait("gpsimd", "sDBG")

        for eng in ALL_ENG:
            p.wait(eng, "sOUT")

        with nc.Block() as block:
            def mk(engname):
                def run(e):
                    for it in p.items:
                        if it[0] != engname:
                            continue
                        if it[1] == "wait":
                            e.wait_ge(sems[it[2]], it[3])
                        else:
                            instr = it[2](e)
                            if it[3] is not None:
                                instr.then_inc(sems[it[3]], it[4])
                return run
            block.sync(mk("sync"))
            block.scalar(mk("scalar"))
            block.vector(mk("vector"))
            block.tensor(mk("tensor"))
            block.gpsimd(mk("gpsimd"))

    return nc


def _host_consts(hg):
    inv_freq = ((np.float32(1.0) / np.float32(ROPE_BASE)) **
                (np.arange(0, HH, 2, dtype=np.float32) / np.float32(HH))).astype(np.float64)
    rotc = np.zeros((128, CQ), np.float32)
    rots = np.zeros((128, CQ), np.float32)
    for hl in range(NHL):
        hgl = hg * NHL + hl
        c = np.cos(np.float64(hgl) * inv_freq).astype(np.float32)
        s = np.sin(np.float64(hgl) * inv_freq).astype(np.float32)
        for half in range(2):
            base = hl * HD + half * HH
            rotc[:, base:base + 64] = c
            rotc[:, base + 64:base + 128] = c
            rots[:, base:base + 64] = s
            rots[:, base + 64:base + 128] = -s
    ki = np.arange(128)[:, None]
    qj = np.arange(512)[None, :]
    m4 = np.zeros((4, 128, 512), np.float32)
    for d in range(4):
        m4[d] = (qj >= ki + d * 128).astype(np.float32)
    mskv = np.zeros((2, 128, 1024), np.float32)
    mskv[0, :, 0:512] = m4[0]
    mskv[0, :, 512:1024] = m4[1]
    mskv[1, :, 0:512] = m4[2]
    mskv[1, :, 512:1024] = m4[3]
    return rotc, rots, mskv.astype(ml_dtypes.bfloat16)




def my_token_ids(hg):
    return np.concatenate([np.arange(c * 1024 + hg * 512, c * 1024 + (hg + 1) * 512)
                           for c in range(T // 1024)])


def make_in_maps(x, wqf, wkf, wvf, wof, w13f, w2f, nwf, lqkf):
    in_maps = []
    for c in range(8):
        b, hg = c // 2, c % 2
        rotcv, rotsv, mskv = _host_consts(hg)
        in_maps.append(dict(
            xb=x[b],
            xmy=np.ascontiguousarray(x[b, my_token_ids(hg)]),
            wq=np.ascontiguousarray(wqf[hg * CQ:(hg + 1) * CQ]),
            wk=np.ascontiguousarray(wkf[hg * CQ:(hg + 1) * CQ]),
            wv=np.ascontiguousarray(wvf[hg * CQ:(hg + 1) * CQ]),
            wo=np.ascontiguousarray(wof[:, hg * CQ:(hg + 1) * CQ]),
            w13=w13f, w2=w2f, nw_in=nwf, lqk=lqkf,
            rotc_in=rotcv, rots_in=rotsv, msk_in=mskv,
        ))
    return in_maps

def kernel(L0_attn_x, L0_attn_wq, L0_attn_wk, L0_attn_wv, L0_attn_wo,
           L0_attn_lq1, L0_attn_lk1, L0_attn_lq2, L0_attn_lk2,
           L0_ffn_w13, L0_ffn_w2, L0_ffn_norm_w):
    x = np.ascontiguousarray(np.asarray(L0_attn_x, dtype=np.float32))
    wqf = np.ascontiguousarray(np.asarray(L0_attn_wq, dtype=np.float32))
    wkf = np.ascontiguousarray(np.asarray(L0_attn_wk, dtype=np.float32))
    wvf = np.ascontiguousarray(np.asarray(L0_attn_wv, dtype=np.float32))
    wof = np.ascontiguousarray(np.asarray(L0_attn_wo, dtype=np.float32))
    w13f = np.ascontiguousarray(np.asarray(L0_ffn_w13, dtype=np.float32))
    w2f = np.ascontiguousarray(np.asarray(L0_ffn_w2, dtype=np.float32))
    nwf = np.ascontiguousarray(np.asarray(L0_ffn_norm_w, dtype=np.float32)).reshape(1, HF)
    lqkf = np.ascontiguousarray(np.stack([np.asarray(v, dtype=np.float32) for v in
                                          (L0_attn_lq1, L0_attn_lk1, L0_attn_lq2, L0_attn_lk2)]))

    nc = build_nc()
    in_maps = make_in_maps(x, wqf, wkf, wvf, wof, w13f, w2f, nwf, lqkf)
    res = run_bass_kernel_spmd(nc, in_maps, core_ids=list(range(8)))
    out = np.empty((B, T, D), dtype=np.float32)
    for c in range(8):
        b, hg = c // 2, c % 2
        out[b, my_token_ids(hg)] = res.results[c]["out_c"]
    return out


# version 14
# speedup vs baseline: 1.0744x; 1.0127x over previous
import math
from collections import Counter
from contextlib import ExitStack

import numpy as np
import ml_dtypes

import concourse.bass as bass
import concourse.mybir as mybir
from concourse.bass_utils import run_bass_kernel_spmd

F32 = mybir.dt.float32
BF16 = mybir.dt.bfloat16
I32 = mybir.dt.int32
AF = mybir.ActivationFunctionType
ALU = mybir.AluOpType
AX = mybir.AxisListType

B, T, D = 4, 4096, 2048
NHL = 4
HD, HH = 256, 128
CQ = NHL * HD
HF = 5632
H13 = 2 * HF
MYT = 2048
TCA = T // 128
TCM = MYT // 128
NDC = D // 128
NCC = CQ // 128
NPG = HF // 512
NHC = HF // 128
LAMBDA_INIT = 0.2
RMS_EPS = 1.1920929e-07
LN_EPS = 1e-05
ROPE_BASE = 10000.0
SCALE = 1.0 / math.sqrt(128.0)

DEBUG = False


class Prog:
    def __init__(self):
        self.items = []
        self.cnt = Counter()

    def op(self, eng, fn, inc=None, n=1):
        if inc is not None:
            self.cnt[inc] += n
        self.items.append((eng, "op", fn, inc, n))

    def dma(self, eng, fn, inc):
        self.op(eng, fn, inc, 16)

    def wait(self, eng, sem, val=None):
        v = self.cnt[sem] if val is None else val
        if v > 0:
            self.items.append((eng, "wait", sem, v))

    def barrier(self, engines, sems):
        for e in engines:
            for s in sems:
                self.wait(e, s)


def build_nc():
    nc = bass.Bass("TRN2", target_bir_lowering=False)

    xb = nc.dram_tensor("xb", [T, D], F32, kind="ExternalInput").ap()
    xmy = nc.dram_tensor("xmy", [MYT, D], F32, kind="ExternalInput").ap()
    wq = nc.dram_tensor("wq", [CQ, D], F32, kind="ExternalInput").ap()
    wk = nc.dram_tensor("wk", [CQ, D], F32, kind="ExternalInput").ap()
    wv = nc.dram_tensor("wv", [CQ, D], F32, kind="ExternalInput").ap()
    wo = nc.dram_tensor("wo", [D, CQ], F32, kind="ExternalInput").ap()
    w13 = nc.dram_tensor("w13", [H13, D], F32, kind="ExternalInput").ap()
    w2 = nc.dram_tensor("w2", [D, HF], F32, kind="ExternalInput").ap()
    nw_in = nc.dram_tensor("nw_in", [1, HF], F32, kind="ExternalInput").ap()
    lqk = nc.dram_tensor("lqk", [4, HD], F32, kind="ExternalInput").ap()
    rotc_in = nc.dram_tensor("rotc_in", [128, CQ], F32, kind="ExternalInput").ap()
    rots_in = nc.dram_tensor("rots_in", [128, CQ], F32, kind="ExternalInput").ap()
    msk_in = nc.dram_tensor("msk_in", [2, 128, 1024], BF16, kind="ExternalInput").ap()

    out_c = nc.dram_tensor("out_c", [MYT, D], F32, kind="ExternalOutput").ap()

    xscr = nc.dram_tensor("xscr", [T, D], BF16, kind="Internal").ap()
    wqscr = nc.dram_tensor("wqscr", [CQ, D], BF16, kind="Internal").ap()
    wkscr = nc.dram_tensor("wkscr", [CQ, D], BF16, kind="Internal").ap()
    wvscr = nc.dram_tensor("wvscr", [CQ, D], BF16, kind="Internal").ap()
    woscr = nc.dram_tensor("woscr", [D, CQ], BF16, kind="Internal").ap()
    w13scr = nc.dram_tensor("w13scr", [H13, D], BF16, kind="Internal").ap()
    w2scr = nc.dram_tensor("w2scr", [D, HF], BF16, kind="Internal").ap()
    qscr = nc.dram_tensor("qscr", [T, CQ], BF16, kind="Internal").ap()
    kscr = nc.dram_tensor("kscr", [T, CQ], BF16, kind="Internal").ap()
    vscr = nc.dram_tensor("vscr", [T, CQ], BF16, kind="Internal").ap()
    yscr = nc.dram_tensor("yscr", [T, CQ], BF16, kind="Internal").ap()
    apart = nc.dram_tensor("apart", [T, D], F32, kind="Internal").ap()
    ascat = nc.dram_tensor("ascat", [MYT, D], F32, kind="Internal").ap()
    xpscr = nc.dram_tensor("xpscr", [MYT, D], F32, kind="Internal").ap()
    xqscr = nc.dram_tensor("xqscr", [MYT, D], BF16, kind="Internal").ap()
    tmpscr = nc.dram_tensor("tmpscr", [MYT, HF], BF16, kind="Internal").ap()
    iqscr = nc.dram_tensor("iqscr", [NHC, 128, MYT], BF16, kind="Internal").ap()
    rsscr = nc.dram_tensor("rsscr", [2, MYT], F32, kind="Internal").ap()

    dbg = {}
    if DEBUG:
        for name, src in [("d_qscr", qscr), ("d_kscr", kscr), ("d_vscr", vscr),
                          ("d_yscr", yscr), ("d_apart", apart), ("d_ascat", ascat),
                          ("d_xpscr", xpscr), ("d_xqscr", xqscr), ("d_tmpscr", tmpscr),
                          ("d_iqscr", iqscr)]:
            dbg[name] = (nc.dram_tensor(name, list(src.shape), src.dtype,
                                        kind="ExternalOutput").ap(), src)

    stack = ExitStack()
    with stack:
        consts = stack.enter_context(nc.sbuf_tensor("consts", [128, 8], F32))
        CONST_VALS = [SCALE, 1.0 / 128.0, RMS_EPS, LN_EPS, 256.5]
        for i, v in enumerate(CONST_VALS):
            nc.const_aps.register(F32, v, consts[:, i:i + 1])
        onesf32 = stack.enter_context(nc.sbuf_tensor("onesf32", [1, 128], F32))
        lamt = stack.enter_context(nc.sbuf_tensor("lamt", [1, 1544], F32))
        neg08lam = stack.enter_context(nc.sbuf_tensor("neg08lam", [128, 1], F32))
        stat = stack.enter_context(nc.sbuf_tensor("stat", [128, 64], F32))
        fstat = stack.enter_context(nc.sbuf_tensor("fstat", [128, 2 * TCM * NPG + 2 * TCM], F32))
        trash_bf = stack.enter_context(nc.sbuf_tensor("trash_bf", [128, D], BF16))

        psT = [stack.enter_context(nc.psum_tensor(f"ps{i}", [128, 512], F32))
               for i in range(4)]
        psB = [stack.enter_context(nc.psum_tensor(f"psb{i}", [128, 1024], F32))
               for i in range(2)]
        ps = psT + [psB[0][:, 0:512], psB[0][:, 512:1024],
                    psB[1][:, 0:512], psB[1][:, 512:1024]]

        SBUF_TOP = 229376
        base0 = SBUF_TOP - nc.sbuf_bytes_remaining

        class Arena:
            def __init__(self, base, top):
                self.off = base
                self.top = top
                self.n = 0

            def alloc(self, shape, dtype):
                per = int(np.prod(shape[1:])) * mybir.dt.size(dtype)
                off = (self.off + 31) & ~31
                self.off = off + per
                assert self.off <= self.top, f"SBUF overflow: {self.off} > {self.top}"
                self.n += 1
                return nc.alloc_sbuf_tensor_at(f"ar{self.n}", shape, dtype, offset=off).ap()

            def mark(self):
                return self.off

            def reset(self, off):
                self.off = off

        ar = Arena(base0 + 32, SBUF_TOP - 64)
        m_pre = ar.mark()
        rotc = ar.alloc([128, CQ], F32)
        rots = ar.alloc([128, CQ], F32)
        msk = ar.alloc([128, 2 * 1024], BF16)

        sem_names = ["sXS", "sWQ", "sWK", "sWV", "sWO", "sW13", "sW2",
                     "sXT", "sWT", "sQKS", "sVS", "sKT", "sVG", "sYS", "sWOT",
                     "sAPS", "sCC", "sLD", "sXPS", "sXQS", "sXQT", "sW13T",
                     "sTMS", "sTMT", "sIQS", "sIQL", "sW2T", "sRS", "sACB",
                     "sOUT", "sPE", "sVE", "sSE", "sGE", "sIN", "sDBG", "sVG2", "sLD2", "sLD3", "sLD4", "sXS2"]
        sems = {n: stack.enter_context(nc.semaphore(n)) for n in sem_names}
        p = Prog()
        ALL_ENG = ["sync", "scalar", "vector", "tensor", "gpsimd"]
        ALLS = list(sem_names)

        def phase_barrier(engines=ALL_ENG):
            for e in engines:
                for sname in ALLS:
                    p.wait(e, sname)

        def ve(fn):
            p.wait("vector", "sVE")
            p.op("vector", fn, "sVE")

        def se(fn):
            p.wait("scalar", "sSE")
            p.op("scalar", fn, "sSE")

        def ge(fn):
            p.wait("gpsimd", "sGE")
            p.op("gpsimd", fn, "sGE")

        def pe(fn, inc=False):
            p.op("tensor", fn, "sPE" if inc else None)

        class Slot:
            def __init__(self):
                self.dep = []

            def require(self, eng):
                for sem, v in self.dep:
                    p.wait(eng, sem, p.cnt[sem] if v is None else v)

            def consumed(self, *sems, total=()):
                self.dep = [(x, p.cnt[x]) for x in sems] + [(x, None) for x in total]

        def tdma(fn, sem):
            p.dma("sync", fn, sem)
            p.wait("sync", sem)

        for i, v in enumerate(CONST_VALS):
            ve(lambda e, i=i, v=v: e.memset(consts[:, i:i + 1], v))
        ve(lambda e: e.memset(onesf32[0:1, :], 1.0))
        p.dma("gpsimd", lambda e: e.dma_start(out=rotc[:], in_=rotc_in), "sIN")
        p.dma("gpsimd", lambda e: e.dma_start(out=rots[:], in_=rots_in), "sIN")
        p.dma("gpsimd", lambda e: e.dma_start(
            out=msk[:].rearrange("p (c j) -> p c j", j=1024),
            in_=msk_in.rearrange("c p j -> p c j")), "sIN")
        p.dma("gpsimd", lambda e: e.dma_start(
            out=lamt[0:1, 0:1024], in_=lqk.rearrange("(o a) j -> o (a j)", o=1)), "sIN")

        p.dma("gpsimd", lambda e: e.dma_start(out=xscr[:, 0:D // 2], in_=xb[:, 0:D // 2]), "sXS")
        p.dma("gpsimd", lambda e: e.dma_start(out=wqscr, in_=wq), "sWQ")
        p.dma("gpsimd", lambda e: e.dma_start(out=xscr[:, D // 2:D], in_=xb[:, D // 2:D]), "sXS2")
        p.dma("gpsimd", lambda e: e.dma_start(out=wkscr, in_=wk), "sWK")
        p.dma("gpsimd", lambda e: e.dma_start(out=wvscr, in_=wv), "sWV")

        p.wait("vector", "sIN")
        L = lamt
        ve(lambda e: e.tensor_tensor(L[0:1, 1024:1280], L[0:1, 0:256], L[0:1, 256:512], ALU.mult))
        ve(lambda e: e.tensor_tensor(L[0:1, 1280:1536], L[0:1, 512:768], L[0:1, 768:1024], ALU.mult))
        p.wait("vector", "sVE")
        ve(lambda e: e.reduce_sum(L[0:1, 1536:1537], L[0:1, 1024:1280], axis=AX.X))
        ve(lambda e: e.reduce_sum(L[0:1, 1537:1538], L[0:1, 1280:1536], axis=AX.X))
        p.wait("scalar", "sVE")
        se(lambda e: e.activation(L[0:1, 1538:1540], L[0:1, 1536:1538], AF.Exp))
        p.wait("vector", "sSE")
        ve(lambda e: e.tensor_tensor(L[0:1, 1540:1541], L[0:1, 1538:1539], L[0:1, 1539:1540], ALU.subtract))
        p.wait("vector", "sVE")
        ve(lambda e: e.tensor_scalar(L[0:1, 1541:1542], L[0:1, 1540:1541], LAMBDA_INIT, -0.8, ALU.add, ALU.mult))
        p.wait("tensor", "sVE")
        p.wait("tensor", "sIN")
        pe(lambda e: e.matmul(ps[0][:, 0:1], onesf32[0:1, :], L[0:1, 1541:1542], start=True, stop=True), inc=True)
        p.wait("vector", "sPE")
        ve(lambda e: e.tensor_copy(neg08lam[:], ps[0][:, 0:1]))

        m0 = ar.mark()
        xT = [ar.alloc([128, T], BF16) for _ in range(NDC)]
        wTg2 = [[ar.alloc([128, 512], BF16) for _ in range(NDC)] for _ in range(2)]
        stg2 = [[ar.alloc([128, 512], F32) for _ in range(4)] for _ in range(2)]
        stg_bf = [ar.alloc([128, 512], BF16) for _ in range(2)]
        seg42 = [ar.alloc([128, 8], F32) for _ in range(2)]

        p.wait("sync", "sXS")
        for dc in range(NDC // 2):
            tdma(lambda e, dc=dc: e.dma_start_transpose(
                xT[dc][:], xscr[:, dc * 128:(dc + 1) * 128]), "sXT")
        p.wait("sync", "sXS2")
        for dc in range(NDC // 2, NDC):
            tdma(lambda e, dc=dc: e.dma_start_transpose(
                xT[dc][:], xscr[:, dc * 128:(dc + 1) * 128]), "sXT")

        groups = []
        for wsem, wscrt, dst_scr, is_qk, dsem in [
                ("sWQ", wqscr, qscr, True, "sQKS"),
                ("sWK", wkscr, kscr, True, "sQKS"),
                ("sWV", wvscr, vscr, False, "sVS")]:
            for cg in range(CQ // 512):
                groups.append((wsem, wscrt, dst_scr, is_qk, dsem, cg))

        wtg_slot = [Slot(), Slot()]
        wtg_sem = [0, 0]
        pb_slot = [Slot() for _ in range(4)]
        pb_slot[0].dep = [("sVE", p.cnt["sVE"])]
        sq_slot = [Slot() for _ in range(2)]
        qn_slot = [Slot() for _ in range(2)]
        t12_slot = [Slot() for _ in range(2)]
        sb_slot = [Slot() for _ in range(2)]
        seg_slot = [Slot() for _ in range(2)]

        def load_group(gi):
            wsem, wscrt, _, _, _, cg = groups[gi]
            buf = gi % 2
            p.wait("sync", wsem, 16)
            wtg_slot[buf].require("sync")
            for dc in range(NDC):
                tdma(lambda e, dc=dc, cg=cg, wscrt=wscrt, buf=buf: e.dma_start_transpose(
                    wTg2[buf][dc][:], wscrt[cg * 512:(cg + 1) * 512, dc * 128:(dc + 1) * 128]), "sWT")
            wtg_sem[buf] = p.cnt["sWT"]

        p.wait("gpsimd", "sXT")
        p.dma("gpsimd", lambda e: e.dma_start(out=woscr, in_=wo), "sWO")
        load_group(0)
        for gi, (wsem, wscrt, dst_scr, is_qk, dsem, cg) in enumerate(groups):
            buf = gi % 2
            if gi + 1 < len(groups):
                load_group(gi + 1)
            wTg = wTg2[buf]
            p.wait("tensor", "sWT", wtg_sem[buf])
            if gi == 0:
                p.wait("tensor", "sXT")
            for tc in range(TCA):
                bank = ps[tc % 4]
                pb_slot[tc % 4].require("tensor")
                for dc in range(NDC):
                    pe(lambda e, dc=dc, tc=tc, bank=bank, wTg=wTg: e.matmul(
                        bank[:, 0:512], xT[dc][:, tc * 128:(tc + 1) * 128],
                        wTg[dc][:, 0:512], start=(dc == 0), stop=(dc == NDC - 1)),
                        inc=(dc == NDC - 1))
                mm_done = p.cnt["sPE"]
                if not is_qk:
                    sb = stg_bf[tc % 2]
                    p.wait("scalar", "sPE", mm_done)
                    sb_slot[tc % 2].require("scalar")
                    se(lambda e, bank=bank, sb=sb: e.copy(sb[:], bank[:, 0:512]))
                    pb_slot[tc % 4].consumed("sSE")
                    p.wait("gpsimd", "sSE")
                    p.dma("gpsimd", lambda e, sb=sb, tc=tc, cg=cg, dst_scr=dst_scr: e.dma_start(
                        out=dst_scr[tc * 128:(tc + 1) * 128, cg * 512:(cg + 1) * 512],
                        in_=sb[:]), dsem)
                    sb_slot[tc % 2].consumed(total=(dsem,))
                else:
                    sq_t, qn_t, t1_t, t2_t = stg2[tc % 2]
                    seg4 = seg42[tc % 2]
                    sb = stg_bf[tc % 2]
                    p.wait("scalar", "sPE", mm_done)
                    sq_slot[tc % 2].require("scalar")
                    se(lambda e, bank=bank, sq_t=sq_t: e.activation(sq_t[:], bank[:, 0:512], AF.Square))
                    p.wait("vector", "sSE")
                    seg_slot[tc % 2].require("vector")
                    ve(lambda e, sq_t=sq_t, seg4=seg4: e.tensor_reduce(
                        seg4[:, 0:4].rearrange("p (a b) -> p a b", b=1),
                        sq_t[:].rearrange("p (a b) -> p a b", a=4), axis=AX.X, op=ALU.add))
                    sq_slot[tc % 2].consumed("sVE")
                    p.wait("scalar", "sVE")
                    se(lambda e, seg4=seg4: e.activation(seg4[:, 4:8], seg4[:, 0:4], AF.Sqrt,
                                                         bias=RMS_EPS, scale=1.0 / 128.0))
                    p.wait("vector", "sSE")
                    ve(lambda e, seg4=seg4: e.reciprocal(seg4[:, 0:4], seg4[:, 4:8]))
                    qn_slot[tc % 2].require("vector")
                    for sg in range(4):
                        ve(lambda e, bank=bank, qn_t=qn_t, sg=sg, seg4=seg4: e.tensor_scalar(
                            qn_t[:, sg * 128:(sg + 1) * 128], bank[:, sg * 128:(sg + 1) * 128],
                            seg4[:, sg:sg + 1], None, ALU.mult))
                    pb_slot[tc % 4].consumed("sSE", "sVE")
                    seg_slot[tc % 2].consumed("sVE")
                    t12_slot[tc % 2].require("vector")
                    ve(lambda e, qn_t=qn_t, t1_t=t1_t, cg=cg: e.tensor_tensor(
                        t1_t[:], qn_t[:], rotc[:, cg * 512:(cg + 1) * 512], ALU.mult))
                    qn_done = p.cnt["sVE"] - 1
                    p.wait("gpsimd", "sVE", qn_done)
                    t12_slot[tc % 2].require("gpsimd")
                    ge(lambda e, qn_t=qn_t, t2_t=t2_t, cg=cg: e.tensor_tensor(
                        t2_t[:].rearrange("p (a b c) -> p a b c", a=4, c=64),
                        qn_t[:].rearrange("p (a b c) -> p a b c", a=4, c=64)[:, :, ::-1, :],
                        rots[:, cg * 512:(cg + 1) * 512].rearrange("p (a b c) -> p a b c", a=4, c=64),
                        ALU.mult))
                    qn_slot[tc % 2].consumed("sVE", "sGE")
                    sb_slot[tc % 2].require("gpsimd")
                    t1_done = p.cnt["sVE"]
                    p.wait("gpsimd", "sVE", t1_done)
                    ge(lambda e, t1_t=t1_t, t2_t=t2_t, sb=sb: e.tensor_tensor(
                        sb[:], t1_t[:], t2_t[:], ALU.add))
                    t12_slot[tc % 2].consumed("sGE")
                    p.wait("gpsimd", "sGE")
                    p.dma("gpsimd", lambda e, sb=sb, tc=tc, cg=cg, dst_scr=dst_scr: e.dma_start(
                        out=dst_scr[tc * 128:(tc + 1) * 128, cg * 512:(cg + 1) * 512],
                        in_=sb[:]), dsem)
                    sb_slot[tc % 2].consumed(total=(dsem,))
            wtg_slot[buf].consumed("sPE")

        ar.reset(m0)
        KQ2 = [[ar.alloc([128, T], BF16) for _ in range(4)] for _ in range(2)]
        Vt2 = [ar.alloc([128, TCA * 257], BF16) for _ in range(2)]
        Pt = [ar.alloc([128, 1024], BF16) for _ in range(2)]
        Ptm = [ar.alloc([128, 1024], BF16) for _ in range(2)]
        oe2 = [ar.alloc([128, 8 * 257], F32) for _ in range(2)]
        eps_t = ar.alloc([128, 280], F32)
        yst = [ar.alloc([128, 4 * 256], BF16) for _ in range(2)]

        phase_barrier()
        for vb in range(2):
            for c in range(TCA):
                ve(lambda e, vb=vb, c=c: e.memset(Vt2[vb][:, c * 257 + 256:c * 257 + 257], 1.0))
        p.wait("gpsimd", "sIN")
        p.wait("gpsimd", "sVE")

        psS2 = [psB[0], psB[1]]
        psO = [psT[0], psT[1], psT[2], psT[3]]
        kq_slot = [Slot(), Slot()]
        vt_slot = [Slot(), Slot()]
        kq_sem = [0, 0]
        vt_sem = [0, 0]
        psS_slot = [Slot(), Slot()]
        psS_slot[0].dep = [("sVE", p.cnt["sVE"])]
        pt_slot = [Slot(), Slot()]
        ptm_slot = [Slot(), Slot()]
        psO_slot = [Slot() for _ in range(4)]
        oe_slot = [Slot(), Slot()]
        ys_slot = [Slot(), Slot()]
        ep_jobs = []

        def emit_epilogue(h, qt, buf, sse_evac):
            oe = oe2[buf]
            c0 = h * HD
            p.wait("vector", "sSE", sse_evac)
            ve(lambda e, oe=oe: e.tensor_copy(
                eps_t[:, 0:8], oe[:, 256::257]))
            ve(lambda e: e.reciprocal(eps_t[:, 8:16], eps_t[:, 0:8]))
            ve(lambda e: e.tensor_scalar(eps_t[:, 8:12], eps_t[:, 8:12], 0.8, None, ALU.mult))
            ve(lambda e: e.tensor_scalar(
                eps_t[:, 12:16], eps_t[:, 12:16], neg08lam[:, 0:1], None, ALU.mult))
            ys = yst[qt % 2]
            ys_slot[qt % 2].require("vector")
            for qs in range(4):
                ve(lambda e, qs=qs, oe=oe: e.tensor_scalar(
                    eps_t[:, 16:272], oe[:, qs * 257:qs * 257 + 256], eps_t[:, 8 + qs:9 + qs], None, ALU.mult))
                ve(lambda e, qs=qs, ys=ys, oe=oe: e.scalar_tensor_tensor(
                    ys[:, qs * 256:(qs + 1) * 256], oe[:, (4 + qs) * 257:(4 + qs) * 257 + 256],
                    eps_t[:, 12 + qs:13 + qs], eps_t[:, 16:272], ALU.mult, ALU.add))
            p.wait("gpsimd", "sVE")
            p.dma("gpsimd", lambda e, ys=ys, qt=qt, c0=c0: e.dma_start(
                out=yscr[qt * 512:(qt + 1) * 512, c0:c0 + 256].rearrange("(a p) j -> p a j", p=128),
                in_=ys[:].rearrange("p (a j) -> p a j", j=256)), "sYS")
            ys_slot[qt % 2].consumed(total=("sYS",))
            oe_slot[buf].consumed("sVE")

        def load_head(h):
            hb = h % 2
            c0 = h * HD
            p.wait("sync", "sQKS")
            kq_slot[hb].require("sync")
            for j, scr_c0 in enumerate([(kscr, c0), (kscr, c0 + 128), (qscr, c0), (qscr, c0 + 128)]):
                scr, cc0 = scr_c0
                tdma(lambda e, j=j, hb=hb, scr=scr, cc0=cc0: e.dma_start_transpose(
                    KQ2[hb][j][:], scr[:, cc0:cc0 + 128]), "sKT")
            kq_sem[hb] = p.cnt["sKT"]
            p.wait("gpsimd", "sVS")
            vt_slot[hb].require("gpsimd")
            vsem = "sVG" if hb == 0 else "sVG2"
            p.dma("gpsimd", lambda e, c0=c0, hb=hb: e.dma_start(
                out=Vt2[hb][:].rearrange("p (c j) -> p c j", j=257)[:, :, 0:256],
                in_=vscr[:, c0:c0 + 256].rearrange("(c p) j -> p c j", p=128)), vsem)
            vt_sem[hb] = (vsem, p.cnt[vsem])

        def emit_Shalf(Kh, Qh, m, half, qt):
            sb = psS2[m % 2]
            if half == 0:
                psS_slot[m % 2].require("tensor")
            pe(lambda e, Kh=Kh, Qh=Qh, m=m, half=half, qt=qt, sb=sb: e.matmul(
                sb[:, half * 512:(half + 1) * 512],
                Kh[:, (2 * m + half) * 128:(2 * m + half + 1) * 128],
                Qh[:, qt * 512:(qt + 1) * 512], start=True, stop=True), inc=True)
            return p.cnt["sPE"]

        load_head(0)
        for h in range(NHL):
            hb = h % 2
            if h + 1 < NHL:
                load_head(h + 1)
            p.wait("gpsimd", "sKT")
            CH13 = H13 // NHL
            p.dma("gpsimd", lambda e, h=h: e.dma_start(
                out=w13scr[h * CH13:(h + 1) * CH13, :], in_=w13[h * CH13:(h + 1) * CH13, :]), "sW13")
            K1, K2, Q1, Q2 = KQ2[hb]
            Vt = Vt2[hb]
            p.wait("tensor", "sKT", kq_sem[hb])
            p.wait("tensor", vt_sem[hb][0], vt_sem[hb][1])
            if h < 2:
                p.wait("tensor", "sVE")
            for qt in range(8):
                depth = 1 if h == NHL - 1 and qt >= 6 else 2
                while len(ep_jobs) >= depth:
                    emit_epilogue(*ep_jobs.pop(0))
                mmax = (qt * 4 + 4) // 2
                for at in range(2):
                    Kh = K1 if at == 0 else K2
                    Qh = Q1 if at == 0 else Q2
                    s_done = [emit_Shalf(Kh, Qh, 0, 0, qt), emit_Shalf(Kh, Qh, 0, 1, qt)]
                    for m in range(mmax):
                        Pk = Pt[m % 2]
                        sb = psS2[m % 2]
                        exp_done = [0, 0]
                        for half in range(2):
                            p.wait("scalar", "sPE", s_done[half])
                            if half == 0:
                                pt_slot[m % 2].require("scalar")
                            se(lambda e, sb=sb, Pk=Pk, half=half: e.activation(
                                Pk[:, half * 512:(half + 1) * 512],
                                sb[:, half * 512:(half + 1) * 512], AF.Exp, scale=SCALE))
                            exp_done[half] = p.cnt["sSE"]
                        psS_slot[m % 2].consumed("sSE")
                        diag = m >= mmax - 2
                        dv2 = m - (mmax - 2) if diag else 0
                        msk_done = [0, 0]
                        if diag:
                            for half in range(2):
                                p.wait("vector", "sSE", exp_done[half])
                                if half == 0:
                                    ptm_slot[m % 2].require("vector")
                                ve(lambda e, Pk=Pk, dv2=dv2, m=m, half=half: e.tensor_tensor(
                                    Ptm[m % 2][:, half * 512:(half + 1) * 512],
                                    Pk[:, half * 512:(half + 1) * 512],
                                    msk[:, dv2 * 1024 + half * 512:dv2 * 1024 + (half + 1) * 512],
                                    ALU.mult))
                                msk_done[half] = p.cnt["sVE"]
                            lhsP = Ptm[m % 2]
                        else:
                            lhsP = Pk
                        for half in range(2):
                            kc = 2 * m + half
                            if half == 0 and m + 1 < mmax:
                                ns0 = emit_Shalf(Kh, Qh, m + 1, 0, qt)
                            if diag:
                                p.wait("tensor", "sVE", msk_done[half])
                            else:
                                p.wait("tensor", "sSE", exp_done[half])
                            if m == 0 and half == 0:
                                for qs in range(4):
                                    psO_slot[qs].require("tensor")
                            for qs in range(4):
                                pe(lambda e, lhsP=lhsP, qs=qs, kc=kc, half=half, mmax=mmax, Vt=Vt: e.matmul(
                                    psO[qs][:, 0:257],
                                    lhsP[:, half * 512 + qs * 128:half * 512 + (qs + 1) * 128],
                                    Vt[:, kc * 257:(kc + 1) * 257],
                                    start=(kc == 0), stop=(kc == 2 * mmax - 1)),
                                    inc=(qs == 3 and half == 1))
                            if half == 0 and m + 1 < mmax:
                                ns1 = emit_Shalf(Kh, Qh, m + 1, 1, qt)
                        if m + 1 < mmax:
                            s_done = [ns0, ns1]
                        if diag:
                            ptm_slot[m % 2].consumed("sPE")
                            pt_slot[m % 2].consumed("sPE", "sVE")
                        else:
                            pt_slot[m % 2].consumed("sPE")
                    pv_done = p.cnt["sPE"]
                    oe = oe2[qt % 2]
                    base = 0 if at == 0 else 4
                    p.wait("vector", "sPE", pv_done)
                    p.wait("scalar", "sPE", pv_done)
                    oe_slot[qt % 2].require("vector")
                    oe_slot[qt % 2].require("scalar")
                    for qs in range(4):
                        if qs % 2 == 0:
                            ve(lambda e, qs=qs, oe=oe, base=base: e.tensor_copy(
                                oe[:, (base + qs) * 257:(base + qs + 1) * 257], psO[qs][:, 0:257]))
                        else:
                            se(lambda e, qs=qs, oe=oe, base=base: e.copy(
                                oe[:, (base + qs) * 257:(base + qs + 1) * 257], psO[qs][:, 0:257]))
                    for qs in range(4):
                        psO_slot[qs].dep = [("sVE", p.cnt["sVE"]), ("sSE", p.cnt["sSE"])]
                ep_jobs.append((h, qt, qt % 2, p.cnt["sSE"]))
            kq_slot[hb].consumed("sPE")
            vt_slot[hb].consumed("sPE")
        while ep_jobs:
            emit_epilogue(*ep_jobs.pop(0))

        ar.reset(m_pre)
        yT = [ar.alloc([128, T], BF16) for _ in range(NCC)]
        woT = [ar.alloc([128, D], BF16) for _ in range(NCC)]
        apst = [ar.alloc([128, D], F32) for _ in range(2)]

        phase_barrier()
        p.dma("gpsimd", lambda e: e.dma_start(out=w2scr[:, 0:HF // 2], in_=w2[:, 0:HF // 2]), "sW2")
        p.dma("gpsimd", lambda e: e.dma_start(out=w2scr[:, HF // 2:HF], in_=w2[:, HF // 2:HF]), "sW2")
        p.wait("sync", "sYS")
        p.wait("sync", "sWO", 16)
        for cc in range(NCC):
            tdma(lambda e, cc=cc: e.dma_start_transpose(
                yT[cc][:], yscr[:, cc * 128:(cc + 1) * 128]), "sWOT")
            tdma(lambda e, cc=cc: e.dma_start_transpose(
                woT[cc][:], woscr[:, cc * 128:(cc + 1) * 128]), "sWOT")
        p.wait("tensor", "sWOT")
        wo_bank = [Slot(), Slot()]
        ap_slot = [Slot(), Slot()]
        for tc in range(TCA):
            banks = ps[0:4] if tc % 2 == 0 else ps[4:8]
            a_t = apst[tc % 2]
            wo_bank[tc % 2].require("tensor")
            for cc in range(NCC):
                for cs in range(4):
                    pe(lambda e, cc=cc, cs=cs, tc=tc, banks=banks: e.matmul(
                        banks[cs][:, 0:512], yT[cc][:, tc * 128:(tc + 1) * 128],
                        woT[cc][:, cs * 512:(cs + 1) * 512],
                        start=(cc == 0), stop=(cc == NCC - 1)),
                        inc=(cc == NCC - 1 and cs == 3))
            mm_done = p.cnt["sPE"]
            p.wait("vector", "sPE", mm_done)
            p.wait("scalar", "sPE", mm_done)
            ap_slot[tc % 2].require("vector")
            ap_slot[tc % 2].require("scalar")
            for cs in range(4):
                if cs % 2 == 0:
                    ve(lambda e, cs=cs, a_t=a_t, banks=banks: e.tensor_copy(
                        a_t[:, cs * 512:(cs + 1) * 512], banks[cs][:, 0:512]))
                else:
                    se(lambda e, cs=cs, a_t=a_t, banks=banks: e.copy(
                        a_t[:, cs * 512:(cs + 1) * 512], banks[cs][:, 0:512]))
            wo_bank[tc % 2].consumed("sVE", "sSE")
            p.wait("sync", "sVE")
            p.wait("sync", "sSE")
            p.dma("sync", lambda e, a_t=a_t, tc=tc: e.dma_start(
                out=apart[tc * 128:(tc + 1) * 128, :], in_=a_t[:]), "sAPS")
            ap_slot[tc % 2].consumed(total=("sAPS",))
            if tc % 8 == 7:
                ch = tc // 8
                p.wait("gpsimd", "sAPS")
                p.op("gpsimd", lambda e, ch=ch: e.collective_compute(
                    "ReduceScatter", ALU.add,
                    replica_groups=[[0, 1], [2, 3], [4, 5], [6, 7]],
                    ins=[apart[ch * 1024:(ch + 1) * 1024, :].opt()],
                    outs=[ascat[ch * 512:(ch + 1) * 512, :].opt()]), "sCC", 1)

        _sCC_save = p.cnt["sCC"]
        p.cnt["sCC"] = 0
        ar.reset(m_pre)
        NPR = 4
        r0t = [ar.alloc([128, D], F32) for _ in range(NPR)]
        xrt = [ar.alloc([128, D], F32) for _ in range(NPR)]
        xpt = [ar.alloc([128, D], F32) for _ in range(NPR)]
        tqt2 = [ar.alloc([128, D], F32) for _ in range(NPR)]
        zqt2 = [ar.alloc([128, D], I32) for _ in range(NPR)]
        xqt = [ar.alloc([128, D], BF16) for _ in range(NPR)]

        phase_barrier()
        p.cnt["sCC"] = _sCC_save
        ST = stat
        r_slot = [Slot() for _ in range(4)]
        xp_slot = [Slot() for _ in range(4)]
        xq_slot = [Slot() for _ in range(4)]
        for tc in range(TCM):
            pr = tc % 4
            SB0 = 12 * pr
            r_t, x_t, xp_t, xq_t = r0t[pr], xrt[pr], xpt[pr], xqt[pr]
            tq_t, zq_t = tqt2[pr], zqt2[pr]
            r_slot[pr].require("sync")
            p.wait("sync", "sCC", tc // 4 + 1)
            lsem = ["sLD", "sLD2", "sLD3", "sLD4"][pr]
            p.dma("sync", lambda e, r_t=r_t, tc=tc: e.dma_start(
                out=r_t[:], in_=ascat[tc * 128:(tc + 1) * 128, :]), lsem)
            p.dma("sync", lambda e, x_t=x_t, tc=tc: e.dma_start(
                out=x_t[:], in_=xmy[tc * 128:(tc + 1) * 128, :]), lsem)
            ld_done = p.cnt[lsem]
            p.wait("vector", lsem, ld_done)
            ve(lambda e, r_t=r_t, x_t=x_t: e.tensor_tensor(r_t[:], r_t[:], x_t[:], ALU.add))
            radd_done = p.cnt["sVE"]
            p.wait("scalar", "sVE", radd_done)
            se(lambda e, r_t=r_t, SB0=SB0: e.activation(trash_bf[:], r_t[:], AF.Copy, accum_out=ST[:, SB0:SB0 + 1]))
            se(lambda e, r_t=r_t, SB0=SB0: e.activation(trash_bf[:], r_t[:], AF.Square, accum_out=ST[:, SB0 + 1:SB0 + 2]))
            acc_done = p.cnt["sSE"]
            p.wait("vector", "sSE", acc_done)
            ve(lambda e, SB0=SB0: e.tensor_scalar(ST[:, SB0 + 2:SB0 + 3], ST[:, SB0:SB0 + 1], 1.0 / D, None, ALU.mult))
            ve(lambda e, SB0=SB0: e.tensor_scalar(ST[:, SB0 + 3:SB0 + 4], ST[:, SB0 + 1:SB0 + 2], 1.0 / D, None, ALU.mult))
            ve(lambda e, SB0=SB0: e.tensor_tensor(ST[:, SB0 + 4:SB0 + 5], ST[:, SB0 + 2:SB0 + 3], ST[:, SB0 + 2:SB0 + 3], ALU.mult))
            ve(lambda e, SB0=SB0: e.tensor_tensor(ST[:, SB0 + 5:SB0 + 6], ST[:, SB0 + 3:SB0 + 4], ST[:, SB0 + 4:SB0 + 5], ALU.subtract))
            var_done = p.cnt["sVE"]
            p.wait("scalar", "sVE", var_done)
            se(lambda e, SB0=SB0: e.activation(ST[:, SB0 + 6:SB0 + 7], ST[:, SB0 + 5:SB0 + 6], AF.Sqrt, bias=LN_EPS))
            p.wait("vector", "sSE")
            ve(lambda e, SB0=SB0: e.reciprocal(ST[:, SB0 + 7:SB0 + 8], ST[:, SB0 + 6:SB0 + 7]))
            xp_slot[pr].require("vector")
            ve(lambda e, r_t=r_t, xp_t=xp_t, SB0=SB0: e.tensor_scalar(
                xp_t[:], r_t[:], ST[:, SB0 + 2:SB0 + 3], ST[:, SB0 + 7:SB0 + 8], ALU.subtract, ALU.mult))
            r_slot[pr].consumed("sVE")
            xpw_done = p.cnt["sVE"]
            p.wait("gpsimd", "sVE", xpw_done)
            p.dma("gpsimd", lambda e, xp_t=xp_t, tc=tc: e.dma_start(
                out=xpscr[tc * 128:(tc + 1) * 128, :], in_=xp_t[:]), "sXPS")
            ve(lambda e, xp_t=xp_t, SB0=SB0: e.tensor_reduce(
                ST[:, SB0 + 8:SB0 + 9], xp_t[:], axis=AX.X, op=ALU.max, apply_absolute_value=True))
            ve(lambda e, SB0=SB0: e.tensor_scalar(ST[:, SB0 + 9:SB0 + 10], ST[:, SB0 + 8:SB0 + 9], 1e-5, 1.0 / 127.0, ALU.max, ALU.mult))
            ve(lambda e, SB0=SB0: e.reciprocal(ST[:, SB0 + 10:SB0 + 11], ST[:, SB0 + 9:SB0 + 10]))
            ve(lambda e, xp_t=xp_t, tq_t=tq_t, SB0=SB0: e.tensor_scalar(
                tq_t[:], xp_t[:], ST[:, SB0 + 10:SB0 + 11], None, ALU.mult))
            ve(lambda e, tq_t=tq_t, zq_t=zq_t: e.tensor_copy(zq_t[:], tq_t[:]))
            xp_slot[pr].consumed("sVE", total=("sXPS",))
            xq_slot[pr].require("vector")
            ve(lambda e, xq_t=xq_t, zq_t=zq_t, SB0=SB0: e.tensor_scalar(
                xq_t[:], zq_t[:], ST[:, SB0 + 9:SB0 + 10], None, ALU.mult))
            xqw_done = p.cnt["sVE"]
            p.wait("gpsimd", "sVE", xqw_done)
            p.dma("gpsimd", lambda e, xq_t=xq_t, tc=tc: e.dma_start(
                out=xqscr[tc * 128:(tc + 1) * 128, :], in_=xq_t[:]), "sXQS")
            xq_slot[pr].consumed(total=("sXQS",))

        ar.reset(m_pre)
        xqT = [ar.alloc([128, MYT], BF16) for _ in range(NDC)]
        w13g2 = [[ar.alloc([128, 512], BF16) for _ in range(2 * NDC)] for _ in range(2)]
        g42 = [[ar.alloc([128, 512], F32) for _ in range(3)] for _ in range(2)]
        tmp_bf = [ar.alloc([128, 512], BF16) for _ in range(2)]
        nwrow = ar.alloc([1, HF], F32)
        nwb = ar.alloc([128, HF], BF16)

        phase_barrier()
        p.wait("sync", "sXQS")
        for dc in range(NDC):
            tdma(lambda e, dc=dc: e.dma_start_transpose(
                xqT[dc][:], xqscr[:, dc * 128:(dc + 1) * 128]), "sXQT")
        p.dma("sync", lambda e: e.dma_start(out=nwrow[0:1, :], in_=nw_in), "sLD")
        p.wait("tensor", "sLD")
        for j in range(NPG):
            p.wait("tensor", "sVE")
            pe(lambda e, j=j: e.matmul(ps[6][:, 0:512], onesf32[0:1, :],
                                       nwrow[0:1, j * 512:(j + 1) * 512], start=True, stop=True), inc=True)
            p.wait("vector", "sPE")
            ve(lambda e, j=j: e.tensor_copy(nwb[:, j * 512:(j + 1) * 512], ps[6][:, 0:512]))

        FS = fstat
        nwb_done_sve = None
        w13_slot = [Slot(), Slot()]
        w13_sem = [0, 0]
        pA_slot = [Slot() for _ in range(4)]
        pB_slot = [Slot() for _ in range(4)]
        pB_slot[2].dep = [("sVE", p.cnt["sVE"])]
        gs_slot = [Slot(), Slot()]
        tb_slot = [Slot(), Slot()]

        def load_pg(pg):
            buf = pg % 2
            p.wait("sync", "sW13")
            w13_slot[buf].require("sync")
            for dc in range(NDC):
                tdma(lambda e, dc=dc, pg=pg, buf=buf: e.dma_start_transpose(
                    w13g2[buf][dc][:], w13scr[pg * 512:(pg + 1) * 512, dc * 128:(dc + 1) * 128]), "sW13T")
                tdma(lambda e, dc=dc, pg=pg, buf=buf: e.dma_start_transpose(
                    w13g2[buf][NDC + dc][:], w13scr[HF + pg * 512:HF + (pg + 1) * 512, dc * 128:(dc + 1) * 128]), "sW13T")
            w13_sem[buf] = p.cnt["sW13T"]

        load_pg(0)
        for pg in range(NPG):
            buf = pg % 2
            if pg + 1 < NPG:
                load_pg(pg + 1)
            w1g = w13g2[buf][0:NDC]
            w3g = w13g2[buf][NDC:2 * NDC]
            p.wait("tensor", "sW13T", w13_sem[buf])
            if pg == 0:
                p.wait("tensor", "sXQT")
            for tc in range(TCM):
                bA = ps[(tc % 4)]
                bB = ps[4 + (tc % 4)]
                pA_slot[tc % 4].require("tensor")
                for dc in range(NDC):
                    pe(lambda e, dc=dc, tc=tc, bA=bA, w1g=w1g: e.matmul(
                        bA[:, 0:512], xqT[dc][:, tc * 128:(tc + 1) * 128], w1g[dc][:],
                        start=(dc == 0), stop=(dc == NDC - 1)), inc=(dc == NDC - 1))
                pB_slot[tc % 4].require("tensor")
                for dc in range(NDC):
                    pe(lambda e, dc=dc, tc=tc, bB=bB, w3g=w3g: e.matmul(
                        bB[:, 0:512], xqT[dc][:, tc * 128:(tc + 1) * 128], w3g[dc][:],
                        start=(dc == 0), stop=(dc == NDC - 1)), inc=(dc == NDC - 1))
                mm_done = p.cnt["sPE"]
                r1_t, rr_t, g_t = g42[tc % 2]
                tb = tmp_bf[tc % 2]
                p.wait("scalar", "sPE", mm_done)
                gs_slot[tc % 2].require("scalar")
                se(lambda e, bA=bA, r1_t=r1_t: e.activation(r1_t[:], bA[:, 0:512], AF.Relu))
                pA_slot[tc % 4].consumed("sSE")
                se(lambda e, r1_t=r1_t, rr_t=rr_t: e.activation(rr_t[:], r1_t[:], AF.Square))
                sq_done = p.cnt["sSE"]
                p.wait("vector", "sSE", sq_done)
                ve(lambda e, rr_t=rr_t, g_t=g_t, bB=bB: e.tensor_tensor(
                    g_t[:], rr_t[:], bB[:, 0:512], ALU.mult))
                pB_slot[tc % 4].consumed("sVE")
                g_done = p.cnt["sVE"]
                p.wait("scalar", "sVE", g_done)
                se(lambda e, g_t=g_t, pg=pg, tc=tc: e.activation(
                    trash_bf[:, 0:512], g_t[:], AF.Square,
                    accum_out=FS[:, (pg * TCM + tc):(pg * TCM + tc) + 1]))
                p.wait("gpsimd", "sVE", g_done)
                tb_slot[tc % 2].require("gpsimd")
                ge(lambda e, g_t=g_t, tb=tb, pg=pg: e.tensor_tensor(
                    tb[:], g_t[:], nwb[:, pg * 512:(pg + 1) * 512], ALU.mult))
                gs_slot[tc % 2].consumed("sSE", "sGE")
                tt_done = p.cnt["sGE"]
                p.wait("vector", "sGE", tt_done)
                ve(lambda e, tb=tb, pg=pg, tc=tc: e.tensor_reduce(
                    FS[:, (NPG * TCM + pg * TCM + tc):(NPG * TCM + pg * TCM + tc) + 1],
                    tb[:], axis=AX.X, op=ALU.max, apply_absolute_value=True))
                p.wait("gpsimd", "sGE", tt_done)
                p.dma("gpsimd", lambda e, tb=tb, tc=tc, pg=pg: e.dma_start(
                    out=tmpscr[tc * 128:(tc + 1) * 128, pg * 512:(pg + 1) * 512],
                    in_=tb[:]), "sTMS")
                tb_slot[tc % 2].consumed("sVE", total=("sTMS",))
            w13_slot[buf].consumed("sPE")

        off5 = 2 * NPG * TCM
        p.wait("vector", "sSE")
        p.wait("vector", "sGE")
        for tc in range(TCM):
            ve(lambda e, tc=tc: e.tensor_reduce(
                FS[:, off5 + tc:off5 + tc + 1],
                FS[:, tc:NPG * TCM + tc:TCM], axis=AX.X, op=ALU.add))
            ve(lambda e, tc=tc: e.tensor_reduce(
                FS[:, off5 + TCM + tc:off5 + TCM + tc + 1],
                FS[:, NPG * TCM + tc:2 * NPG * TCM + tc:TCM], axis=AX.X, op=ALU.max))
        p.wait("scalar", "sVE")
        se(lambda e: e.activation(ST[:, 16:32], FS[:, off5:off5 + TCM], AF.Sqrt,
                                  bias=LN_EPS, scale=1.0 / HF))
        p.wait("vector", "sSE")
        ve(lambda e: e.reciprocal(ST[:, 16:32], ST[:, 16:32]))
        ve(lambda e: e.tensor_tensor(ST[:, 32:48], FS[:, off5 + TCM:off5 + 2 * TCM], ST[:, 16:32], ALU.mult))
        ve(lambda e: e.tensor_scalar(ST[:, 32:48], ST[:, 32:48], 1e-5, 1.0 / 127.0, ALU.max, ALU.mult))
        ve(lambda e: e.reciprocal(ST[:, 48:64], ST[:, 32:48]))
        ve(lambda e: e.tensor_tensor(ST[:, 16:32], ST[:, 16:32], ST[:, 48:64], ALU.mult))
        rsv_done = p.cnt["sVE"]
        p.wait("gpsimd", "sVE", rsv_done)
        for tc in range(TCM):
            p.dma("gpsimd", lambda e, tc=tc: e.dma_start(
                out=rsscr[0:1, tc * 128:(tc + 1) * 128].rearrange("o (j i) -> (o j) i", i=1),
                in_=ST[:, 16 + tc:17 + tc]), "sRS")
            p.dma("gpsimd", lambda e, tc=tc: e.dma_start(
                out=rsscr[1:2, tc * 128:(tc + 1) * 128].rearrange("o (j i) -> (o j) i", i=1),
                in_=ST[:, 32 + tc:33 + tc]), "sRS")

        ar.reset(m_pre)
        rs_row = ar.alloc([1, 2 * MYT], F32)
        rs_b = ar.alloc([128, MYT], BF16)
        inv2_b = ar.alloc([128, MYT], F32)
        t1_t52 = [ar.alloc([128, MYT], F32) for _ in range(2)]
        z52 = [ar.alloc([128, MYT], I32) for _ in range(2)]
        tmpf = [ar.alloc([128, MYT], BF16) for _ in range(2)]
        iq5 = [ar.alloc([128, MYT], BF16) for _ in range(2)]

        phase_barrier()
        p.wait("sync", "sRS")
        p.dma("sync", lambda e: e.dma_start(out=rs_row[0:1, :], in_=rsscr.rearrange("(o a) j -> o (a j)", o=1)), "sLD")
        p.wait("tensor", "sLD")
        p.wait("tensor", "sSE")
        for j in range(MYT // 512):
            p.wait("tensor", "sVE")
            pe(lambda e, j=j: e.matmul(ps[0][:, 0:512], onesf32[0:1, :],
                                       rs_row[0:1, j * 512:(j + 1) * 512], start=True, stop=True), inc=True)
            pe(lambda e, j=j: e.matmul(ps[1][:, 0:512], onesf32[0:1, :],
                                       rs_row[0:1, MYT + j * 512:MYT + (j + 1) * 512], start=True, stop=True), inc=True)
            p.wait("vector", "sPE")
            ve(lambda e, j=j: e.tensor_copy(rs_b[:, j * 512:(j + 1) * 512], ps[0][:, 0:512]))
            ve(lambda e, j=j: e.tensor_copy(inv2_b[:, j * 512:(j + 1) * 512], ps[1][:, 0:512]))
        p.wait("tensor", "sVE")

        p.wait("sync", "sTMS")
        tf_slot = [Slot(), Slot()]
        tz_slot = [Slot(), Slot()]
        iq_slot = [Slot(), Slot()]
        for hc in range(NHC):
            pr = hc % 2
            tf = tmpf[pr]
            iqt = iq5[pr]
            t1_t5, z5 = t1_t52[pr], z52[pr]
            tf_slot[pr].require("sync")
            tdma(lambda e, tf=tf, hc=hc: e.dma_start_transpose(
                tf[:], tmpscr[:, hc * 128:(hc + 1) * 128]), "sTMT")
            tmt_done = p.cnt["sTMT"]
            p.wait("vector", "sTMT", tmt_done)
            tz_slot[pr].require("vector")
            if pr == 0:
                ve(lambda e, tf=tf, t1_t5=t1_t5: e.tensor_tensor(t1_t5[:], tf[:], rs_b[:], ALU.mult))
                tf_slot[pr].consumed("sVE")
                p.wait("vector", "sVE")
            else:
                p.wait("gpsimd", "sTMT", tmt_done)
                tz_slot[pr].require("gpsimd")
                ge(lambda e, tf=tf, t1_t5=t1_t5: e.tensor_tensor(t1_t5[:], tf[:], rs_b[:], ALU.mult))
                tf_slot[pr].consumed("sGE")
                p.wait("vector", "sGE")
            ve(lambda e, t1_t5=t1_t5, z5=z5: e.tensor_copy(z5[:], t1_t5[:]))
            iq_slot[pr].require("vector")
            ve(lambda e, iqt=iqt, z5=z5: e.tensor_tensor(iqt[:], z5[:], inv2_b[:], ALU.mult))
            tz_slot[pr].consumed("sVE", "sGE")
            iqw = ("sVE", p.cnt["sVE"])
            p.wait("gpsimd", iqw[0], iqw[1])
            p.dma("gpsimd", lambda e, iqt=iqt, hc=hc: e.dma_start(
                out=iqscr[hc], in_=iqt[:]), "sIQS")
            iq_slot[pr].consumed(total=("sIQS",))

        ar.reset(m_pre)
        NG5 = 4
        GSZ = NHC // NG5
        w2T52 = [[ar.alloc([128, D], BF16) for _ in range(GSZ)] for _ in range(2)]
        iqg = [ar.alloc([128, 128 * (TCM // 2)], BF16) for _ in range(GSZ)]
        acc = [ar.alloc([128, D], F32) for _ in range(TCM // 2)]
        ot = [ar.alloc([128, D], F32) for _ in range(2)]

        w2t_slot = [Slot(), Slot()]
        w2t_sem = [0, 0]
        iqg_slot = Slot()
        acc_slot = [Slot() for _ in range(TCM // 2)]
        b5_slot = [Slot(), Slot()]
        ot_slot = [Slot(), Slot()]

        def load_w2t(gg):
            buf = gg % 2
            p.wait("sync", "sW2")
            w2t_slot[buf].require("sync")
            for i in range(GSZ):
                hc = (gg % NG5) * GSZ + i
                tdma(lambda e, i=i, hc=hc, buf=buf: e.dma_start_transpose(
                    w2T52[buf][i][:], w2scr[:, hc * 128:(hc + 1) * 128]), "sW2T")
            w2t_sem[buf] = p.cnt["sW2T"]

        def emit_ln2(th, i):
            tc = th * (TCM // 2) + i
            pr = i % 2
            SB0 = 12 * pr
            o_t = ot[pr]
            p.wait("scalar", "sVE")
            se(lambda e, i=i, SB0=SB0: e.activation(trash_bf[:], acc[i][:], AF.Copy, accum_out=ST[:, SB0:SB0 + 1]))
            se(lambda e, i=i, SB0=SB0: e.activation(trash_bf[:], acc[i][:], AF.Square, accum_out=ST[:, SB0 + 1:SB0 + 2]))
            acc_done2 = p.cnt["sSE"]
            p.wait("vector", "sSE", acc_done2)
            ve(lambda e, SB0=SB0: e.tensor_scalar(ST[:, SB0 + 2:SB0 + 3], ST[:, SB0:SB0 + 1], 1.0 / D, None, ALU.mult))
            ve(lambda e, SB0=SB0: e.tensor_scalar(ST[:, SB0 + 3:SB0 + 4], ST[:, SB0 + 1:SB0 + 2], 1.0 / D, None, ALU.mult))
            ve(lambda e, SB0=SB0: e.tensor_tensor(ST[:, SB0 + 4:SB0 + 5], ST[:, SB0 + 2:SB0 + 3], ST[:, SB0 + 2:SB0 + 3], ALU.mult))
            ve(lambda e, SB0=SB0: e.tensor_tensor(ST[:, SB0 + 5:SB0 + 6], ST[:, SB0 + 3:SB0 + 4], ST[:, SB0 + 4:SB0 + 5], ALU.subtract))
            var_done2 = p.cnt["sVE"]
            p.wait("scalar", "sVE", var_done2)
            se(lambda e, SB0=SB0: e.activation(ST[:, SB0 + 6:SB0 + 7], ST[:, SB0 + 5:SB0 + 6], AF.Sqrt, bias=LN_EPS))
            p.wait("vector", "sSE")
            ve(lambda e, SB0=SB0: e.reciprocal(ST[:, SB0 + 7:SB0 + 8], ST[:, SB0 + 6:SB0 + 7]))
            ot_slot[pr].require("vector")
            ve(lambda e, i=i, o_t=o_t, SB0=SB0: e.tensor_scalar(
                o_t[:], acc[i][:], ST[:, SB0 + 2:SB0 + 3], ST[:, SB0 + 7:SB0 + 8], ALU.subtract, ALU.mult))
            acc_slot[i].consumed("sVE")
            ow_done = p.cnt["sVE"]
            p.wait("sync", "sVE", ow_done)
            p.dma("sync", lambda e, o_t=o_t, tc=tc: e.dma_start(
                out=out_c[tc * 128:(tc + 1) * 128, :], in_=o_t[:]), "sOUT")
            ot_slot[pr].consumed(total=("sOUT",))

        first = True
        for th in range(2):
            if first:
                phase_barrier(["sync", "scalar"])
            p.wait("sync", "sXPS")
            p.wait("sync", "sVE")
            p.wait("sync", "sSE")
            for i in range(TCM // 2):
                tc = th * (TCM // 2) + i
                acc_slot[i].require("sync")
                p.dma("sync", lambda e, i=i, tc=tc: e.dma_start(
                    out=acc[i][:], in_=xpscr[tc * 128:(tc + 1) * 128, :]), "sACB")
            acb_done = p.cnt["sACB"]
            if first:
                load_w2t(0)
                first = False
            for g in range(NG5):
                gg = th * NG5 + g
                buf = gg % 2
                if gg + 1 < 2 * NG5:
                    load_w2t(gg + 1)
                p.wait("scalar", "sIQS")
                iqg_slot.require("scalar")
                for i in range(GSZ):
                    hc = g * GSZ + i
                    p.dma("scalar", lambda e, i=i, hc=hc, th=th: e.dma_start(
                        out=iqg[i][:], in_=iqscr[hc][:, th * (MYT // 2):(th + 1) * (MYT // 2)]), "sIQL")
                iql_done = p.cnt["sIQL"]
                p.wait("tensor", "sW2T", w2t_sem[buf])
                p.wait("tensor", "sIQL", iql_done)
                p.wait("tensor", "sACB", acb_done)
                for i in range(TCM // 2):
                    banks = ps[0:4] if i % 2 == 0 else ps[4:8]
                    b5_slot[i % 2].require("tensor")
                    for j in range(GSZ):
                        for cs in range(4):
                            pe(lambda e, i=i, j=j, cs=cs, banks=banks, buf=buf: e.matmul(
                                banks[cs][:, 0:512], iqg[j][:, i * 128:(i + 1) * 128],
                                w2T52[buf][j][:, cs * 512:(cs + 1) * 512],
                                start=(j == 0), stop=(j == GSZ - 1)),
                                inc=(j == GSZ - 1 and cs == 3))
                    mm_done = p.cnt["sPE"]
                    p.wait("vector", "sPE", mm_done)
                    p.wait("vector", "sACB", acb_done)
                    acc_slot[i].require("vector")
                    for cs in range(4):
                        ve(lambda e, i=i, cs=cs, banks=banks: e.tensor_tensor(
                            acc[i][:, cs * 512:(cs + 1) * 512], acc[i][:, cs * 512:(cs + 1) * 512],
                            banks[cs][:, 0:512], ALU.add))
                    b5_slot[i % 2].consumed("sVE")
                    if g == NG5 - 1:
                        emit_ln2(th, i)
                iqg_slot.consumed("sPE")
                w2t_slot[buf].consumed("sPE")
        if DEBUG:
            p.barrier(["gpsimd"], ["sQKS", "sVS", "sYS", "sAPS", "sCC", "sXPS",
                                   "sXQS", "sTMS", "sOUT"])
            for name, (dst, src) in dbg.items():
                p.dma("gpsimd", lambda e, dst=dst, src=src: e.dma_start(out=dst, in_=src), "sDBG")
            p.wait("gpsimd", "sDBG")

        for eng in ALL_ENG:
            p.wait(eng, "sOUT")

        with nc.Block() as block:
            def mk(engname):
                def run(e):
                    for it in p.items:
                        if it[0] != engname:
                            continue
                        if it[1] == "wait":
                            e.wait_ge(sems[it[2]], it[3])
                        else:
                            instr = it[2](e)
                            if it[3] is not None:
                                instr.then_inc(sems[it[3]], it[4])
                return run
            block.sync(mk("sync"))
            block.scalar(mk("scalar"))
            block.vector(mk("vector"))
            block.tensor(mk("tensor"))
            block.gpsimd(mk("gpsimd"))

    return nc


def _host_consts(hg):
    inv_freq = ((np.float32(1.0) / np.float32(ROPE_BASE)) **
                (np.arange(0, HH, 2, dtype=np.float32) / np.float32(HH))).astype(np.float64)
    rotc = np.zeros((128, CQ), np.float32)
    rots = np.zeros((128, CQ), np.float32)
    for hl in range(NHL):
        hgl = hg * NHL + hl
        c = np.cos(np.float64(hgl) * inv_freq).astype(np.float32)
        s = np.sin(np.float64(hgl) * inv_freq).astype(np.float32)
        for half in range(2):
            base = hl * HD + half * HH
            rotc[:, base:base + 64] = c
            rotc[:, base + 64:base + 128] = c
            rots[:, base:base + 64] = s
            rots[:, base + 64:base + 128] = -s
    ki = np.arange(128)[:, None]
    qj = np.arange(512)[None, :]
    m4 = np.zeros((4, 128, 512), np.float32)
    for d in range(4):
        m4[d] = (qj >= ki + d * 128).astype(np.float32)
    mskv = np.zeros((2, 128, 1024), np.float32)
    mskv[0, :, 0:512] = m4[0]
    mskv[0, :, 512:1024] = m4[1]
    mskv[1, :, 0:512] = m4[2]
    mskv[1, :, 512:1024] = m4[3]
    return rotc, rots, mskv.astype(ml_dtypes.bfloat16)




def my_token_ids(hg):
    return np.concatenate([np.arange(c * 1024 + hg * 512, c * 1024 + (hg + 1) * 512)
                           for c in range(T // 1024)])


def make_in_maps(x, wqf, wkf, wvf, wof, w13f, w2f, nwf, lqkf):
    in_maps = []
    for c in range(8):
        b, hg = c // 2, c % 2
        rotcv, rotsv, mskv = _host_consts(hg)
        in_maps.append(dict(
            xb=x[b],
            xmy=np.ascontiguousarray(x[b, my_token_ids(hg)]),
            wq=np.ascontiguousarray(wqf[hg * CQ:(hg + 1) * CQ]),
            wk=np.ascontiguousarray(wkf[hg * CQ:(hg + 1) * CQ]),
            wv=np.ascontiguousarray(wvf[hg * CQ:(hg + 1) * CQ]),
            wo=np.ascontiguousarray(wof[:, hg * CQ:(hg + 1) * CQ]),
            w13=w13f, w2=w2f, nw_in=nwf, lqk=lqkf,
            rotc_in=rotcv, rots_in=rotsv, msk_in=mskv,
        ))
    return in_maps

def kernel(L0_attn_x, L0_attn_wq, L0_attn_wk, L0_attn_wv, L0_attn_wo,
           L0_attn_lq1, L0_attn_lk1, L0_attn_lq2, L0_attn_lk2,
           L0_ffn_w13, L0_ffn_w2, L0_ffn_norm_w):
    x = np.ascontiguousarray(np.asarray(L0_attn_x, dtype=np.float32))
    wqf = np.ascontiguousarray(np.asarray(L0_attn_wq, dtype=np.float32))
    wkf = np.ascontiguousarray(np.asarray(L0_attn_wk, dtype=np.float32))
    wvf = np.ascontiguousarray(np.asarray(L0_attn_wv, dtype=np.float32))
    wof = np.ascontiguousarray(np.asarray(L0_attn_wo, dtype=np.float32))
    w13f = np.ascontiguousarray(np.asarray(L0_ffn_w13, dtype=np.float32))
    w2f = np.ascontiguousarray(np.asarray(L0_ffn_w2, dtype=np.float32))
    nwf = np.ascontiguousarray(np.asarray(L0_ffn_norm_w, dtype=np.float32)).reshape(1, HF)
    lqkf = np.ascontiguousarray(np.stack([np.asarray(v, dtype=np.float32) for v in
                                          (L0_attn_lq1, L0_attn_lk1, L0_attn_lq2, L0_attn_lk2)]))

    nc = build_nc()
    in_maps = make_in_maps(x, wqf, wkf, wvf, wof, w13f, w2f, nwf, lqkf)
    res = run_bass_kernel_spmd(nc, in_maps, core_ids=list(range(8)))
    out = np.empty((B, T, D), dtype=np.float32)
    for c in range(8):
        b, hg = c // 2, c % 2
        out[b, my_token_ids(hg)] = res.results[c]["out_c"]
    return out


# version 15 (final)
# speedup vs baseline: 1.1072x; 1.0306x over previous
import math
from collections import Counter
from contextlib import ExitStack

import numpy as np
import ml_dtypes

import concourse.bass as bass
import concourse.mybir as mybir
from concourse.bass_utils import run_bass_kernel_spmd

F32 = mybir.dt.float32
BF16 = mybir.dt.bfloat16
I32 = mybir.dt.int32
AF = mybir.ActivationFunctionType
ALU = mybir.AluOpType
AX = mybir.AxisListType

B, T, D = 4, 4096, 2048
NHL = 4
HD, HH = 256, 128
CQ = NHL * HD
HF = 5632
H13 = 2 * HF
MYT = 2048
TCA = T // 128
TCM = MYT // 128
NDC = D // 128
NCC = CQ // 128
NPG = HF // 512
NHC = HF // 128
LAMBDA_INIT = 0.2
RMS_EPS = 1.1920929e-07
LN_EPS = 1e-05
ROPE_BASE = 10000.0
SCALE = 1.0 / math.sqrt(128.0)

DEBUG = False


class Prog:
    def __init__(self):
        self.items = []
        self.cnt = Counter()

    def op(self, eng, fn, inc=None, n=1):
        if inc is not None:
            self.cnt[inc] += n
        self.items.append((eng, "op", fn, inc, n))

    def dma(self, eng, fn, inc):
        self.op(eng, fn, inc, 16)

    def wait(self, eng, sem, val=None):
        v = self.cnt[sem] if val is None else val
        if v > 0:
            self.items.append((eng, "wait", sem, v))

    def barrier(self, engines, sems):
        for e in engines:
            for s in sems:
                self.wait(e, s)


def build_nc():
    nc = bass.Bass("TRN2", target_bir_lowering=False)

    xb = nc.dram_tensor("xb", [T, D], F32, kind="ExternalInput").ap()
    xmy = nc.dram_tensor("xmy", [MYT, D], F32, kind="ExternalInput").ap()
    wq = nc.dram_tensor("wq", [CQ, D], F32, kind="ExternalInput").ap()
    wk = nc.dram_tensor("wk", [CQ, D], F32, kind="ExternalInput").ap()
    wv = nc.dram_tensor("wv", [CQ, D], F32, kind="ExternalInput").ap()
    wo = nc.dram_tensor("wo", [D, CQ], F32, kind="ExternalInput").ap()
    w13 = nc.dram_tensor("w13", [H13, D], F32, kind="ExternalInput").ap()
    w2 = nc.dram_tensor("w2", [D, HF], F32, kind="ExternalInput").ap()
    nw_in = nc.dram_tensor("nw_in", [1, HF], F32, kind="ExternalInput").ap()
    lqk = nc.dram_tensor("lqk", [4, HD], F32, kind="ExternalInput").ap()
    rotc_in = nc.dram_tensor("rotc_in", [128, CQ], F32, kind="ExternalInput").ap()
    rots_in = nc.dram_tensor("rots_in", [128, CQ], F32, kind="ExternalInput").ap()
    msk_in = nc.dram_tensor("msk_in", [2, 128, 1024], BF16, kind="ExternalInput").ap()

    out_c = nc.dram_tensor("out_c", [MYT, D], F32, kind="ExternalOutput").ap()

    xscr = nc.dram_tensor("xscr", [T, D], BF16, kind="Internal").ap()
    wqscr = nc.dram_tensor("wqscr", [CQ, D], BF16, kind="Internal").ap()
    wkscr = nc.dram_tensor("wkscr", [CQ, D], BF16, kind="Internal").ap()
    wvscr = nc.dram_tensor("wvscr", [CQ, D], BF16, kind="Internal").ap()
    woscr = nc.dram_tensor("woscr", [D, CQ], BF16, kind="Internal").ap()
    w13scr = nc.dram_tensor("w13scr", [H13, D], BF16, kind="Internal").ap()
    w2scr = nc.dram_tensor("w2scr", [D, HF], BF16, kind="Internal").ap()
    qscr = nc.dram_tensor("qscr", [T, CQ], BF16, kind="Internal").ap()
    kscr = nc.dram_tensor("kscr", [T, CQ], BF16, kind="Internal").ap()
    vscr = nc.dram_tensor("vscr", [T, CQ], BF16, kind="Internal").ap()
    yscr = nc.dram_tensor("yscr", [T, CQ], BF16, kind="Internal").ap()
    apart = nc.dram_tensor("apart", [T, D], F32, kind="Internal").ap()
    ascat = nc.dram_tensor("ascat", [MYT, D], F32, kind="Internal").ap()
    xpscr = nc.dram_tensor("xpscr", [MYT, D], F32, kind="Internal").ap()
    xqscr = nc.dram_tensor("xqscr", [MYT, D], BF16, kind="Internal").ap()
    tmpscr = nc.dram_tensor("tmpscr", [MYT, HF], BF16, kind="Internal").ap()
    iqscr = nc.dram_tensor("iqscr", [NHC, 128, MYT], BF16, kind="Internal").ap()
    rsscr = nc.dram_tensor("rsscr", [2, MYT], F32, kind="Internal").ap()

    dbg = {}
    if DEBUG:
        for name, src in [("d_qscr", qscr), ("d_kscr", kscr), ("d_vscr", vscr),
                          ("d_yscr", yscr), ("d_apart", apart), ("d_ascat", ascat),
                          ("d_xpscr", xpscr), ("d_xqscr", xqscr), ("d_tmpscr", tmpscr),
                          ("d_iqscr", iqscr)]:
            dbg[name] = (nc.dram_tensor(name, list(src.shape), src.dtype,
                                        kind="ExternalOutput").ap(), src)

    stack = ExitStack()
    with stack:
        consts = stack.enter_context(nc.sbuf_tensor("consts", [128, 8], F32))
        CONST_VALS = [SCALE, 1.0 / 128.0, RMS_EPS, LN_EPS, 256.5]
        for i, v in enumerate(CONST_VALS):
            nc.const_aps.register(F32, v, consts[:, i:i + 1])
        onesf32 = stack.enter_context(nc.sbuf_tensor("onesf32", [1, 128], F32))
        lamt = stack.enter_context(nc.sbuf_tensor("lamt", [1, 1544], F32))
        neg08lam = stack.enter_context(nc.sbuf_tensor("neg08lam", [128, 1], F32))
        stat = stack.enter_context(nc.sbuf_tensor("stat", [128, 64], F32))
        fstat = stack.enter_context(nc.sbuf_tensor("fstat", [128, 2 * TCM * NPG + 2 * TCM], F32))
        trash_bf = stack.enter_context(nc.sbuf_tensor("trash_bf", [128, D], BF16))

        psT = [stack.enter_context(nc.psum_tensor(f"ps{i}", [128, 512], F32))
               for i in range(4)]
        psB = [stack.enter_context(nc.psum_tensor(f"psb{i}", [128, 1024], F32))
               for i in range(2)]
        ps = psT + [psB[0][:, 0:512], psB[0][:, 512:1024],
                    psB[1][:, 0:512], psB[1][:, 512:1024]]

        SBUF_TOP = 229376
        base0 = SBUF_TOP - nc.sbuf_bytes_remaining

        class Arena:
            def __init__(self, base, top):
                self.off = base
                self.top = top
                self.n = 0

            def alloc(self, shape, dtype):
                per = int(np.prod(shape[1:])) * mybir.dt.size(dtype)
                off = (self.off + 31) & ~31
                self.off = off + per
                assert self.off <= self.top, f"SBUF overflow: {self.off} > {self.top}"
                self.n += 1
                return nc.alloc_sbuf_tensor_at(f"ar{self.n}", shape, dtype, offset=off).ap()

            def mark(self):
                return self.off

            def reset(self, off):
                self.off = off

        ar = Arena(base0 + 32, SBUF_TOP - 64)
        m_pre = ar.mark()
        rotc = ar.alloc([128, CQ], F32)
        rots = ar.alloc([128, CQ], F32)
        msk = ar.alloc([128, 2 * 1024], BF16)

        sem_names = ["sXS", "sWQ", "sWK", "sWV", "sWO", "sW13", "sW2",
                     "sXT", "sWT", "sQKS", "sVS", "sKT", "sVG", "sYS", "sWOT",
                     "sAPS", "sCC", "sLD", "sXPS", "sXQS", "sXQT", "sW13T",
                     "sTMS", "sTMT", "sIQS", "sIQL", "sW2T", "sRS", "sACB",
                     "sOUT", "sPE", "sVE", "sSE", "sGE", "sIN", "sDBG", "sVG2", "sLD2", "sLD3", "sLD4", "sXS2"]
        sems = {n: stack.enter_context(nc.semaphore(n)) for n in sem_names}
        p = Prog()
        ALL_ENG = ["sync", "scalar", "vector", "tensor", "gpsimd"]
        ALLS = list(sem_names)

        def phase_barrier(engines=ALL_ENG):
            for e in engines:
                for sname in ALLS:
                    p.wait(e, sname)

        def ve(fn):
            p.wait("vector", "sVE")
            p.op("vector", fn, "sVE")

        def se(fn):
            p.wait("scalar", "sSE")
            p.op("scalar", fn, "sSE")

        def ge(fn):
            p.wait("gpsimd", "sGE")
            p.op("gpsimd", fn, "sGE")

        def pe(fn, inc=False):
            p.op("tensor", fn, "sPE" if inc else None)

        class Slot:
            def __init__(self):
                self.dep = []

            def require(self, eng):
                for sem, v in self.dep:
                    p.wait(eng, sem, p.cnt[sem] if v is None else v)

            def consumed(self, *sems, total=()):
                self.dep = [(x, p.cnt[x]) for x in sems] + [(x, None) for x in total]

        def tdma(fn, sem):
            p.dma("sync", fn, sem)
            p.wait("sync", sem)

        def tdma_nw(fn, sem):
            p.dma("sync", fn, sem)

        def tflush(sem):
            p.wait("sync", sem)

        for i, v in enumerate(CONST_VALS):
            ve(lambda e, i=i, v=v: e.memset(consts[:, i:i + 1], v))
        ve(lambda e: e.memset(onesf32[0:1, :], 1.0))
        p.dma("gpsimd", lambda e: e.dma_start(out=rotc[:], in_=rotc_in), "sIN")
        p.dma("gpsimd", lambda e: e.dma_start(out=rots[:], in_=rots_in), "sIN")
        p.dma("gpsimd", lambda e: e.dma_start(
            out=msk[:].rearrange("p (c j) -> p c j", j=1024),
            in_=msk_in.rearrange("c p j -> p c j")), "sIN")
        p.dma("gpsimd", lambda e: e.dma_start(
            out=lamt[0:1, 0:1024], in_=lqk.rearrange("(o a) j -> o (a j)", o=1)), "sIN")

        p.dma("gpsimd", lambda e: e.dma_start(out=xscr[:, 0:D // 2], in_=xb[:, 0:D // 2]), "sXS")
        p.dma("gpsimd", lambda e: e.dma_start(out=wqscr, in_=wq), "sWQ")
        p.dma("gpsimd", lambda e: e.dma_start(out=xscr[:, D // 2:D], in_=xb[:, D // 2:D]), "sXS2")
        p.dma("gpsimd", lambda e: e.dma_start(out=wkscr, in_=wk), "sWK")
        p.dma("gpsimd", lambda e: e.dma_start(out=wvscr, in_=wv), "sWV")

        p.wait("vector", "sIN")
        L = lamt
        ve(lambda e: e.tensor_tensor(L[0:1, 1024:1280], L[0:1, 0:256], L[0:1, 256:512], ALU.mult))
        ve(lambda e: e.tensor_tensor(L[0:1, 1280:1536], L[0:1, 512:768], L[0:1, 768:1024], ALU.mult))
        p.wait("vector", "sVE")
        ve(lambda e: e.reduce_sum(L[0:1, 1536:1537], L[0:1, 1024:1280], axis=AX.X))
        ve(lambda e: e.reduce_sum(L[0:1, 1537:1538], L[0:1, 1280:1536], axis=AX.X))
        p.wait("scalar", "sVE")
        se(lambda e: e.activation(L[0:1, 1538:1540], L[0:1, 1536:1538], AF.Exp))
        p.wait("vector", "sSE")
        ve(lambda e: e.tensor_tensor(L[0:1, 1540:1541], L[0:1, 1538:1539], L[0:1, 1539:1540], ALU.subtract))
        p.wait("vector", "sVE")
        ve(lambda e: e.tensor_scalar(L[0:1, 1541:1542], L[0:1, 1540:1541], LAMBDA_INIT, -0.8, ALU.add, ALU.mult))
        p.wait("tensor", "sVE")
        p.wait("tensor", "sIN")
        pe(lambda e: e.matmul(ps[0][:, 0:1], onesf32[0:1, :], L[0:1, 1541:1542], start=True, stop=True), inc=True)
        p.wait("vector", "sPE")
        ve(lambda e: e.tensor_copy(neg08lam[:], ps[0][:, 0:1]))

        m0 = ar.mark()
        xT = [ar.alloc([128, T], BF16) for _ in range(NDC)]
        wTg2 = [[ar.alloc([128, 512], BF16) for _ in range(NDC)] for _ in range(2)]
        stg2 = [[ar.alloc([128, 512], F32) for _ in range(4)] for _ in range(2)]
        stg_bf = [ar.alloc([128, 512], BF16) for _ in range(2)]
        seg42 = [ar.alloc([128, 8], F32) for _ in range(2)]

        p.wait("sync", "sXS")
        for dc in range(NDC // 2):
            tdma_nw(lambda e, dc=dc: e.dma_start_transpose(
                xT[dc][:], xscr[:, dc * 128:(dc + 1) * 128]), "sXT")
        p.wait("sync", "sXS2")
        for dc in range(NDC // 2, NDC):
            tdma_nw(lambda e, dc=dc: e.dma_start_transpose(
                xT[dc][:], xscr[:, dc * 128:(dc + 1) * 128]), "sXT")
        tflush("sXT")

        groups = []
        for wsem, wscrt, dst_scr, is_qk, dsem in [
                ("sWQ", wqscr, qscr, True, "sQKS"),
                ("sWK", wkscr, kscr, True, "sQKS"),
                ("sWV", wvscr, vscr, False, "sVS")]:
            for cg in range(CQ // 512):
                groups.append((wsem, wscrt, dst_scr, is_qk, dsem, cg))

        wtg_slot = [Slot(), Slot()]
        wtg_sem = [0, 0]
        pb_slot = [Slot() for _ in range(4)]
        pb_slot[0].dep = [("sVE", p.cnt["sVE"])]
        sq_slot = [Slot() for _ in range(2)]
        qn_slot = [Slot() for _ in range(2)]
        t12_slot = [Slot() for _ in range(2)]
        sb_slot = [Slot() for _ in range(2)]
        seg_slot = [Slot() for _ in range(2)]

        def load_group(gi):
            wsem, wscrt, _, _, _, cg = groups[gi]
            buf = gi % 2
            p.wait("sync", wsem, 16)
            wtg_slot[buf].require("sync")
            for dc in range(NDC):
                tdma_nw(lambda e, dc=dc, cg=cg, wscrt=wscrt, buf=buf: e.dma_start_transpose(
                    wTg2[buf][dc][:], wscrt[cg * 512:(cg + 1) * 512, dc * 128:(dc + 1) * 128]), "sWT")
            tflush("sWT")
            wtg_sem[buf] = p.cnt["sWT"]

        p.wait("gpsimd", "sXT")
        p.dma("gpsimd", lambda e: e.dma_start(out=woscr, in_=wo), "sWO")
        load_group(0)
        for gi, (wsem, wscrt, dst_scr, is_qk, dsem, cg) in enumerate(groups):
            buf = gi % 2
            if gi + 1 < len(groups):
                load_group(gi + 1)
            wTg = wTg2[buf]
            p.wait("tensor", "sWT", wtg_sem[buf])
            if gi == 0:
                p.wait("tensor", "sXT")
            for tc in range(TCA):
                bank = ps[tc % 4]
                pb_slot[tc % 4].require("tensor")
                for dc in range(NDC):
                    pe(lambda e, dc=dc, tc=tc, bank=bank, wTg=wTg: e.matmul(
                        bank[:, 0:512], xT[dc][:, tc * 128:(tc + 1) * 128],
                        wTg[dc][:, 0:512], start=(dc == 0), stop=(dc == NDC - 1)),
                        inc=(dc == NDC - 1))
                mm_done = p.cnt["sPE"]
                if not is_qk:
                    sb = stg_bf[tc % 2]
                    p.wait("scalar", "sPE", mm_done)
                    sb_slot[tc % 2].require("scalar")
                    se(lambda e, bank=bank, sb=sb: e.copy(sb[:], bank[:, 0:512]))
                    pb_slot[tc % 4].consumed("sSE")
                    p.wait("gpsimd", "sSE")
                    p.dma("gpsimd", lambda e, sb=sb, tc=tc, cg=cg, dst_scr=dst_scr: e.dma_start(
                        out=dst_scr[tc * 128:(tc + 1) * 128, cg * 512:(cg + 1) * 512],
                        in_=sb[:]), dsem)
                    sb_slot[tc % 2].consumed(total=(dsem,))
                else:
                    sq_t, qn_t, t1_t, t2_t = stg2[tc % 2]
                    seg4 = seg42[tc % 2]
                    sb = stg_bf[tc % 2]
                    p.wait("scalar", "sPE", mm_done)
                    sq_slot[tc % 2].require("scalar")
                    se(lambda e, bank=bank, sq_t=sq_t: e.activation(sq_t[:], bank[:, 0:512], AF.Square))
                    p.wait("vector", "sSE")
                    seg_slot[tc % 2].require("vector")
                    ve(lambda e, sq_t=sq_t, seg4=seg4: e.tensor_reduce(
                        seg4[:, 0:4].rearrange("p (a b) -> p a b", b=1),
                        sq_t[:].rearrange("p (a b) -> p a b", a=4), axis=AX.X, op=ALU.add))
                    sq_slot[tc % 2].consumed("sVE")
                    p.wait("scalar", "sVE")
                    se(lambda e, seg4=seg4: e.activation(seg4[:, 4:8], seg4[:, 0:4], AF.Sqrt,
                                                         bias=RMS_EPS, scale=1.0 / 128.0))
                    p.wait("vector", "sSE")
                    ve(lambda e, seg4=seg4: e.reciprocal(seg4[:, 0:4], seg4[:, 4:8]))
                    qn_slot[tc % 2].require("vector")
                    for sg in range(4):
                        ve(lambda e, bank=bank, qn_t=qn_t, sg=sg, seg4=seg4: e.tensor_scalar(
                            qn_t[:, sg * 128:(sg + 1) * 128], bank[:, sg * 128:(sg + 1) * 128],
                            seg4[:, sg:sg + 1], None, ALU.mult))
                    pb_slot[tc % 4].consumed("sSE", "sVE")
                    seg_slot[tc % 2].consumed("sVE")
                    t12_slot[tc % 2].require("vector")
                    ve(lambda e, qn_t=qn_t, t1_t=t1_t, cg=cg: e.tensor_tensor(
                        t1_t[:], qn_t[:], rotc[:, cg * 512:(cg + 1) * 512], ALU.mult))
                    qn_done = p.cnt["sVE"] - 1
                    p.wait("gpsimd", "sVE", qn_done)
                    t12_slot[tc % 2].require("gpsimd")
                    ge(lambda e, qn_t=qn_t, t2_t=t2_t, cg=cg: e.tensor_tensor(
                        t2_t[:].rearrange("p (a b c) -> p a b c", a=4, c=64),
                        qn_t[:].rearrange("p (a b c) -> p a b c", a=4, c=64)[:, :, ::-1, :],
                        rots[:, cg * 512:(cg + 1) * 512].rearrange("p (a b c) -> p a b c", a=4, c=64),
                        ALU.mult))
                    qn_slot[tc % 2].consumed("sVE", "sGE")
                    sb_slot[tc % 2].require("gpsimd")
                    t1_done = p.cnt["sVE"]
                    p.wait("gpsimd", "sVE", t1_done)
                    ge(lambda e, t1_t=t1_t, t2_t=t2_t, sb=sb: e.tensor_tensor(
                        sb[:], t1_t[:], t2_t[:], ALU.add))
                    t12_slot[tc % 2].consumed("sGE")
                    p.wait("gpsimd", "sGE")
                    p.dma("gpsimd", lambda e, sb=sb, tc=tc, cg=cg, dst_scr=dst_scr: e.dma_start(
                        out=dst_scr[tc * 128:(tc + 1) * 128, cg * 512:(cg + 1) * 512],
                        in_=sb[:]), dsem)
                    sb_slot[tc % 2].consumed(total=(dsem,))
            wtg_slot[buf].consumed("sPE")

        ar.reset(m0)
        KQ2 = [[ar.alloc([128, T], BF16) for _ in range(4)] for _ in range(2)]
        Vt2 = [ar.alloc([128, TCA * 257], BF16) for _ in range(2)]
        Pt = [ar.alloc([128, 1024], BF16) for _ in range(2)]
        Ptm = [ar.alloc([128, 1024], BF16) for _ in range(2)]
        oe2 = [ar.alloc([128, 8 * 257], F32) for _ in range(2)]
        eps_t = ar.alloc([128, 280], F32)
        yst = [ar.alloc([128, 4 * 256], BF16) for _ in range(2)]

        phase_barrier()
        for vb in range(2):
            for c in range(TCA):
                ve(lambda e, vb=vb, c=c: e.memset(Vt2[vb][:, c * 257 + 256:c * 257 + 257], 1.0))
        p.wait("gpsimd", "sIN")
        p.wait("gpsimd", "sVE")

        psS2 = [psB[0], psB[1]]
        psO = [psT[0], psT[1], psT[2], psT[3]]
        kq_slot = [Slot(), Slot()]
        vt_slot = [Slot(), Slot()]
        kq_sem = [0, 0]
        vt_sem = [0, 0]
        psS_slot = [Slot(), Slot()]
        psS_slot[0].dep = [("sVE", p.cnt["sVE"])]
        pt_slot = [Slot(), Slot()]
        ptm_slot = [Slot(), Slot()]
        psO_slot = [Slot() for _ in range(4)]
        oe_slot = [Slot(), Slot()]
        ys_slot = [Slot(), Slot()]
        ep_jobs = []

        def emit_epilogue(h, qt, buf, sse_evac):
            oe = oe2[buf]
            c0 = h * HD
            p.wait("vector", "sSE", sse_evac)
            ve(lambda e, oe=oe: e.tensor_copy(
                eps_t[:, 0:8], oe[:, 256::257]))
            ve(lambda e: e.reciprocal(eps_t[:, 8:16], eps_t[:, 0:8]))
            ve(lambda e: e.tensor_scalar(eps_t[:, 8:12], eps_t[:, 8:12], 0.8, None, ALU.mult))
            ve(lambda e: e.tensor_scalar(
                eps_t[:, 12:16], eps_t[:, 12:16], neg08lam[:, 0:1], None, ALU.mult))
            ys = yst[qt % 2]
            ys_slot[qt % 2].require("vector")
            for qs in range(4):
                ve(lambda e, qs=qs, oe=oe: e.tensor_scalar(
                    eps_t[:, 16:272], oe[:, qs * 257:qs * 257 + 256], eps_t[:, 8 + qs:9 + qs], None, ALU.mult))
                ve(lambda e, qs=qs, ys=ys, oe=oe: e.scalar_tensor_tensor(
                    ys[:, qs * 256:(qs + 1) * 256], oe[:, (4 + qs) * 257:(4 + qs) * 257 + 256],
                    eps_t[:, 12 + qs:13 + qs], eps_t[:, 16:272], ALU.mult, ALU.add))
            p.wait("gpsimd", "sVE")
            p.dma("gpsimd", lambda e, ys=ys, qt=qt, c0=c0: e.dma_start(
                out=yscr[qt * 512:(qt + 1) * 512, c0:c0 + 256].rearrange("(a p) j -> p a j", p=128),
                in_=ys[:].rearrange("p (a j) -> p a j", j=256)), "sYS")
            ys_slot[qt % 2].consumed(total=("sYS",))
            oe_slot[buf].consumed("sVE")

        def load_head(h):
            hb = h % 2
            c0 = h * HD
            p.wait("sync", "sQKS")
            kq_slot[hb].require("sync")
            for j, scr_c0 in enumerate([(kscr, c0), (kscr, c0 + 128), (qscr, c0), (qscr, c0 + 128)]):
                scr, cc0 = scr_c0
                tdma_nw(lambda e, j=j, hb=hb, scr=scr, cc0=cc0: e.dma_start_transpose(
                    KQ2[hb][j][:], scr[:, cc0:cc0 + 128]), "sKT")
            tflush("sKT")
            kq_sem[hb] = p.cnt["sKT"]
            p.wait("gpsimd", "sVS")
            vt_slot[hb].require("gpsimd")
            vsem = "sVG" if hb == 0 else "sVG2"
            p.dma("gpsimd", lambda e, c0=c0, hb=hb: e.dma_start(
                out=Vt2[hb][:].rearrange("p (c j) -> p c j", j=257)[:, :, 0:256],
                in_=vscr[:, c0:c0 + 256].rearrange("(c p) j -> p c j", p=128)), vsem)
            vt_sem[hb] = (vsem, p.cnt[vsem])

        def emit_Shalf(Kh, Qh, m, half, qt):
            sb = psS2[m % 2]
            if half == 0:
                psS_slot[m % 2].require("tensor")
            pe(lambda e, Kh=Kh, Qh=Qh, m=m, half=half, qt=qt, sb=sb: e.matmul(
                sb[:, half * 512:(half + 1) * 512],
                Kh[:, (2 * m + half) * 128:(2 * m + half + 1) * 128],
                Qh[:, qt * 512:(qt + 1) * 512], start=True, stop=True), inc=True)
            return p.cnt["sPE"]

        load_head(0)
        for h in range(NHL):
            hb = h % 2
            if h + 1 < NHL:
                load_head(h + 1)
            p.wait("gpsimd", "sKT")
            CH13 = H13 // NHL
            p.dma("gpsimd", lambda e, h=h: e.dma_start(
                out=w13scr[h * CH13:(h + 1) * CH13, :], in_=w13[h * CH13:(h + 1) * CH13, :]), "sW13")
            K1, K2, Q1, Q2 = KQ2[hb]
            Vt = Vt2[hb]
            p.wait("tensor", "sKT", kq_sem[hb])
            p.wait("tensor", vt_sem[hb][0], vt_sem[hb][1])
            if h < 2:
                p.wait("tensor", "sVE")
            for qt in range(8):
                depth = 1 if h == NHL - 1 and qt >= 6 else 2
                while len(ep_jobs) >= depth:
                    emit_epilogue(*ep_jobs.pop(0))
                mmax = (qt * 4 + 4) // 2
                for at in range(2):
                    Kh = K1 if at == 0 else K2
                    Qh = Q1 if at == 0 else Q2
                    s_done = [emit_Shalf(Kh, Qh, 0, 0, qt), emit_Shalf(Kh, Qh, 0, 1, qt)]
                    for m in range(mmax):
                        Pk = Pt[m % 2]
                        sb = psS2[m % 2]
                        exp_done = [0, 0]
                        for half in range(2):
                            p.wait("scalar", "sPE", s_done[half])
                            if half == 0:
                                pt_slot[m % 2].require("scalar")
                            se(lambda e, sb=sb, Pk=Pk, half=half: e.activation(
                                Pk[:, half * 512:(half + 1) * 512],
                                sb[:, half * 512:(half + 1) * 512], AF.Exp, scale=SCALE))
                            exp_done[half] = p.cnt["sSE"]
                        psS_slot[m % 2].consumed("sSE")
                        diag = m >= mmax - 2
                        dv2 = m - (mmax - 2) if diag else 0
                        msk_done = [0, 0]
                        if diag:
                            for half in range(2):
                                p.wait("vector", "sSE", exp_done[half])
                                if half == 0:
                                    ptm_slot[m % 2].require("vector")
                                ve(lambda e, Pk=Pk, dv2=dv2, m=m, half=half: e.tensor_tensor(
                                    Ptm[m % 2][:, half * 512:(half + 1) * 512],
                                    Pk[:, half * 512:(half + 1) * 512],
                                    msk[:, dv2 * 1024 + half * 512:dv2 * 1024 + (half + 1) * 512],
                                    ALU.mult))
                                msk_done[half] = p.cnt["sVE"]
                            lhsP = Ptm[m % 2]
                        else:
                            lhsP = Pk
                        for half in range(2):
                            kc = 2 * m + half
                            if half == 0 and m + 1 < mmax:
                                ns0 = emit_Shalf(Kh, Qh, m + 1, 0, qt)
                            if diag:
                                p.wait("tensor", "sVE", msk_done[half])
                            else:
                                p.wait("tensor", "sSE", exp_done[half])
                            if m == 0 and half == 0:
                                for qs in range(4):
                                    psO_slot[qs].require("tensor")
                            for qs in range(4):
                                pe(lambda e, lhsP=lhsP, qs=qs, kc=kc, half=half, mmax=mmax, Vt=Vt: e.matmul(
                                    psO[qs][:, 0:257],
                                    lhsP[:, half * 512 + qs * 128:half * 512 + (qs + 1) * 128],
                                    Vt[:, kc * 257:(kc + 1) * 257],
                                    start=(kc == 0), stop=(kc == 2 * mmax - 1)),
                                    inc=(qs == 3 and half == 1))
                            if half == 0 and m + 1 < mmax:
                                ns1 = emit_Shalf(Kh, Qh, m + 1, 1, qt)
                        if m + 1 < mmax:
                            s_done = [ns0, ns1]
                        if diag:
                            ptm_slot[m % 2].consumed("sPE")
                            pt_slot[m % 2].consumed("sPE", "sVE")
                        else:
                            pt_slot[m % 2].consumed("sPE")
                    pv_done = p.cnt["sPE"]
                    oe = oe2[qt % 2]
                    base = 0 if at == 0 else 4
                    p.wait("vector", "sPE", pv_done)
                    p.wait("scalar", "sPE", pv_done)
                    oe_slot[qt % 2].require("vector")
                    oe_slot[qt % 2].require("scalar")
                    for qs in range(4):
                        if qs % 2 == 0:
                            ve(lambda e, qs=qs, oe=oe, base=base: e.tensor_copy(
                                oe[:, (base + qs) * 257:(base + qs + 1) * 257], psO[qs][:, 0:257]))
                        else:
                            se(lambda e, qs=qs, oe=oe, base=base: e.copy(
                                oe[:, (base + qs) * 257:(base + qs + 1) * 257], psO[qs][:, 0:257]))
                    for qs in range(4):
                        psO_slot[qs].dep = [("sVE", p.cnt["sVE"]), ("sSE", p.cnt["sSE"])]
                ep_jobs.append((h, qt, qt % 2, p.cnt["sSE"]))
            kq_slot[hb].consumed("sPE")
            vt_slot[hb].consumed("sPE")
        while ep_jobs:
            emit_epilogue(*ep_jobs.pop(0))

        ar.reset(m_pre)
        yT = [ar.alloc([128, T], BF16) for _ in range(NCC)]
        woT = [ar.alloc([128, D], BF16) for _ in range(NCC)]
        apst = [ar.alloc([128, D], F32) for _ in range(2)]

        phase_barrier()
        p.dma("gpsimd", lambda e: e.dma_start(out=w2scr[:, 0:HF // 2], in_=w2[:, 0:HF // 2]), "sW2")
        p.dma("gpsimd", lambda e: e.dma_start(out=w2scr[:, HF // 2:HF], in_=w2[:, HF // 2:HF]), "sW2")
        p.wait("sync", "sYS")
        p.wait("sync", "sWO", 16)
        for cc in range(NCC):
            tdma_nw(lambda e, cc=cc: e.dma_start_transpose(
                yT[cc][:], yscr[:, cc * 128:(cc + 1) * 128]), "sWOT")
            tdma_nw(lambda e, cc=cc: e.dma_start_transpose(
                woT[cc][:], woscr[:, cc * 128:(cc + 1) * 128]), "sWOT")
        tflush("sWOT")
        p.wait("tensor", "sWOT")
        wo_bank = [Slot(), Slot()]
        ap_slot = [Slot(), Slot()]
        for tc in range(TCA):
            banks = ps[0:4] if tc % 2 == 0 else ps[4:8]
            a_t = apst[tc % 2]
            wo_bank[tc % 2].require("tensor")
            for cc in range(NCC):
                for cs in range(4):
                    pe(lambda e, cc=cc, cs=cs, tc=tc, banks=banks: e.matmul(
                        banks[cs][:, 0:512], yT[cc][:, tc * 128:(tc + 1) * 128],
                        woT[cc][:, cs * 512:(cs + 1) * 512],
                        start=(cc == 0), stop=(cc == NCC - 1)),
                        inc=(cc == NCC - 1 and cs == 3))
            mm_done = p.cnt["sPE"]
            p.wait("vector", "sPE", mm_done)
            p.wait("scalar", "sPE", mm_done)
            ap_slot[tc % 2].require("vector")
            ap_slot[tc % 2].require("scalar")
            for cs in range(4):
                if cs % 2 == 0:
                    ve(lambda e, cs=cs, a_t=a_t, banks=banks: e.tensor_copy(
                        a_t[:, cs * 512:(cs + 1) * 512], banks[cs][:, 0:512]))
                else:
                    se(lambda e, cs=cs, a_t=a_t, banks=banks: e.copy(
                        a_t[:, cs * 512:(cs + 1) * 512], banks[cs][:, 0:512]))
            wo_bank[tc % 2].consumed("sVE", "sSE")
            p.wait("sync", "sVE")
            p.wait("sync", "sSE")
            p.dma("sync", lambda e, a_t=a_t, tc=tc: e.dma_start(
                out=apart[tc * 128:(tc + 1) * 128, :], in_=a_t[:]), "sAPS")
            ap_slot[tc % 2].consumed(total=("sAPS",))
            if tc % 8 == 7:
                ch = tc // 8
                p.wait("gpsimd", "sAPS")
                p.op("gpsimd", lambda e, ch=ch: e.collective_compute(
                    "ReduceScatter", ALU.add,
                    replica_groups=[[0, 1], [2, 3], [4, 5], [6, 7]],
                    ins=[apart[ch * 1024:(ch + 1) * 1024, :].opt()],
                    outs=[ascat[ch * 512:(ch + 1) * 512, :].opt()]), "sCC", 1)

        _sCC_save = p.cnt["sCC"]
        p.cnt["sCC"] = 0
        ar.reset(m_pre)
        NPR = 4
        r0t = [ar.alloc([128, D], F32) for _ in range(NPR)]
        xrt = [ar.alloc([128, D], F32) for _ in range(NPR)]
        xpt = [ar.alloc([128, D], F32) for _ in range(NPR)]
        tqt2 = [ar.alloc([128, D], F32) for _ in range(NPR)]
        zqt2 = [ar.alloc([128, D], I32) for _ in range(NPR)]
        xqt = [ar.alloc([128, D], BF16) for _ in range(NPR)]

        phase_barrier()
        p.cnt["sCC"] = _sCC_save
        ST = stat
        r_slot = [Slot() for _ in range(4)]
        xp_slot = [Slot() for _ in range(4)]
        xq_slot = [Slot() for _ in range(4)]
        for tc in range(TCM):
            pr = tc % 4
            SB0 = 12 * pr
            r_t, x_t, xp_t, xq_t = r0t[pr], xrt[pr], xpt[pr], xqt[pr]
            tq_t, zq_t = tqt2[pr], zqt2[pr]
            r_slot[pr].require("sync")
            p.wait("sync", "sCC", tc // 4 + 1)
            lsem = ["sLD", "sLD2", "sLD3", "sLD4"][pr]
            p.dma("sync", lambda e, r_t=r_t, tc=tc: e.dma_start(
                out=r_t[:], in_=ascat[tc * 128:(tc + 1) * 128, :]), lsem)
            p.dma("sync", lambda e, x_t=x_t, tc=tc: e.dma_start(
                out=x_t[:], in_=xmy[tc * 128:(tc + 1) * 128, :]), lsem)
            ld_done = p.cnt[lsem]
            p.wait("vector", lsem, ld_done)
            ve(lambda e, r_t=r_t, x_t=x_t: e.tensor_tensor(r_t[:], r_t[:], x_t[:], ALU.add))
            radd_done = p.cnt["sVE"]
            p.wait("scalar", "sVE", radd_done)
            se(lambda e, r_t=r_t, SB0=SB0: e.activation(trash_bf[:], r_t[:], AF.Copy, accum_out=ST[:, SB0:SB0 + 1]))
            se(lambda e, r_t=r_t, SB0=SB0: e.activation(trash_bf[:], r_t[:], AF.Square, accum_out=ST[:, SB0 + 1:SB0 + 2]))
            acc_done = p.cnt["sSE"]
            p.wait("vector", "sSE", acc_done)
            ve(lambda e, SB0=SB0: e.tensor_scalar(ST[:, SB0 + 2:SB0 + 3], ST[:, SB0:SB0 + 1], 1.0 / D, None, ALU.mult))
            ve(lambda e, SB0=SB0: e.tensor_scalar(ST[:, SB0 + 3:SB0 + 4], ST[:, SB0 + 1:SB0 + 2], 1.0 / D, None, ALU.mult))
            ve(lambda e, SB0=SB0: e.tensor_tensor(ST[:, SB0 + 4:SB0 + 5], ST[:, SB0 + 2:SB0 + 3], ST[:, SB0 + 2:SB0 + 3], ALU.mult))
            ve(lambda e, SB0=SB0: e.tensor_tensor(ST[:, SB0 + 5:SB0 + 6], ST[:, SB0 + 3:SB0 + 4], ST[:, SB0 + 4:SB0 + 5], ALU.subtract))
            var_done = p.cnt["sVE"]
            p.wait("scalar", "sVE", var_done)
            se(lambda e, SB0=SB0: e.activation(ST[:, SB0 + 6:SB0 + 7], ST[:, SB0 + 5:SB0 + 6], AF.Sqrt, bias=LN_EPS))
            p.wait("vector", "sSE")
            ve(lambda e, SB0=SB0: e.reciprocal(ST[:, SB0 + 7:SB0 + 8], ST[:, SB0 + 6:SB0 + 7]))
            xp_slot[pr].require("vector")
            ve(lambda e, r_t=r_t, xp_t=xp_t, SB0=SB0: e.tensor_scalar(
                xp_t[:], r_t[:], ST[:, SB0 + 2:SB0 + 3], ST[:, SB0 + 7:SB0 + 8], ALU.subtract, ALU.mult))
            r_slot[pr].consumed("sVE")
            xpw_done = p.cnt["sVE"]
            p.wait("gpsimd", "sVE", xpw_done)
            p.dma("gpsimd", lambda e, xp_t=xp_t, tc=tc: e.dma_start(
                out=xpscr[tc * 128:(tc + 1) * 128, :], in_=xp_t[:]), "sXPS")
            ve(lambda e, xp_t=xp_t, SB0=SB0: e.tensor_reduce(
                ST[:, SB0 + 8:SB0 + 9], xp_t[:], axis=AX.X, op=ALU.max, apply_absolute_value=True))
            ve(lambda e, SB0=SB0: e.tensor_scalar(ST[:, SB0 + 9:SB0 + 10], ST[:, SB0 + 8:SB0 + 9], 1e-5, 1.0 / 127.0, ALU.max, ALU.mult))
            ve(lambda e, SB0=SB0: e.reciprocal(ST[:, SB0 + 10:SB0 + 11], ST[:, SB0 + 9:SB0 + 10]))
            ve(lambda e, xp_t=xp_t, tq_t=tq_t, SB0=SB0: e.tensor_scalar(
                tq_t[:], xp_t[:], ST[:, SB0 + 10:SB0 + 11], None, ALU.mult))
            ve(lambda e, tq_t=tq_t, zq_t=zq_t: e.tensor_copy(zq_t[:], tq_t[:]))
            xp_slot[pr].consumed("sVE", total=("sXPS",))
            xq_slot[pr].require("vector")
            ve(lambda e, xq_t=xq_t, zq_t=zq_t, SB0=SB0: e.tensor_scalar(
                xq_t[:], zq_t[:], ST[:, SB0 + 9:SB0 + 10], None, ALU.mult))
            xqw_done = p.cnt["sVE"]
            p.wait("gpsimd", "sVE", xqw_done)
            p.dma("gpsimd", lambda e, xq_t=xq_t, tc=tc: e.dma_start(
                out=xqscr[tc * 128:(tc + 1) * 128, :], in_=xq_t[:]), "sXQS")
            xq_slot[pr].consumed(total=("sXQS",))

        ar.reset(m_pre)
        xqT = [ar.alloc([128, MYT], BF16) for _ in range(NDC)]
        w13g2 = [[ar.alloc([128, 512], BF16) for _ in range(2 * NDC)] for _ in range(2)]
        g42 = [[ar.alloc([128, 512], F32) for _ in range(3)] for _ in range(2)]
        tmp_bf = [ar.alloc([128, 512], BF16) for _ in range(2)]
        nwrow = ar.alloc([1, HF], F32)
        nwb = ar.alloc([128, HF], BF16)

        phase_barrier()
        p.wait("sync", "sXQS")
        for dc in range(NDC):
            tdma_nw(lambda e, dc=dc: e.dma_start_transpose(
                xqT[dc][:], xqscr[:, dc * 128:(dc + 1) * 128]), "sXQT")
        tflush("sXQT")
        p.dma("sync", lambda e: e.dma_start(out=nwrow[0:1, :], in_=nw_in), "sLD")
        p.wait("tensor", "sLD")
        for j in range(NPG):
            p.wait("tensor", "sVE")
            pe(lambda e, j=j: e.matmul(ps[6][:, 0:512], onesf32[0:1, :],
                                       nwrow[0:1, j * 512:(j + 1) * 512], start=True, stop=True), inc=True)
            p.wait("vector", "sPE")
            ve(lambda e, j=j: e.tensor_copy(nwb[:, j * 512:(j + 1) * 512], ps[6][:, 0:512]))

        FS = fstat
        nwb_done_sve = None
        w13_slot = [Slot(), Slot()]
        w13_sem = [0, 0]
        pA_slot = [Slot() for _ in range(4)]
        pB_slot = [Slot() for _ in range(4)]
        pB_slot[2].dep = [("sVE", p.cnt["sVE"])]
        gs_slot = [Slot(), Slot()]
        tb_slot = [Slot(), Slot()]

        def load_pg(pg):
            buf = pg % 2
            p.wait("sync", "sW13")
            w13_slot[buf].require("sync")
            for dc in range(NDC):
                tdma_nw(lambda e, dc=dc, pg=pg, buf=buf: e.dma_start_transpose(
                    w13g2[buf][dc][:], w13scr[pg * 512:(pg + 1) * 512, dc * 128:(dc + 1) * 128]), "sW13T")
                tdma_nw(lambda e, dc=dc, pg=pg, buf=buf: e.dma_start_transpose(
                    w13g2[buf][NDC + dc][:], w13scr[HF + pg * 512:HF + (pg + 1) * 512, dc * 128:(dc + 1) * 128]), "sW13T")
            tflush("sW13T")
            w13_sem[buf] = p.cnt["sW13T"]

        load_pg(0)
        for pg in range(NPG):
            buf = pg % 2
            if pg + 1 < NPG:
                load_pg(pg + 1)
            w1g = w13g2[buf][0:NDC]
            w3g = w13g2[buf][NDC:2 * NDC]
            p.wait("tensor", "sW13T", w13_sem[buf])
            if pg == 0:
                p.wait("tensor", "sXQT")
            for tc in range(TCM):
                bA = ps[(tc % 4)]
                bB = ps[4 + (tc % 4)]
                pA_slot[tc % 4].require("tensor")
                for dc in range(NDC):
                    pe(lambda e, dc=dc, tc=tc, bA=bA, w1g=w1g: e.matmul(
                        bA[:, 0:512], xqT[dc][:, tc * 128:(tc + 1) * 128], w1g[dc][:],
                        start=(dc == 0), stop=(dc == NDC - 1)), inc=(dc == NDC - 1))
                pB_slot[tc % 4].require("tensor")
                for dc in range(NDC):
                    pe(lambda e, dc=dc, tc=tc, bB=bB, w3g=w3g: e.matmul(
                        bB[:, 0:512], xqT[dc][:, tc * 128:(tc + 1) * 128], w3g[dc][:],
                        start=(dc == 0), stop=(dc == NDC - 1)), inc=(dc == NDC - 1))
                mm_done = p.cnt["sPE"]
                r1_t, rr_t, g_t = g42[tc % 2]
                tb = tmp_bf[tc % 2]
                p.wait("scalar", "sPE", mm_done)
                gs_slot[tc % 2].require("scalar")
                se(lambda e, bA=bA, r1_t=r1_t: e.activation(r1_t[:], bA[:, 0:512], AF.Relu))
                pA_slot[tc % 4].consumed("sSE")
                se(lambda e, r1_t=r1_t, rr_t=rr_t: e.activation(rr_t[:], r1_t[:], AF.Square))
                sq_done = p.cnt["sSE"]
                p.wait("vector", "sSE", sq_done)
                ve(lambda e, rr_t=rr_t, g_t=g_t, bB=bB: e.tensor_tensor(
                    g_t[:], rr_t[:], bB[:, 0:512], ALU.mult))
                pB_slot[tc % 4].consumed("sVE")
                g_done = p.cnt["sVE"]
                p.wait("scalar", "sVE", g_done)
                se(lambda e, g_t=g_t, pg=pg, tc=tc: e.activation(
                    trash_bf[:, 0:512], g_t[:], AF.Square,
                    accum_out=FS[:, (pg * TCM + tc):(pg * TCM + tc) + 1]))
                p.wait("gpsimd", "sVE", g_done)
                tb_slot[tc % 2].require("gpsimd")
                ge(lambda e, g_t=g_t, tb=tb, pg=pg: e.tensor_tensor(
                    tb[:], g_t[:], nwb[:, pg * 512:(pg + 1) * 512], ALU.mult))
                gs_slot[tc % 2].consumed("sSE", "sGE")
                tt_done = p.cnt["sGE"]
                p.wait("vector", "sGE", tt_done)
                ve(lambda e, tb=tb, pg=pg, tc=tc: e.tensor_reduce(
                    FS[:, (NPG * TCM + pg * TCM + tc):(NPG * TCM + pg * TCM + tc) + 1],
                    tb[:], axis=AX.X, op=ALU.max, apply_absolute_value=True))
                p.wait("gpsimd", "sGE", tt_done)
                p.dma("gpsimd", lambda e, tb=tb, tc=tc, pg=pg: e.dma_start(
                    out=tmpscr[tc * 128:(tc + 1) * 128, pg * 512:(pg + 1) * 512],
                    in_=tb[:]), "sTMS")
                tb_slot[tc % 2].consumed("sVE", total=("sTMS",))
            w13_slot[buf].consumed("sPE")

        off5 = 2 * NPG * TCM
        p.wait("vector", "sSE")
        p.wait("vector", "sGE")
        for tc in range(TCM):
            ve(lambda e, tc=tc: e.tensor_reduce(
                FS[:, off5 + tc:off5 + tc + 1],
                FS[:, tc:NPG * TCM + tc:TCM], axis=AX.X, op=ALU.add))
            ve(lambda e, tc=tc: e.tensor_reduce(
                FS[:, off5 + TCM + tc:off5 + TCM + tc + 1],
                FS[:, NPG * TCM + tc:2 * NPG * TCM + tc:TCM], axis=AX.X, op=ALU.max))
        p.wait("scalar", "sVE")
        se(lambda e: e.activation(ST[:, 16:32], FS[:, off5:off5 + TCM], AF.Sqrt,
                                  bias=LN_EPS, scale=1.0 / HF))
        p.wait("vector", "sSE")
        ve(lambda e: e.reciprocal(ST[:, 16:32], ST[:, 16:32]))
        ve(lambda e: e.tensor_tensor(ST[:, 32:48], FS[:, off5 + TCM:off5 + 2 * TCM], ST[:, 16:32], ALU.mult))
        ve(lambda e: e.tensor_scalar(ST[:, 32:48], ST[:, 32:48], 1e-5, 1.0 / 127.0, ALU.max, ALU.mult))
        ve(lambda e: e.reciprocal(ST[:, 48:64], ST[:, 32:48]))
        ve(lambda e: e.tensor_tensor(ST[:, 16:32], ST[:, 16:32], ST[:, 48:64], ALU.mult))
        rsv_done = p.cnt["sVE"]
        p.wait("gpsimd", "sVE", rsv_done)
        for tc in range(TCM):
            p.dma("gpsimd", lambda e, tc=tc: e.dma_start(
                out=rsscr[0:1, tc * 128:(tc + 1) * 128].rearrange("o (j i) -> (o j) i", i=1),
                in_=ST[:, 16 + tc:17 + tc]), "sRS")
            p.dma("gpsimd", lambda e, tc=tc: e.dma_start(
                out=rsscr[1:2, tc * 128:(tc + 1) * 128].rearrange("o (j i) -> (o j) i", i=1),
                in_=ST[:, 32 + tc:33 + tc]), "sRS")

        ar.reset(m_pre)
        rs_row = ar.alloc([1, 2 * MYT], F32)
        rs_b = ar.alloc([128, MYT], BF16)
        inv2_b = ar.alloc([128, MYT], F32)
        t1_t52 = [ar.alloc([128, MYT], F32) for _ in range(2)]
        z52 = [ar.alloc([128, MYT], I32) for _ in range(2)]
        tmpf = [ar.alloc([128, MYT], BF16) for _ in range(2)]
        iq5 = [ar.alloc([128, MYT], BF16) for _ in range(2)]

        phase_barrier()
        p.wait("sync", "sRS")
        p.dma("sync", lambda e: e.dma_start(out=rs_row[0:1, :], in_=rsscr.rearrange("(o a) j -> o (a j)", o=1)), "sLD")
        p.wait("tensor", "sLD")
        p.wait("tensor", "sSE")
        for j in range(MYT // 512):
            p.wait("tensor", "sVE")
            pe(lambda e, j=j: e.matmul(ps[0][:, 0:512], onesf32[0:1, :],
                                       rs_row[0:1, j * 512:(j + 1) * 512], start=True, stop=True), inc=True)
            pe(lambda e, j=j: e.matmul(ps[1][:, 0:512], onesf32[0:1, :],
                                       rs_row[0:1, MYT + j * 512:MYT + (j + 1) * 512], start=True, stop=True), inc=True)
            p.wait("vector", "sPE")
            ve(lambda e, j=j: e.tensor_copy(rs_b[:, j * 512:(j + 1) * 512], ps[0][:, 0:512]))
            ve(lambda e, j=j: e.tensor_copy(inv2_b[:, j * 512:(j + 1) * 512], ps[1][:, 0:512]))
        p.wait("tensor", "sVE")

        p.wait("sync", "sTMS")
        tf_slot = [Slot(), Slot()]
        tz_slot = [Slot(), Slot()]
        iq_slot = [Slot(), Slot()]
        for hc in range(NHC):
            pr = hc % 2
            tf = tmpf[pr]
            iqt = iq5[pr]
            t1_t5, z5 = t1_t52[pr], z52[pr]
            tf_slot[pr].require("sync")
            tdma(lambda e, tf=tf, hc=hc: e.dma_start_transpose(
                tf[:], tmpscr[:, hc * 128:(hc + 1) * 128]), "sTMT")
            tmt_done = p.cnt["sTMT"]
            p.wait("vector", "sTMT", tmt_done)
            tz_slot[pr].require("vector")
            if pr == 0:
                ve(lambda e, tf=tf, t1_t5=t1_t5: e.tensor_tensor(t1_t5[:], tf[:], rs_b[:], ALU.mult))
                tf_slot[pr].consumed("sVE")
                p.wait("vector", "sVE")
            else:
                p.wait("gpsimd", "sTMT", tmt_done)
                tz_slot[pr].require("gpsimd")
                ge(lambda e, tf=tf, t1_t5=t1_t5: e.tensor_tensor(t1_t5[:], tf[:], rs_b[:], ALU.mult))
                tf_slot[pr].consumed("sGE")
                p.wait("vector", "sGE")
            ve(lambda e, t1_t5=t1_t5, z5=z5: e.tensor_copy(z5[:], t1_t5[:]))
            iq_slot[pr].require("vector")
            ve(lambda e, iqt=iqt, z5=z5: e.tensor_tensor(iqt[:], z5[:], inv2_b[:], ALU.mult))
            tz_slot[pr].consumed("sVE", "sGE")
            iqw = ("sVE", p.cnt["sVE"])
            p.wait("gpsimd", iqw[0], iqw[1])
            p.dma("gpsimd", lambda e, iqt=iqt, hc=hc: e.dma_start(
                out=iqscr[hc], in_=iqt[:]), "sIQS")
            iq_slot[pr].consumed(total=("sIQS",))

        ar.reset(m_pre)
        NG5 = 4
        GSZ = NHC // NG5
        w2T52 = [[ar.alloc([128, D], BF16) for _ in range(GSZ)] for _ in range(2)]
        iqg = [ar.alloc([128, 128 * (TCM // 2)], BF16) for _ in range(GSZ)]
        acc = [ar.alloc([128, D], F32) for _ in range(TCM // 2)]
        ot = [ar.alloc([128, D], F32) for _ in range(2)]

        w2t_slot = [Slot(), Slot()]
        w2t_sem = [0, 0]
        iqg_slot = Slot()
        acc_slot = [Slot() for _ in range(TCM // 2)]
        b5_slot = [Slot(), Slot()]
        ot_slot = [Slot(), Slot()]

        def load_w2t(gg):
            buf = gg % 2
            p.wait("sync", "sW2")
            w2t_slot[buf].require("sync")
            for i in range(GSZ):
                hc = (gg % NG5) * GSZ + i
                tdma_nw(lambda e, i=i, hc=hc, buf=buf: e.dma_start_transpose(
                    w2T52[buf][i][:], w2scr[:, hc * 128:(hc + 1) * 128]), "sW2T")
            tflush("sW2T")
            w2t_sem[buf] = p.cnt["sW2T"]

        def emit_ln2(th, i):
            tc = th * (TCM // 2) + i
            pr = i % 2
            SB0 = 12 * pr
            o_t = ot[pr]
            p.wait("scalar", "sVE")
            se(lambda e, i=i, SB0=SB0: e.activation(trash_bf[:], acc[i][:], AF.Copy, accum_out=ST[:, SB0:SB0 + 1]))
            se(lambda e, i=i, SB0=SB0: e.activation(trash_bf[:], acc[i][:], AF.Square, accum_out=ST[:, SB0 + 1:SB0 + 2]))
            acc_done2 = p.cnt["sSE"]
            p.wait("vector", "sSE", acc_done2)
            ve(lambda e, SB0=SB0: e.tensor_scalar(ST[:, SB0 + 2:SB0 + 3], ST[:, SB0:SB0 + 1], 1.0 / D, None, ALU.mult))
            ve(lambda e, SB0=SB0: e.tensor_scalar(ST[:, SB0 + 3:SB0 + 4], ST[:, SB0 + 1:SB0 + 2], 1.0 / D, None, ALU.mult))
            ve(lambda e, SB0=SB0: e.tensor_tensor(ST[:, SB0 + 4:SB0 + 5], ST[:, SB0 + 2:SB0 + 3], ST[:, SB0 + 2:SB0 + 3], ALU.mult))
            ve(lambda e, SB0=SB0: e.tensor_tensor(ST[:, SB0 + 5:SB0 + 6], ST[:, SB0 + 3:SB0 + 4], ST[:, SB0 + 4:SB0 + 5], ALU.subtract))
            var_done2 = p.cnt["sVE"]
            p.wait("scalar", "sVE", var_done2)
            se(lambda e, SB0=SB0: e.activation(ST[:, SB0 + 6:SB0 + 7], ST[:, SB0 + 5:SB0 + 6], AF.Sqrt, bias=LN_EPS))
            p.wait("vector", "sSE")
            ve(lambda e, SB0=SB0: e.reciprocal(ST[:, SB0 + 7:SB0 + 8], ST[:, SB0 + 6:SB0 + 7]))
            ot_slot[pr].require("vector")
            ve(lambda e, i=i, o_t=o_t, SB0=SB0: e.tensor_scalar(
                o_t[:], acc[i][:], ST[:, SB0 + 2:SB0 + 3], ST[:, SB0 + 7:SB0 + 8], ALU.subtract, ALU.mult))
            acc_slot[i].consumed("sVE")
            ow_done = p.cnt["sVE"]
            p.wait("sync", "sVE", ow_done)
            p.dma("sync", lambda e, o_t=o_t, tc=tc: e.dma_start(
                out=out_c[tc * 128:(tc + 1) * 128, :], in_=o_t[:]), "sOUT")
            ot_slot[pr].consumed(total=("sOUT",))

        first = True
        for th in range(2):
            if first:
                phase_barrier(["sync", "scalar"])
            p.wait("sync", "sXPS")
            p.wait("sync", "sVE")
            p.wait("sync", "sSE")
            for i in range(TCM // 2):
                tc = th * (TCM // 2) + i
                acc_slot[i].require("sync")
                p.dma("sync", lambda e, i=i, tc=tc: e.dma_start(
                    out=acc[i][:], in_=xpscr[tc * 128:(tc + 1) * 128, :]), "sACB")
            acb_done = p.cnt["sACB"]
            if first:
                load_w2t(0)
                first = False
            for g in range(NG5):
                gg = th * NG5 + g
                buf = gg % 2
                if gg + 1 < 2 * NG5:
                    load_w2t(gg + 1)
                p.wait("scalar", "sIQS")
                iqg_slot.require("scalar")
                for i in range(GSZ):
                    hc = g * GSZ + i
                    p.dma("scalar", lambda e, i=i, hc=hc, th=th: e.dma_start(
                        out=iqg[i][:], in_=iqscr[hc][:, th * (MYT // 2):(th + 1) * (MYT // 2)]), "sIQL")
                iql_done = p.cnt["sIQL"]
                p.wait("tensor", "sW2T", w2t_sem[buf])
                p.wait("tensor", "sIQL", iql_done)
                p.wait("tensor", "sACB", acb_done)
                for i in range(TCM // 2):
                    banks = ps[0:4] if i % 2 == 0 else ps[4:8]
                    b5_slot[i % 2].require("tensor")
                    for j in range(GSZ):
                        for cs in range(4):
                            pe(lambda e, i=i, j=j, cs=cs, banks=banks, buf=buf: e.matmul(
                                banks[cs][:, 0:512], iqg[j][:, i * 128:(i + 1) * 128],
                                w2T52[buf][j][:, cs * 512:(cs + 1) * 512],
                                start=(j == 0), stop=(j == GSZ - 1)),
                                inc=(j == GSZ - 1 and cs == 3))
                    mm_done = p.cnt["sPE"]
                    p.wait("vector", "sPE", mm_done)
                    p.wait("vector", "sACB", acb_done)
                    acc_slot[i].require("vector")
                    for cs in range(4):
                        ve(lambda e, i=i, cs=cs, banks=banks: e.tensor_tensor(
                            acc[i][:, cs * 512:(cs + 1) * 512], acc[i][:, cs * 512:(cs + 1) * 512],
                            banks[cs][:, 0:512], ALU.add))
                    b5_slot[i % 2].consumed("sVE")
                    if g == NG5 - 1:
                        emit_ln2(th, i)
                iqg_slot.consumed("sPE")
                w2t_slot[buf].consumed("sPE")
        if DEBUG:
            p.barrier(["gpsimd"], ["sQKS", "sVS", "sYS", "sAPS", "sCC", "sXPS",
                                   "sXQS", "sTMS", "sOUT"])
            for name, (dst, src) in dbg.items():
                p.dma("gpsimd", lambda e, dst=dst, src=src: e.dma_start(out=dst, in_=src), "sDBG")
            p.wait("gpsimd", "sDBG")

        for eng in ALL_ENG:
            p.wait(eng, "sOUT")

        with nc.Block() as block:
            def mk(engname):
                def run(e):
                    for it in p.items:
                        if it[0] != engname:
                            continue
                        if it[1] == "wait":
                            e.wait_ge(sems[it[2]], it[3])
                        else:
                            instr = it[2](e)
                            if it[3] is not None:
                                instr.then_inc(sems[it[3]], it[4])
                return run
            block.sync(mk("sync"))
            block.scalar(mk("scalar"))
            block.vector(mk("vector"))
            block.tensor(mk("tensor"))
            block.gpsimd(mk("gpsimd"))

    return nc


def _host_consts(hg):
    inv_freq = ((np.float32(1.0) / np.float32(ROPE_BASE)) **
                (np.arange(0, HH, 2, dtype=np.float32) / np.float32(HH))).astype(np.float64)
    rotc = np.zeros((128, CQ), np.float32)
    rots = np.zeros((128, CQ), np.float32)
    for hl in range(NHL):
        hgl = hg * NHL + hl
        c = np.cos(np.float64(hgl) * inv_freq).astype(np.float32)
        s = np.sin(np.float64(hgl) * inv_freq).astype(np.float32)
        for half in range(2):
            base = hl * HD + half * HH
            rotc[:, base:base + 64] = c
            rotc[:, base + 64:base + 128] = c
            rots[:, base:base + 64] = s
            rots[:, base + 64:base + 128] = -s
    ki = np.arange(128)[:, None]
    qj = np.arange(512)[None, :]
    m4 = np.zeros((4, 128, 512), np.float32)
    for d in range(4):
        m4[d] = (qj >= ki + d * 128).astype(np.float32)
    mskv = np.zeros((2, 128, 1024), np.float32)
    mskv[0, :, 0:512] = m4[0]
    mskv[0, :, 512:1024] = m4[1]
    mskv[1, :, 0:512] = m4[2]
    mskv[1, :, 512:1024] = m4[3]
    return rotc, rots, mskv.astype(ml_dtypes.bfloat16)




def my_token_ids(hg):
    return np.concatenate([np.arange(c * 1024 + hg * 512, c * 1024 + (hg + 1) * 512)
                           for c in range(T // 1024)])


def make_in_maps(x, wqf, wkf, wvf, wof, w13f, w2f, nwf, lqkf):
    in_maps = []
    for c in range(8):
        b, hg = c // 2, c % 2
        rotcv, rotsv, mskv = _host_consts(hg)
        in_maps.append(dict(
            xb=x[b],
            xmy=np.ascontiguousarray(x[b, my_token_ids(hg)]),
            wq=np.ascontiguousarray(wqf[hg * CQ:(hg + 1) * CQ]),
            wk=np.ascontiguousarray(wkf[hg * CQ:(hg + 1) * CQ]),
            wv=np.ascontiguousarray(wvf[hg * CQ:(hg + 1) * CQ]),
            wo=np.ascontiguousarray(wof[:, hg * CQ:(hg + 1) * CQ]),
            w13=w13f, w2=w2f, nw_in=nwf, lqk=lqkf,
            rotc_in=rotcv, rots_in=rotsv, msk_in=mskv,
        ))
    return in_maps

def kernel(L0_attn_x, L0_attn_wq, L0_attn_wk, L0_attn_wv, L0_attn_wo,
           L0_attn_lq1, L0_attn_lk1, L0_attn_lq2, L0_attn_lk2,
           L0_ffn_w13, L0_ffn_w2, L0_ffn_norm_w):
    x = np.ascontiguousarray(np.asarray(L0_attn_x, dtype=np.float32))
    wqf = np.ascontiguousarray(np.asarray(L0_attn_wq, dtype=np.float32))
    wkf = np.ascontiguousarray(np.asarray(L0_attn_wk, dtype=np.float32))
    wvf = np.ascontiguousarray(np.asarray(L0_attn_wv, dtype=np.float32))
    wof = np.ascontiguousarray(np.asarray(L0_attn_wo, dtype=np.float32))
    w13f = np.ascontiguousarray(np.asarray(L0_ffn_w13, dtype=np.float32))
    w2f = np.ascontiguousarray(np.asarray(L0_ffn_w2, dtype=np.float32))
    nwf = np.ascontiguousarray(np.asarray(L0_ffn_norm_w, dtype=np.float32)).reshape(1, HF)
    lqkf = np.ascontiguousarray(np.stack([np.asarray(v, dtype=np.float32) for v in
                                          (L0_attn_lq1, L0_attn_lk1, L0_attn_lq2, L0_attn_lk2)]))

    nc = build_nc()
    in_maps = make_in_maps(x, wqf, wkf, wvf, wof, w13f, w2f, nwf, lqkf)
    res = run_bass_kernel_spmd(nc, in_maps, core_ids=list(range(8)))
    out = np.empty((B, T, D), dtype=np.float32)
    for c in range(8):
        b, hg = c // 2, c % 2
        out[b, my_token_ids(hg)] = res.results[c]["out_c"]
    return out
